# version 09
# baseline (speedup 1.0000x reference)
import numpy as np
from contextlib import ExitStack

import concourse.bass as bass
import concourse.mybir as mybir
from concourse.bass_utils import run_bass_kernel_spmd
from concourse.alu_op_type import AluOpType as OP

F32 = mybir.dt.float32
BF16 = mybir.dt.bfloat16
I32 = mybir.dt.int32
AF = mybir.ActivationFunctionType
AX = mybir.AxisListType

D_MODEL = 1024
D_INNER = 2048
N_HEADS = 32
D_HEAD = 64
D_STATE = 64
CONV_DIM = D_INNER + 2 * D_STATE
D_IN_PROJ = 2 * D_INNER + 2 * D_STATE + N_HEADS
EPS = 1e-5
CHUNK = 128
SEG = 512
NCH = SEG // CHUNK
E_XBC0 = D_INNER
E_DT0 = D_INNER + CONV_DIM


class Sem:
    __slots__ = ("name", "cm", "h", "count")

    def __init__(self, nc, name):
        self.name = name
        self.cm = nc.semaphore(name)
        self.h = None
        self.count = 0


class Planner:
    def __init__(self):
        self.script = []
        self.hw = {}
        self.res = {}

    def _wait_list(self, ename, reads, writes):
        need = {}
        def add(sem, val):
            if val <= 0:
                return
            if self.hw.get((ename, sem.name), 0) >= val:
                return
            if need.get(sem, 0) < val:
                need[sem] = val
        for n in reads:
            r = self.res.get(n)
            if r and r['w']:
                add(*r['w'])
        for n in writes:
            r = self.res.get(n)
            if not r:
                continue
            if r['w']:
                add(*r['w'])
            for m in r['r']:
                add(*m)
        out = list(need.items())
        for s, v in out:
            self.hw[(ename, s.name)] = v
        return out

    def op(self, ename, sem, reads, writes, emit, inc_amt=1, inc_attached=False):
        waits = self._wait_list(ename, reads, writes)
        sem.count += inc_amt
        val = sem.count
        for n in writes:
            self.res[n] = {'w': (sem, val), 'r': []}
        for n in reads:
            r = self.res.setdefault(n, {'w': None, 'r': []})
            r['r'].append((sem, val))

        def closure(eng, waits=waits, emit=emit, sem=sem, inc_amt=inc_amt,
                    inc_attached=inc_attached):
            for s, v in waits:
                eng.wait_ge(s.h, v)
            ins = emit(eng)
            if not inc_attached:
                if ins is None:
                    ins = eng.nop()
                ins.then_inc(sem.h, inc_amt)
        self.script.append((ename, closure))

    def raw(self, ename, fn):
        self.script.append((ename, fn))


def build(T, n_ranks, moe_tok, dbg=False):
    nc = bass.Bass("TRN2", target_bir_lowering=False)
    NSEG = (T // 2) // SEG
    NTT = moe_tok // 128

    TH = T // 2
    xin = nc.dram_tensor("xin", [TH, D_MODEL], F32, kind="ExternalInput").ap()
    w = {}
    for L in range(2):
        w[L] = dict(
            inwT=nc.dram_tensor(f"inwT{L}", [D_MODEL, D_IN_PROJ], F32, kind="ExternalInput").ap(),
            convw=nc.dram_tensor(f"convw{L}", [CONV_DIM, 4], F32, kind="ExternalInput").ap(),
            convb=nc.dram_tensor(f"convb{L}", [128, 17], F32, kind="ExternalInput").ap(),
            dtb=nc.dram_tensor(f"dtb{L}", [N_HEADS], F32, kind="ExternalInput").ap(),
            alog=nc.dram_tensor(f"alog{L}", [N_HEADS], F32, kind="ExternalInput").ap(),
            dsk=nc.dram_tensor(f"dsk{L}", [N_HEADS], F32, kind="ExternalInput").ap(),
            nw=nc.dram_tensor(f"nw{L}", [128, 8], F32, kind="ExternalInput").ap(),
            gw=nc.dram_tensor(f"gw{L}", [128, 16], F32, kind="ExternalInput").ap(),
            owT=nc.dram_tensor(f"owT{L}", [D_INNER, D_MODEL], F32, kind="ExternalInput").ap(),
        )
    Wg = nc.dram_tensor("Wg", [D_MODEL, 8], F32, kind="ExternalInput").ap()
    bg = nc.dram_tensor("bg", [1, 8], F32, kind="ExternalInput").ap()
    W1 = nc.dram_tensor("W1", [8, D_MODEL, 4096], F32, kind="ExternalInput").ap()
    b1 = nc.dram_tensor("b1", [8, 4, 128, 8], F32, kind="ExternalInput").ap()
    W2 = nc.dram_tensor("W2", [8, 4096, D_MODEL], F32, kind="ExternalInput").ap()
    b2 = nc.dram_tensor("b2", [8, D_MODEL], F32, kind="ExternalInput").ap()
    i128f = nc.dram_tensor("i128f", [128, 128], F32, kind="ExternalInput").ap()
    i128b = nc.dram_tensor("i128b", [128, 128], F32, kind="ExternalInput").ap()
    triuc = nc.dram_tensor("triuc", [128, 128], F32, kind="ExternalInput").ap()
    onesr = nc.dram_tensor("onesr", [1, 128], F32, kind="ExternalInput").ap()
    cvec = nc.dram_tensor("cvec", [128, 4], F32, kind="ExternalInput").ap()
    rankin = nc.dram_tensor("rankin", [1, 16], I32, kind="ExternalInput").ap()

    out = nc.dram_tensor("out", [moe_tok, D_MODEL], F32, kind="ExternalOutput").ap()

    rkind = "ExternalOutput" if dbg else "Internal"
    r1 = nc.dram_tensor("r1d", [T // 2, D_MODEL], F32, kind=rkind).ap()
    r2 = nc.dram_tensor("r2d", [T // 2, D_MODEL], F32, kind=rkind).ap()
    dach_d = nc.dram_tensor("dachd", [2 * N_HEADS, T // 2], BF16, kind="Internal").ap()
    xr_d = nc.dram_tensor("xr_d", [128, NSEG * NCH * D_INNER], BF16, kind="Internal").ap()
    hf_d = nc.dram_tensor("hf_d", [128, NSEG * 8 * SEG], BF16, kind="Internal").ap()
    bc_d = nc.dram_tensor("bc_d", [64, NSEG * 2 * SEG], BF16, kind="Internal").ap()
    dtt_d = nc.dram_tensor("dtt_d", [128, NSEG * 2 * NCH * N_HEADS], F32, kind="Internal").ap()
    dzx_d = nc.dram_tensor("dzx_d", [64, NSEG * NCH * D_INNER], BF16, kind="Internal").ap()
    stz_d = nc.dram_tensor("stz_d", [64, NSEG * NCH * D_INNER], BF16, kind="Internal").ap()
    zz_d = nc.dram_tensor("zz_d", [64, D_INNER], F32, kind="Internal").ap()
    dum_d = nc.dram_tensor("dum_d", [64, D_INNER], F32, kind="Internal").ap()
    shS_d = nc.dram_tensor("shS_d", [2, 64, D_INNER], F32, kind="Internal",
                           addr_space="Shared").ap()
    shH_d = nc.dram_tensor("shH_d", [3, D_MODEL], F32, kind="Internal",
                           addr_space="Shared").ap()
    shR_d = nc.dram_tensor("shR_d", [2048, D_MODEL], F32, kind="Internal",
                           addr_space="Shared").ap()
    shF_d = nc.dram_tensor("shF_d", [8, 16], I32, kind="Internal",
                           addr_space="Shared").ap()
    hx = nc.dram_tensor("hx", [3, D_MODEL], F32, kind="ExternalInput").ap()

    P = Planner()
    dbg_t = {}
    es = ExitStack()
    with es:
        sbT = lambda name, shape, dt: es.enter_context(nc.sbuf_tensor(name, shape, dt))
        psT = lambda name, shape, dt: es.enter_context(nc.psum_tensor(name, shape, dt))

        consts = sbT("consts", [128, 4], F32)
        i128b_s = sbT("i128b_s", [128, 128], BF16)
        i128f_s = sbT("i128f_s", [128, 128], F32)
        triu_s = sbT("triu_s", [128, 128], BF16)
        ones_b = sbT("ones_b", [1, 128], BF16)
        onesq = sbT("onesq", [65, 128], BF16)
        ones_f = sbT("ones_f", [1, 128], F32)
        rank_s = sbT("rank_s", [1, 16], I32)
        pollb = sbT("pollb", [1, 16], I32)
        zf = sbT("zf", [4, 16], I32)
        sc1 = sbT("sc1", [128, 8], F32)
        hT = sbT("hT", [128, D_MODEL], BF16)

        pAt = [psT("pAb%d" % i, [128, 512], F32) for i in range(4)]
        pB0 = psT("pB0", [128, 1024], F32)
        pB2 = psT("pB2", [128, 512], F32)
        pT = psT("pT", [128, 1024], BF16)

        SYL = [Sem(nc, "qS%d" % i) for i in range(4)]
        POL = [Sem(nc, "qP%d" % i) for i in range(3)]
        sPE = Sem(nc, "sPE"); sSC = Sem(nc, "sSC"); sVE = Sem(nc, "sVE"); sPO = Sem(nc, "sPO")
        allsems = SYL + POL + [sPE, sSC, sVE, sPO]
        for s in allsems:
            s.h = es.enter_context(s.cm)
        ESEM = {"pe": sPE, "sc": sSC, "ve": sVE, "po": sPO, "sy": sPO}
        LANES = {"sy": SYL, "po": POL}
        rrq = {"sy": 0, "po": 0}

        def dma(q, outap, inap, reads, writes):
            lanes = LANES[q]
            lane = lanes[rrq[q] % len(lanes)]
            rrq[q] += 1
            gate = "lane_" + lane.name
            P.op(q, lane, list(reads) + [gate], list(writes) + [gate],
                 lambda eng, o=outap, i=inap: eng.dma_start(out=o, in_=i), inc_amt=16)

        def op(ename, reads, writes, emit):
            P.op(ename, ESEM[ename], reads, writes, emit)

        def dump(name, ap, reads):
            if not dbg:
                return
            shape = list(ap.shape)
            t = nc.dram_tensor("dbg_" + name, shape, F32, kind="ExternalOutput").ap()
            dbg_t[name] = t
            dma("po", t, ap, reads, ["dbg_" + name])

        dma("sy", consts[:], cvec, [], ["consts"])
        dma("sy", i128f_s[:], i128f, [], ["i128f"])
        dma("po", i128b_s[:], i128b, [], ["i128b"])
        dma("po", triu_s[:], triuc, [], ["triu"])
        dma("po", ones_b[:], onesr, [], ["ones_b"])
        dma("po", onesq[0:1, :], onesr, [], ["onesq0"])
        dma("po", onesq[32:33, :], onesr, [], ["onesq1"])
        dma("po", onesq[64:65, :], onesr, [], ["onesq2"])
        dma("sy", ones_f[:], onesr, [], ["ones_f"])
        dma("sy", rank_s[:], rankin, [], ["rank"])

        MAGICS = [1111, 2222, 3333, 4444]
        dynsems = []
        regc = [0]

        def rname(p):
            regc[0] += 1
            return "%s%d" % (p, regc[0])

        def dsem(name):
            s = Sem(nc, name)
            s.h = es.enter_context(s.cm)
            dynsems.append(s)
            return s
        pollS = [dsem("plS0"), dsem("plS1")]
        pollH = dsem("plH")
        pollM = dsem("plM")

        def polane():
            lanes = LANES["po"]
            lane = lanes[rrq["po"] % len(lanes)]
            rrq["po"] += 1
            return lane, lane.count, "lane_" + lane.name

        def zflags():
            P.op("po", ESEM["po"], [], ["zf"], lambda g: g.memset(zf[:], 0))
            lane, base, gate = polane()

            def emit(g, lane=lane, base=base):
                with nc.gpsimd.register(rname("r_p")) as r_p, \
                     nc.gpsimd.register(rname("r_o")) as r_o:
                    g.reg_load(r_p, rank_s[:1, 9:10])
                    with g.If_eq(r_p, 0):
                        g.dma_start(out=shF_d[0:3, :], in_=zf[0:3, :]).then_inc(lane.h, 16)
                    with g.Else():
                        g.dma_start(out=shF_d[4:7, :], in_=zf[0:3, :]).then_inc(lane.h, 16)
                    g.end_ifs()
                    g.reg_load(r_o, rank_s[:1, 10:11])
                    with g.If_eq(r_o, 1):
                        g.dma_start(out=shF_d[3:4, :], in_=zf[0:1, :]).then_inc(lane.h, 16)
                    with g.Else():
                        g.dma_start(out=shF_d[7:8, :], in_=zf[0:1, :]).then_inc(lane.h, 16)
                    g.end_ifs()
                g.wait_ge(lane.h, base + 32)
                return None
            P.op("po", lane, ["rank", "zf", gate], ["shFz", gate], emit,
                 inc_amt=32, inc_attached=True)

        def pub2(even_out, odd_out, payload_in, row_even, row_odd, mcol,
                 reads, writes):
            lane, base, gate = polane()

            def emit(g, lane=lane, base=base):
                with nc.gpsimd.register(rname("r_p")) as r_p:
                    g.reg_load(r_p, rank_s[:1, 9:10])
                    for par in (0, 1):
                        ifg = g.If_eq(r_p, 0) if par == 0 else g.Else()
                        with ifg:
                            po = even_out if par == 0 else odd_out
                            fr = row_even if par == 0 else row_odd
                            g.dma_start(out=po, in_=payload_in).then_inc(lane.h, 16)
                            g.wait_ge(lane.h, base + 16)
                            g.dma_start(out=shF_d[fr:fr + 1, 0:1],
                                        in_=rank_s[:1, mcol:mcol + 1]).then_inc(lane.h, 16)
                    g.end_ifs()
                return None
            P.op("po", lane, list(reads) + ["rank", "shFz", gate],
                 list(writes) + [gate], emit, inc_amt=32, inc_attached=True)

        def init_pair(dst, shared_src, zero_src, row, magic, psem, reads, writes):
            lane, base, gate = polane()

            def emit(g, lane=lane, psem=psem, magic=magic):
                with nc.gpsimd.register(rname("r_p")) as r_p, \
                     nc.gpsimd.register(rname("r_c")) as r_c, \
                     nc.gpsimd.register(rname("r_v")) as r_v, \
                     nc.gpsimd.register(rname("r_w")) as r_w:
                    g.reg_load(r_p, rank_s[:1, 9:10])
                    with g.If_eq(r_p, 0):
                        g.dma_start(out=dst, in_=zero_src).then_inc(lane.h, 16)
                    with g.Else():
                        g.reg_mov(r_c, 1)
                        g.reg_mov(r_w, 0)
                        with g.While(r_c):
                            g.reg_add(r_w, r_w, 16)
                            g.dma_start(out=pollb[:1, 0:1],
                                        in_=shF_d[row:row + 1, 0:1]).then_inc(psem.h, 16)
                            g.wait_ge(psem.h, r_w)
                            g.reg_load(r_v, pollb[:1, 0:1])
                            g.reg_alu(r_c, r_v, magic, OP.bitwise_xor)
                        g.dma_start(out=dst, in_=shared_src).then_inc(lane.h, 16)
                    g.end_ifs()
                return None
            P.op("po", lane, list(reads) + ["rank", gate],
                 list(writes) + [gate], emit, inc_amt=16, inc_attached=True)

        zflags()

        with ExitStack() as ms:
            msb = lambda name, shape, dt: ms.enter_context(nc.sbuf_tensor(name, shape, dt))
            WB = msb("WBs", [128, 8, D_IN_PROJ], BF16)
            OWT = msb("OWTs", [128, 16, D_MODEL], BF16)
            cw_s = msb("cw_s", [128, 17, 4], F32)
            cb_s = msb("cb_s", [128, 17], F32)
            nw_s = msb("nw_s", [128, 8], F32)
            gw_s = msb("gw_s", [128, 16], F32)
            hd_s = msb("hd_s", [N_HEADS, 8], F32)
            Dbt = msb("Dbt", [128, N_HEADS], F32)
            drow = msb("drow", [1, N_HEADS], F32)
            rbufh = msb("rbufh", [128, D_MODEL], F32)
            rbuf2 = msb("rbuf2", [128, D_MODEL], F32)
            hF = msb("hF", [128, 8, SEG], BF16)
            xbc = msb("xbc", [128, 3 + SEG], BF16)
            xhal = msb("xhal", [128, 17, 3], BF16)
            hHal = msb("hHal", [128, 8, 3], BF16)
            xacc = msb("xacc", [128, SEG], BF16)
            xsil = msb("xsil", [128, SEG], BF16)
            Xr = msb("Xr", [128, NCH, D_INNER], BF16)
            BFt = msb("BFt", [64, SEG], BF16)
            CFt = msb("CFt", [64, SEG], BF16)
            BTt = msb("BTt", [128, NCH, D_STATE], BF16)
            dtF = msb("dtF", [N_HEADS, SEG], F32)
            dtmpF = msb("dtmpF", [N_HEADS, SEG], F32)
            dAcsF = msb("dAcsF", [N_HEADS, SEG], F32)
            decF = msb("decF", [N_HEADS, SEG], F32)
            dAcsT = msb("dAcsT", [128, NCH, N_HEADS], F32)
            ldT = msb("ldT", [128, NCH, N_HEADS], F32)
            lmdF = msb("lmdF", [N_HEADS, SEG], F32)
            his = msb("his", [N_HEADS, SEG], BF16)
            decT = msb("decT", [128, NCH, N_HEADS], F32)
            thrT = decT
            expT = msb("expT", [128, N_HEADS], F32)
            dzrow = msb("dzrow", [1, NCH, N_HEADS], F32)
            dzrx = msb("dzrx", [1, 512], BF16)
            rowq = msb("rowq", [65, 2048], BF16)
            dzB = msb("dzB", [64, D_INNER], BF16)
            S_run = msb("S_run", [64, D_INNER], F32)
            prevb = msb("prevb", [64, D_INNER], BF16)
            stz = msb("stz", [64, D_INNER], BF16)
            xds = msb("xds", [128, D_INNER], BF16)
            Mt = msb("Mt", [128, 8 * CHUNK], BF16)
            Mt2 = msb("Mt2", [128, 8 * CHUNK], BF16)
            Gt2 = msb("Gt2", [128, CHUNK], BF16)
            yosb = msb("yosb", [128, 1024], F32)
            Gt = msb("Gt", [128, CHUNK], BF16)
            Ybf = msb("Ybf", [128, D_INNER], BF16)
            zs = msb("zs", [128, D_INNER], BF16)
            yg = msb("yg", [128, D_INNER], BF16)
            yn = yg
            yF = msb("yF", [128, 16, 128], BF16)

            hF3 = hF[:]

            def mamba_layer(L, rin, rout):
                wl = w[L]
                dma("po", WB[:], wl["inwT"].rearrange("(dt p) e -> p dt e", p=128), [], ["WB"])
                dma("po", OWT[:], wl["owT"].rearrange("(it p) d -> p it d", p=128), [], ["OWT"])
                dma("sy", cw_s[:], wl["convw"].rearrange("(et p) k -> p et k", p=128), [], ["convw"])
                dma("sy", cb_s[:], wl["convb"], [], ["convb"])
                dma("sy", nw_s[:], wl["nw"], [], ["nw"])
                dma("sy", gw_s[:], wl["gw"], [], ["gw"])
                dma("sy", hd_s[:, 0:1], wl["dtb"].unsqueeze(1), ["dtU"], ["hd0"])
                dma("sy", hd_s[:, 1:2], wl["alog"].unsqueeze(1), ["hdA"], ["hd1"])
                dma("sy", hd_s[:, 3:4], wl["dsk"].unsqueeze(1), ["Dbt"], ["hd3"])

                def fold_wb(v):
                    for dt in range(8):
                        ins = v.tensor_scalar(WB[:, dt, :], WB[:, dt, :], nw_s[:, dt:dt + 1],
                                              0.0, op0=OP.mult, op1=OP.bypass)
                    return ins
                op("ve", ["WB", "nw"], ["WB"], fold_wb)

                def fold_ow(v):
                    for it in range(16):
                        ins = v.tensor_scalar(OWT[:, it, :], OWT[:, it, :], gw_s[:, it:it + 1],
                                              0.0, op0=OP.mult, op1=OP.bypass)
                    return ins
                op("ve", ["OWT", "gw"], ["OWT"], fold_ow)

                op("sc", ["hd1"], ["hdA"],
                   lambda s: s.activation(hd_s[:, 2:3], hd_s[:, 1:2], AF.Exp))
                op("ve", ["hdA"], ["hdA"],
                   lambda v: v.tensor_scalar(hd_s[:, 2:3], hd_s[:, 2:3], -1.0, 0.0,
                                             op0=OP.mult, op1=OP.bypass))
                op("pe", ["hd3", "i128f"], ["pB6"],
                   lambda t: t.transpose(pB2[0:1, 0:N_HEADS], hd_s[:, 3:4],
                                         i128f_s[0:N_HEADS, 0:N_HEADS]))
                op("ve", ["pB6"], ["drow", "pB6"],
                   lambda v: v.tensor_copy(drow[:], pB2[0:1, 0:N_HEADS]))
                op("pe", ["drow", "ones_f"], ["pB6"],
                   lambda t: t.matmul(pB2[:, 0:N_HEADS], ones_f[:, 0:128], drow[:],
                                      start=True, stop=True))
                op("ve", ["pB6"], ["Dbt", "pB6"],
                   lambda v: v.tensor_copy(Dbt[:], pB2[:, 0:N_HEADS]))

                def zinit(g):
                    g.memset(xhal[:], 0.0)
                    return g.memset(S_run[:], 0.0)
                op("po", ["xhal", "S_run"], ["xhal", "S_run"], zinit)
                if L == 0:
                    dma("sy", zz_d[:, :], S_run[:], ["S_run"], ["zzd"])

                hF_deps = ["hF%d" % c for c in range(NCH)]
                xr_deps = ["Xr%d" % et for et in range(16)]

                if L == 0:
                    dma("sy", rbufh[0:3, :], hx, ["rbufh_u"], ["rbufh"])
                else:
                    init_pair(rbufh[0:3, :], shH_d[:, :], zz_d[0:3, 0:D_MODEL],
                              2, MAGICS[2], pollH,
                              ["hpub", "zzd", "rbufh_u"], ["rbufh"])
                op("sc", ["rbufh"], ["ssq", "hT"],
                   lambda s: s.activation(hT[0:3, :], rbufh[0:3, :], AF.Square,
                                          accum_out=sc1[0:3, 0:1]))
                op("ve", ["ssq"], ["ms"],
                   lambda v: v.tensor_scalar(sc1[0:3, 1:2], sc1[0:3, 0:1], 1.0 / D_MODEL,
                                             EPS, op0=OP.mult, op1=OP.add))
                op("sc", ["ms"], ["sqv"],
                   lambda s: s.activation(sc1[0:3, 2:3], sc1[0:3, 1:2], AF.Ln))
                op("ve", ["sqv"], ["sqv"],
                   lambda v: v.tensor_scalar(sc1[0:3, 2:3], sc1[0:3, 2:3], -0.5, 0.0,
                                             op0=OP.mult, op1=OP.bypass))
                op("sc", ["sqv"], ["rstd"],
                   lambda s: s.activation(sc1[0:3, 3:4], sc1[0:3, 2:3], AF.Exp))
                op("ve", ["rstd", "rbufh"], ["hT", "rbufh_u"],
                   lambda v: v.tensor_scalar(hT[0:3, :], rbufh[0:3, :], sc1[0:3, 3:4],
                                             0.0, op0=OP.mult, op1=OP.bypass))

                def htrh(t):
                    for dt in range(8):
                        ins = t.transpose(pT[:, dt * 4:dt * 4 + 3],
                                          hT[0:3, dt * 128:(dt + 1) * 128],
                                          i128b_s[0:3, 0:3])
                    return ins
                op("pe", ["hT", "i128b"], ["pT"], htrh)
                def hcp(sx):
                    for dt in range(8):
                        ins = sx.copy(hHal[:, dt, :], pT[:, dt * 4:dt * 4 + 3])
                    return ins
                op("sc", ["pT"], ["hHal"], hcp)
                for et in range(17):
                    preg = "pB6" if et % 2 == 0 else "pB5"
                    pap = pB2[:, 0:3] if et % 2 == 0 else pB0[:, 512:515]

                    def hmm_h(t, et=et, pap=pap):
                        e0 = E_XBC0 + et * 128
                        for dt in range(8):
                            ins = t.matmul(pap, WB[:, dt, e0:e0 + 128],
                                           hHal[:, dt, :],
                                           start=(dt == 0), stop=(dt == 7))
                        return ins
                    op("pe", ["WB", "hHal"], [preg], hmm_h)
                    op("ve", [preg], ["xhal", preg],
                       lambda v, et=et, pap=pap: v.tensor_copy(xhal[:, et, :], pap))

                def seg_A(sg):
                    t0 = sg * SEG
                    for c in range(NCH):
                        tok0 = t0 + c * CHUNK
                        p = c % 2
                        rb = rbufh if p == 0 else rbuf2
                        ht = hT[:] if p == 0 else Ybf[:, 1024:2048]
                        scr = hT[:] if p == 0 else Ybf[:, 0:1024]
                        cb = 4 * p
                        rN, rU, nS, nM, nV, nR, nH = (
                            ("rbufh", "rbufh_u", "ssq", "ms", "sqv", "rstd", "hT")
                            if p == 0 else
                            ("rbuf2", "rbuf2_u", "ssqy", "msy", "sqy", "rsy", "Ybf1"))
                        nSw = [nS, nH] if p == 0 else [nS, "Ybf0"]
                        dma("sy", rb[:], rin[tok0:tok0 + CHUNK, :], [rU], [rN])
                        op("sc", [rN], nSw,
                           lambda s, rb=rb, scr=scr, cb=cb: s.activation(
                               scr, rb[:], AF.Square, accum_out=sc1[:, cb:cb + 1]))
                        op("ve", [nS], [nM],
                           lambda v, cb=cb: v.tensor_scalar(
                               sc1[:, cb + 1:cb + 2], sc1[:, cb:cb + 1], 1.0 / D_MODEL,
                               EPS, op0=OP.mult, op1=OP.add))
                        op("sc", [nM], [nV],
                           lambda s, cb=cb: s.activation(
                               sc1[:, cb + 2:cb + 3], sc1[:, cb + 1:cb + 2], AF.Ln))
                        op("ve", [nV], [nV],
                           lambda v, cb=cb: v.tensor_scalar(
                               sc1[:, cb + 2:cb + 3], sc1[:, cb + 2:cb + 3], -0.5, 0.0,
                               op0=OP.mult, op1=OP.bypass))
                        op("sc", [nV], [nR],
                           lambda s, cb=cb: s.activation(
                               sc1[:, cb + 3:cb + 4], sc1[:, cb + 2:cb + 3], AF.Exp))
                        op("ve", [nR, rN], [nH, rU],
                           lambda v, rb=rb, ht=ht, cb=cb: v.tensor_scalar(
                               ht, rb[:], sc1[:, cb + 3:cb + 4],
                               0.0, op0=OP.mult, op1=OP.bypass))
                        def htr(t, ht=ht):
                            for dt in range(8):
                                ins = t.transpose(pT[:, dt * 128:(dt + 1) * 128],
                                                  ht[:, dt * 128:(dt + 1) * 128], i128b_s[:])
                            return ins
                        op("pe", [nH, "i128b"], ["pT"], htr)
                        op("sc", ["pT"], ["hF%d" % c],
                           lambda sx, c=c: sx.copy(
                               hF3[:, :, c * 128:(c + 1) * 128],
                               pT[:].rearrange("p (a s) -> p a s", a=8)))
                    dma("po", hf_d[:, sg * 8 * SEG:(sg + 1) * 8 * SEG],
                        hF3.rearrange("p a t -> p (a t)"), hF_deps, ["hfd%d" % sg])
                    if L == 0 and sg == 0:
                        dump("hF", hF[:], hF_deps)

                    def dtmm(t):
                        for dt in range(8):
                            ins = t.matmul(pB2[0:N_HEADS, 0:SEG], WB[:, dt, E_DT0:E_DT0 + N_HEADS],
                                           hF3[:, dt, :], start=(dt == 0), stop=(dt == 7))
                        return ins
                    op("pe", ["WB"] + hF_deps, ["pB6"], dtmm)
                    op("ve", ["pB6", "hd0"], ["dtU", "pB6"],
                       lambda v: v.tensor_scalar(dtF[:], pB2[0:N_HEADS, 0:SEG],
                                                 hd_s[0:N_HEADS, 0:1], 0.0,
                                                 op0=OP.add, op1=OP.bypass))
                    op("ve", ["dtU"], ["dtm"],
                       lambda v: v.tensor_scalar(dtmpF[:], dtF[:], -1.0, 0.0,
                                                 op0=OP.mult, op1=OP.bypass))
                    op("ve", ["dtm", "dtU"], ["dtm"],
                       lambda v: v.tensor_tensor(dtmpF[:], dtmpF[:], dtF[:], op=OP.min))
                    op("sc", ["dtm"], ["dtm"],
                       lambda s: s.activation(dtmpF[:], dtmpF[:], AF.Exp))
                    op("sc", ["dtm", "consts"], ["dtm"],
                       lambda s: s.activation(dtmpF[:], dtmpF[:], AF.Ln,
                                              bias=consts[0:N_HEADS, 1:2]))
                    op("ve", ["dtU"], ["dtF"],
                       lambda v: v.tensor_scalar(dtF[:], dtF[:], 0.0, 0.0,
                                                 op0=OP.max, op1=OP.bypass))
                    op("ve", ["dtm", "dtF"], ["dtF"],
                       lambda v: v.tensor_tensor(dtF[:], dtF[:], dtmpF[:], op=OP.add))
                    op("ve", ["dtF", "hdA"], ["dAF"],
                       lambda v: v.tensor_scalar(dtmpF[:], dtF[:], hd_s[0:N_HEADS, 2:3], 0.0,
                                                 op0=OP.mult, op1=OP.bypass))
                    def dscan(v):
                        for c in range(NCH):
                            sl = slice(c * CHUNK, (c + 1) * CHUNK)
                            ins = v.tensor_tensor_scan(dAcsF[:, sl], dtmpF[:, sl], dtmpF[:, sl],
                                                       0.0, op0=OP.add, op1=OP.bypass)
                        return ins
                    op("ve", ["dAF"], ["dAcsF"], dscan)
                    def dec1(v):
                        for c in range(NCH):
                            sl = slice(c * CHUNK, (c + 1) * CHUNK)
                            last = slice(c * CHUNK + CHUNK - 1, c * CHUNK + CHUNK)
                            ins = v.tensor_scalar(decF[:, sl], dAcsF[:, sl], dAcsF[:, last],
                                                  -1.0, op0=OP.subtract, op1=OP.mult)
                        return ins
                    op("ve", ["dAcsF"], ["decF"], dec1)
                    op("sc", ["decF"], ["decF"],
                       lambda s: s.activation(decF[:], decF[:], AF.Exp))
                    op("ve", ["decF", "dtF"], ["decF"],
                       lambda v: v.tensor_tensor(decF[:], decF[:], dtF[:], op=OP.mult))
                    if L == 0 and sg == 0:
                        dump("dtF", dtF[:], ["dtF"])
                        dump("dAcsF", dAcsF[:], ["dAcsF"])
                        dump("decF", decF[:], ["decF"])
                    op("sc", ["dtF"], ["lmdF"],
                       lambda s: s.activation(lmdF[:], dtF[:], AF.Ln))
                    op("ve", ["lmdF", "dAcsF"], ["lmdF"],
                       lambda v: v.tensor_tensor(lmdF[:], lmdF[:], dAcsF[:], op=OP.subtract))
                    op("ve", ["dAcsF"], ["his"],
                       lambda v: v.tensor_copy(his[:], dAcsF[:]))
                    op("ve", ["his", "dAcsF"], ["dtmF2"],
                       lambda v: v.tensor_tensor(dtmpF[:], dAcsF[:], his[:], op=OP.subtract))
                    dma("sy", dach_d[0:N_HEADS, t0:t0 + SEG], his[:], ["his"], ["dach_d"])
                    dma("po", dach_d[N_HEADS:2 * N_HEADS, t0:t0 + SEG], dtmpF[:],
                        ["dtmF2"], ["dach_d2"])
                    def dtr(t):
                        for c in range(NCH):
                            t.transpose(pB0[:, c * 32:(c + 1) * 32],
                                        lmdF[:, c * CHUNK:(c + 1) * CHUNK],
                                        i128f_s[0:N_HEADS, 0:N_HEADS])
                        for c in range(NCH):
                            t.transpose(pB0[:, 128 + c * 32:128 + (c + 1) * 32],
                                        dAcsF[:, c * CHUNK:(c + 1) * CHUNK],
                                        i128f_s[0:N_HEADS, 0:N_HEADS])
                        for c in range(NCH):
                            ins = t.transpose(pB0[:, 256 + c * 32:256 + (c + 1) * 32],
                                              decF[:, c * CHUNK:(c + 1) * CHUNK],
                                              i128f_s[0:N_HEADS, 0:N_HEADS])
                        return ins
                    op("pe", ["lmdF", "dAcsF", "decF", "i128f"], ["pB4"], dtr)
                    def dtcp(sx):
                        sx.copy(ldT[:].rearrange("p c h -> p (c h)"), pB0[:, 0:128])
                        sx.copy(dAcsT[:].rearrange("p c h -> p (c h)"), pB0[:, 128:256])
                        return sx.copy(decT[:].rearrange("p c h -> p (c h)"),
                                       pB0[:, 256:384])
                    op("sc", ["pB4"], ["ldT", "dAcsT", "decT"], dtcp)
                    dma("sy", dzrow[:].rearrange("o c h -> o (c h)"),
                        dAcsT[127:128, :, :].rearrange("o c h -> o (c h)"),
                        ["dAcsT"], ["dzrow"])
                    op("sc", ["dzrow"], ["dzrow"],
                       lambda s: s.activation(dzrow[:].rearrange("o c h -> o (c h)"),
                                              dzrow[:].rearrange("o c h -> o (c h)"), AF.Exp))
                    dma("sy", dtt_d[:, sg * 256:sg * 256 + 128],
                        dAcsT[:].rearrange("p c h -> p (c h)"), ["dAcsT"],
                        ["dttd%d" % sg])
                    dma("sy", dtt_d[:, sg * 256 + 128:sg * 256 + 256],
                        ldT[:].rearrange("p c h -> p (c h)"), ["ldT"],
                        ["dttl%d" % sg])

                    def emit_amm(et):
                        e0 = E_XBC0 + et * 128
                        preg = "pB6" if et % 2 == 0 else "pB5"
                        pap = pB2[:, 0:SEG] if et % 2 == 0 else pB0[:, 512:512 + SEG]
                        def amm(t, e0=e0, pap=pap):
                            for dt in range(8):
                                ins = t.matmul(pap, WB[:, dt, e0:e0 + 128],
                                               hF3[:, dt, :], start=(dt == 0), stop=(dt == 7))
                            return ins
                        op("pe", ["WB"] + hF_deps, [preg], amm)

                    def emit_rest(et):
                        preg = "pB6" if et % 2 == 0 else "pB5"
                        pap = pB2[:, 0:SEG] if et % 2 == 0 else pB0[:, 512:512 + SEG]
                        op("ve", ["xhal"], ["xbchal"],
                           lambda v, et=et: v.tensor_copy(xbc[:, 0:3], xhal[:, et, :]))
                        op("sc", [preg], ["xbc", preg],
                           lambda sx, pap=pap: sx.copy(xbc[:, 3:3 + SEG], pap))
                        op("ve", ["xbc"], ["xhal"],
                           lambda v, et=et: v.tensor_copy(xhal[:, et, :], xbc[:, SEG:SEG + 3]))
                        op("ve", ["xbc", "xbchal", "convw", "convb"], ["xacc"],
                           lambda v, et=et: v.tensor_scalar(xacc[:], xbc[:, 0:SEG],
                                                            cw_s[:, et, 0:1], cb_s[:, et:et + 1],
                                                            op0=OP.mult, op1=OP.add))
                        for k in (1, 2, 3):
                            op("ve", ["xbc", "xbchal", "convw", "xacc"], ["xacc"],
                               lambda v, et=et, k=k: v.scalar_tensor_tensor(
                                   xacc[:], xbc[:, k:SEG + k], cw_s[:, et, k:k + 1], xacc[:],
                                   op0=OP.mult, op1=OP.add))
                        op("sc", ["xacc"], ["xsil"],
                           lambda s: s.activation(xsil[:], xacc[:], AF.Silu))
                        if et < 16:
                            def xtr(t):
                                for c in range(NCH):
                                    ins = t.transpose(pT[:, c * 128:(c + 1) * 128],
                                                      xsil[:, c * 128:(c + 1) * 128], i128b_s[:])
                                return ins
                            op("pe", ["xsil", "i128b"], ["pT"], xtr)
                            op("sc", ["pT"], ["Xr%d" % et],
                               lambda sx, et=et: sx.copy(
                                   Xr[:, :, et * 128:(et + 1) * 128],
                                   pT[:, 0:NCH * 128].rearrange("p (c f) -> p c f", c=NCH)))
                        else:
                            op("ve", ["xsil"], ["BFt"],
                               lambda v: v.tensor_copy(BFt[:], xsil[0:64, :]))
                            dma("po", CFt[:], xsil[64:128, :], ["xsil"], ["CFt"])
                            def btr(t):
                                for c in range(NCH):
                                    ins = t.transpose(pT[:, c * 64:(c + 1) * 64],
                                                      BFt[:, c * 128:(c + 1) * 128],
                                                      i128b_s[0:64, 0:64])
                                return ins
                            op("pe", ["BFt", "i128b"], ["pT"], btr)
                            op("sc", ["pT"], ["BTt"],
                               lambda sx: sx.copy(
                                   BTt[:].rearrange("p c n -> p (c n)"), pT[:, 0:NCH * 64]))

                    emit_amm(0)
                    for et in range(1, 17):
                        emit_amm(et)
                        emit_rest(et - 1)
                    emit_rest(16)
                    dma("po", xr_d[:, sg * NCH * D_INNER:(sg + 1) * NCH * D_INNER],
                        Xr[:].rearrange("p c f -> p (c f)"), xr_deps, ["xrd%d" % sg])
                    dma("sy", bc_d[:, sg * 2 * SEG:sg * 2 * SEG + SEG], BFt[:],
                        ["BFt"], ["bcb%d" % sg])
                    dma("sy", bc_d[:, sg * 2 * SEG + SEG:(sg + 1) * 2 * SEG], CFt[:],
                        ["CFt"], ["bcc%d" % sg])
                    if L == 0 and sg == 0:
                        dump("Xr", Xr[:], xr_deps)
                        dump("BFt", BFt[:], ["BFt"])
                        dump("CFt", CFt[:], ["CFt"])
                        dump("BTt", BTt[:], ["BTt"])
                        dump("decT", decT[:], ["decT"])
                        dump("dzrow", dzrow[:], ["dzrow"])

                    for c in range(NCH):
                        op("ve", xr_deps + ["decT"], ["xds"],
                           lambda v, c=c: v.tensor_tensor(
                               xds[:].rearrange("p (h q) -> p h q", h=N_HEADS),
                               Xr[:, c, :].rearrange("p (h q) -> p h q", h=N_HEADS),
                               decT[:, c, :].unsqueeze(2).broadcast_to([128, N_HEADS, D_HEAD]),
                               op=OP.mult))
                        def smm(t, c=c):
                            for n in range(4):
                                ins = t.matmul(pAt[n][0:64, :], BTt[:, c, :],
                                               xds[:, n * 512:(n + 1) * 512],
                                               start=True, stop=True)
                            return ins
                        op("pe", ["BTt", "xds"], ["pA0", "pA1", "pA2", "pA3"], smm)
                        def stzev(sx):
                            for n in range(4):
                                ins = sx.copy(stz[:, n * 512:(n + 1) * 512], pAt[n][0:64, :])
                            return ins
                        op("sc", ["pA0", "pA1", "pA2", "pA3"], ["stz"], stzev)
                        dma("po", stz_d[:, (sg * NCH + c) * D_INNER:
                                        (sg * NCH + c + 1) * D_INNER],
                            stz[:], ["stz"], ["stzd_%d_%d" % (sg, c)])
                        for n in range(4):
                            op("ve", ["dzrow"], ["dzrx"],
                               lambda v, c=c, n=n: v.tensor_copy(
                                   dzrx[:].rearrange("o (h q) -> o h q", h=8),
                                   dzrow[:, c, n * 8:(n + 1) * 8].unsqueeze(2)
                                       .broadcast_to([1, 8, D_HEAD])))
                            op("pe", ["dzrx", "ones_b", "stz"], ["pA%d" % n],
                               lambda t, n=n: t.matmul(pAt[n][0:64, :],
                                                       ones_b[:, 0:64], dzrx[:],
                                                       start=True, stop=True))
                        def dzev(sx):
                            for n in range(4):
                                ins = sx.copy(dzB[:, n * 512:(n + 1) * 512],
                                              pAt[n][0:64, :])
                            return ins
                        op("sc", ["pA0", "pA1", "pA2", "pA3", "dzB"], ["dzB"], dzev)
                        dma("po", dzx_d[:, (sg * NCH + c) * D_INNER:
                                        (sg * NCH + c + 1) * D_INNER],
                            dzB[:], ["dzB"], ["dzx_%d_%d" % (sg, c)])
                        op("ve", ["dzB", "S_run"], ["S_run"],
                           lambda v: v.tensor_tensor(S_run[:], S_run[:], dzB[:],
                                                     op=OP.mult))
                        op("ve", ["stz", "S_run"], ["S_run"],
                           lambda v: v.tensor_tensor(S_run[:], S_run[:], stz[:], op=OP.add))
                        if dbg and L == 0 and sg == 0 and c == 0:
                            dump("xds0", xds[:], ["xds"])
                            dump("stz0", stz[:], ["stz"])
                        if dbg and L == 0 and sg == 0 and c == 1:
                            dump("Srun1", S_run[:], ["S_run"])

                def seg_B(sg):
                    t0 = sg * SEG
                    dma("sy", hF3.rearrange("p a t -> p (a t)"),
                        hf_d[:, sg * 8 * SEG:(sg + 1) * 8 * SEG],
                        ["hfd%d" % sg], hF_deps)
                    dma("po", Xr[:].rearrange("p c f -> p (c f)"),
                        xr_d[:, sg * NCH * D_INNER:(sg + 1) * NCH * D_INNER],
                        ["xrd%d" % sg], xr_deps)
                    dma("sy", BFt[:], bc_d[:, sg * 2 * SEG:sg * 2 * SEG + SEG],
                        ["bcb%d" % sg], ["BFt"])
                    dma("sy", CFt[:], bc_d[:, sg * 2 * SEG + SEG:(sg + 1) * 2 * SEG],
                        ["bcc%d" % sg], ["CFt"])
                    dma("po", dAcsT[:].rearrange("p c h -> p (c h)"),
                        dtt_d[:, sg * 256:sg * 256 + 128], ["dttd%d" % sg], ["dAcsT"])
                    dma("po", ldT[:].rearrange("p c h -> p (c h)"),
                        dtt_d[:, sg * 256 + 128:sg * 256 + 256], ["dttl%d" % sg], ["ldT"])

                    def emit_proB(c):
                        gts = Gt if c % 2 == 0 else Gt2
                        op("pe", ["BFt", "CFt"], ["pB6"],
                           lambda t, c=c: t.matmul(pB2[:, 0:128],
                                                   BFt[:, c * 128:(c + 1) * 128],
                                                   CFt[:, c * 128:(c + 1) * 128],
                                                   start=True, stop=True))
                        op("ve", ["pB6", "triu"], ["Gt%d" % (c % 2), "pB6"],
                           lambda v, gts=gts: v.tensor_tensor(gts[:], pB2[:, 0:128],
                                                              triu_s[:], op=OP.mult))
                        op("ve", ["S_run"], ["prevb"],
                           lambda v: v.tensor_copy(prevb[:], S_run[:]))
                        dma("sy", stz[:],
                            stz_d[:, (sg * NCH + c) * D_INNER:
                                  (sg * NCH + c + 1) * D_INNER],
                            ["stzd_%d_%d" % (sg, c)], ["stz"])
                        dma("po", dzB[:],
                            dzx_d[:, (sg * NCH + c) * D_INNER:
                                  (sg * NCH + c + 1) * D_INNER],
                            ["dzx_%d_%d" % (sg, c)], ["dzB"])
                        op("ve", ["dzB", "prevb", "S_run"], ["S_run"],
                           lambda v: v.tensor_tensor(S_run[:], S_run[:], dzB[:],
                                                     op=OP.mult))
                        op("ve", ["stz", "S_run"], ["S_run"],
                           lambda v: v.tensor_tensor(S_run[:], S_run[:], stz[:], op=OP.add))
                        if dbg and L == 0 and sg == 0 and c == 1:
                            dump("prev1", prevb[:], ["prevb"])

                    def emit_mb(c):
                        tok0 = t0 + c * CHUNK
                        gts = Gt if c % 2 == 0 else Gt2
                        op("sc", ["dAcsT"], ["expT"],
                           lambda s, c=c: s.activation(expT[:], dAcsT[:, c, :], AF.Exp))
                        def yd_emit(q, h8):
                            mts = Mt if q % 2 == 0 else Mt2
                            def ydmm(t, h8=h8, q=q, c=c, mts=mts):
                                for hh in range(8):
                                    h = h8 + hh
                                    ins = t.matmul(pAt[q][:, hh * 64:(hh + 1) * 64],
                                                   mts[:, hh * 128:(hh + 1) * 128],
                                                   Xr[:, c, h * 64:(h + 1) * 64],
                                                   start=(hh == 0), stop=False)
                                return ins
                            op("pe", ["Mtq%d" % (q % 2), "S_run"] + xr_deps,
                               ["pA%d" % q, "Mtq%d" % (q % 2)], ydmm)
                        def rql(qq4):
                            h84 = qq4 * 8
                            bq = qq4 % 3
                            dma("sy", rowq[32 * bq:32 * bq + 1, :]
                                .rearrange("o (g h l) -> o g h l", g=2, h=8),
                                dach_d[:, tok0:tok0 + CHUNK]
                                .rearrange("(g h) l -> g h l", g=2)[:, h84:h84 + 8, :]
                                .unsqueeze(0),
                                ["dach_d", "dach_d2", "rowf_u%d" % bq],
                                ["rowf%d" % bq])
                        for qq4 in range(3):
                            rql(qq4)
                        pend = []
                        for half in range(2):
                            for q in range(2):
                                h8 = half * 16 + q * 8
                                qq = half * 2 + q
                                if qq % 2 == 0:
                                    pregs = ["pB4", "pB5"]
                                    bcaps = [pB0[:, 0:512], pB0[:, 512:1024]]
                                else:
                                    pregs = ["pA2", "pA3"]
                                    bcaps = [pAt[2][:, :], pAt[3][:, :]]
                                mts = Mt if qq % 2 == 0 else Mt2
                                bq3 = 32 * (qq % 3)
                                rqa = rowq[bq3:bq3 + 1, :]
                                oq = onesq[bq3:bq3 + 1, :]
                                def bcmm(t, bcaps=bcaps, rqa=rqa, oq=oq):
                                    for n in range(2):
                                        t.matmul(bcaps[n], oq,
                                                 rqa[:, n * 512:(n + 1) * 512],
                                                 start=True, stop=False)
                                        ins = t.matmul(bcaps[n], oq,
                                                       rqa[:, 1024 + n * 512:1024 + (n + 1) * 512],
                                                       start=False, stop=True)
                                    return ins
                                op("pe", ["rowf%d" % (qq % 3), "onesq%d" % (qq % 3)],
                                   pregs, bcmm)
                                def madd(v, c=c, h8=h8, bcaps=bcaps):
                                    for n in range(2):
                                        ins = v.tensor_tensor(
                                            bcaps[n].rearrange("p (h l) -> p h l", h=4),
                                            bcaps[n].rearrange("p (h l) -> p h l", h=4),
                                            ldT[:, c, h8 + n * 4:h8 + (n + 1) * 4]
                                                .unsqueeze(2).broadcast_to([128, 4, CHUNK]),
                                            op=OP.add)
                                    return ins
                                op("ve", pregs + ["ldT"],
                                   pregs + ["rowf_u%d" % (qq % 3)], madd)
                                if qq == 0:
                                    rql(3)
                                def mmin(v, bcaps=bcaps):
                                    for n in range(2):
                                        ins = v.tensor_scalar(bcaps[n], bcaps[n], 80.0, 0.0,
                                                              op0=OP.min, op1=OP.bypass)
                                    return ins
                                op("ve", pregs, pregs, mmin)
                                def mexp(sx, bcaps=bcaps):
                                    for n in range(2):
                                        ins = sx.activation(bcaps[n], bcaps[n], AF.Exp)
                                    return ins
                                op("sc", pregs, pregs, mexp)
                                def gmult(v, bcaps=bcaps, mts=mts, gts=gts):
                                    for n in range(2):
                                        ins = v.tensor_tensor(
                                            mts[:, n * 512:(n + 1) * 512]
                                                .rearrange("p (h l) -> p h l", h=4),
                                            bcaps[n].rearrange("p (h l) -> p h l", h=4),
                                            gts[:].unsqueeze(1).broadcast_to([128, 4, CHUNK]),
                                            op=OP.mult)
                                    return ins
                                op("ve", pregs + ["Gt%d" % (c % 2)], ["Mtq%d" % (qq % 2)], gmult)
                                pend.append((qq, h8))
                                if len(pend) >= 2:
                                    yd_emit(*pend.pop(0))
                        for pq in pend:
                            yd_emit(*pq)
                        op("ve", xr_deps + ["Dbt", "xds"], ["xds"],
                           lambda v, c=c: v.tensor_tensor(
                               xds[:].rearrange("p (h q) -> p h q", h=N_HEADS),
                               Xr[:, c, :].rearrange("p (h q) -> p h q", h=N_HEADS),
                               Dbt[:, :].unsqueeze(2).broadcast_to([128, N_HEADS, D_HEAD]),
                               op=OP.mult))
                        def skmm(t):
                            for n in range(4):
                                ins = t.matmul(pAt[n][:, :], i128b_s[:],
                                               xds[:, n * 512:(n + 1) * 512],
                                               start=False, stop=True)
                            return ins
                        op("pe", ["xds", "i128b"], ["pA0", "pA1", "pA2", "pA3"], skmm)
                        for half in range(2):
                            def yomm(t, c=c, half=half):
                                for n in range(2):
                                    ins = t.matmul(
                                        pB0[:, n * 512:(n + 1) * 512],
                                        CFt[:, c * 128:(c + 1) * 128],
                                        prevb[:, half * 1024 + n * 512:half * 1024 + (n + 1) * 512],
                                        start=True, stop=True)
                                return ins
                            op("pe", ["CFt", "prevb"], ["pB4", "pB5"], yomm)
                            op("ve", ["pB4", "pB5", "expT"], ["yosb"],
                               lambda v, half=half: v.tensor_tensor(
                                   yosb[:].rearrange("p (h q) -> p h q", h=16),
                                   pB0[:, 0:1024].rearrange("p (h q) -> p h q", h=16),
                                   expT[:, half * 16:(half + 1) * 16].unsqueeze(2)
                                       .broadcast_to([128, 16, D_HEAD]),
                                   op=OP.mult))
                            def ybadd(v, half=half):
                                for n in range(2):
                                    bk = 2 * half + n
                                    ins = v.tensor_tensor(
                                        Ybf[:, bk * 512:(bk + 1) * 512],
                                        pAt[2 * half + n][:, :],
                                        yosb[:, n * 512:(n + 1) * 512],
                                        op=OP.add)
                                return ins
                            op("ve", ["yosb", "pA%d" % (2 * half), "pA%d" % (2 * half + 1)],
                               ["Ybf%d" % half, "pA%d" % (2 * half), "pA%d" % (2 * half + 1)],
                               ybadd)
                        if dbg and L == 0 and sg == 0 and c == 0:
                            dump("Ybf0d", Ybf[:], ["Ybf0", "Ybf1"])

                    def emit_epi1(c):
                        tok0 = t0 + c * CHUNK
                        for nt in range(4):
                            zreg = "pB6" if nt % 2 == 0 else "pB4"
                            zap = pB2[:, 0:512] if nt % 2 == 0 else pB0[:, 0:512]
                            def zmm(t, c=c, nt=nt, zap=zap):
                                for dt in range(8):
                                    ins = t.matmul(zap,
                                                   hF3[:, dt, c * 128:(c + 1) * 128],
                                                   WB[:, dt, nt * 512:(nt + 1) * 512],
                                                   start=(dt == 0), stop=(dt == 7))
                                return ins
                            op("pe", ["WB"] + hF_deps + ["zs%d" % nt], [zreg], zmm)
                            op("sc", [zreg], ["zs%d" % nt, zreg],
                               lambda s, nt=nt, zap=zap: s.activation(
                                   zs[:, nt * 512:(nt + 1) * 512], zap, AF.Silu))
                        if dbg and L == 0 and sg == 0 and c == 0:
                            dump("zs0d", zs[:], ["zs0", "zs1", "zs2", "zs3"])
                        op("ve", ["Ybf0", "Ybf1", "zs0", "zs1", "zs2", "zs3"], ["yg"],
                           lambda v: v.tensor_tensor(yg[:], Ybf[:], zs[:], op=OP.mult))
                        op("sc", ["yg"], ["ssqy", "Ybf0", "Ybf1"],
                           lambda s: s.activation(Ybf[:], yg[:], AF.Square,
                                                  accum_out=sc1[:, 4:5]))
                        op("ve", ["ssqy"], ["msy"],
                           lambda v: v.tensor_scalar(sc1[:, 5:6], sc1[:, 4:5], 1.0 / D_INNER,
                                                     EPS, op0=OP.mult, op1=OP.add))
                        op("sc", ["msy"], ["sqy"],
                           lambda s: s.activation(sc1[:, 6:7], sc1[:, 5:6], AF.Ln))
                        op("ve", ["sqy"], ["sqy"],
                           lambda v: v.tensor_scalar(sc1[:, 6:7], sc1[:, 6:7], -0.5, 0.0,
                                                     op0=OP.mult, op1=OP.bypass))
                        op("sc", ["sqy"], ["rsy"],
                           lambda s: s.activation(sc1[:, 7:8], sc1[:, 6:7], AF.Exp))
                        op("ve", ["rsy", "yg"], ["yg"],
                           lambda v: v.tensor_scalar(yn[:], yg[:], sc1[:, 7:8], 0.0,
                                                     op0=OP.mult, op1=OP.bypass))
                        if dbg and L == 0 and sg == 0 and c == 0:
                            dump("yn0", yn[:], ["yg"])
                    def emit_epi2(c):
                        tok0 = t0 + c * CHUNK
                        for g2 in range(2):
                            def ytr(t, g2=g2):
                                for it in range(8):
                                    ins = t.transpose(
                                        pT[:, it * 128:(it + 1) * 128],
                                        yn[:, (g2 * 8 + it) * 128:(g2 * 8 + it + 1) * 128],
                                        i128b_s[:])
                                return ins
                            op("pe", ["yg", "i128b"], ["pT"], ytr)
                            op("ve", ["pT"], ["yF%d" % g2],
                               lambda v, g2=g2: v.tensor_copy(
                                   yF[:, g2 * 8:(g2 + 1) * 8, :].rearrange("p i f -> p (i f)"),
                                   pT[:]))
                        dma("sy", rbuf2[:], rin[tok0:tok0 + CHUNK, :], ["rbuf2_u"], ["rbuf2"])
                        if L == 1:
                            dma("sy", rbufh[:], xin[tok0:tok0 + CHUNK, :], ["rbufh_u"], ["rbufh"])
                        def opmm(t):
                            for dh in range(2):
                                for it in range(16):
                                    ins = t.matmul(pB0[:, dh * 512:(dh + 1) * 512], yF[:, it, :],
                                                   OWT[:, it, dh * 512:(dh + 1) * 512],
                                                   start=(it == 0), stop=(it == 15))
                            return ins
                        op("pe", ["yF0", "yF1", "OWT"], ["pB4", "pB5"], opmm)
                        op("ve", ["pB4", "pB5", "rbuf2"], ["rbuf2", "pB4", "pB5"],
                           lambda v: v.tensor_tensor(rbuf2[:], rbuf2[:], pB0[:, 0:1024],
                                                     op=OP.add))
                        if L == 1:
                            op("ve", ["rbuf2", "rbufh"], ["rbuf2", "rbufh_u"],
                               lambda v: v.tensor_tensor(rbuf2[:], rbuf2[:], rbufh[:],
                                                         op=OP.add))
                        dma("sy", rout[tok0:tok0 + CHUNK, :], rbuf2[:], ["rbuf2"],
                            ["rbuf2_u", "rstore"])

                    emit_proB(0)
                    emit_mb(0)
                    for c in range(1, NCH):
                        emit_epi1(c - 1)
                        emit_proB(c)
                        emit_epi2(c - 1)
                        emit_mb(c)
                    emit_epi1(NCH - 1)
                    emit_epi2(NCH - 1)

                for sg in range(NSEG):
                    seg_A(sg)
                pub2(shS_d[L], dum_d[:, :], S_run[:], L, 4 + L, 1 + L,
                     ["S_run"], ["spub%d" % L])
                init_pair(S_run[:], shS_d[L], zz_d[:, :], L, MAGICS[L],
                          pollS[L], ["spub%d" % L, "zzd"], ["S_run"])
                for sg in range(NSEG):
                    seg_B(sg)
                if L == 0:
                    pub2(shH_d[:, :], dum_d[0:3, 0:D_MODEL], rout[TH - 3:TH, :],
                         2, 6, 3, ["rstore"], ["hpub"])

            mamba_layer(0, xin, r1)
            mamba_layer(1, r1, r2)

            snap = [(s, s.count) for s in allsems]
            for en in ("sy", "po", "pe", "sc", "ve"):
                def bar(eng, snap=snap):
                    for s, v in snap:
                        if v > 0:
                            eng.wait_ge(s.h, v)
                P.raw(en, bar)
                for s, v in snap:
                    P.hw[(en, s.name)] = max(P.hw.get((en, s.name), 0), v)

        RM = sbT("RM", [128, NTT, D_MODEL], F32)
        XF = sbT("XF", [128, 8, moe_tok], BF16)
        W1q = sbT("W1q", [128, 8, 1024], BF16)
        W2q = sbT("W2q", [128, 8, 1024], BF16)
        W1q2 = sbT("W1q2", [128, 8, 1024], BF16)
        W2q2 = sbT("W2q2", [128, 8, 1024], BF16)
        ab = sbT("ab", [128, 8, 512], BF16)
        wg_s = sbT("wg_s", [128, 8, 8], BF16)
        bgb = sbT("bgb", [128, 8], F32)
        bgr = sbT("bgr", [1, 8], F32)
        b1q = sbT("b1q", [128, 8], F32)
        b2b = sbT("b2b", [8, D_MODEL], BF16)
        logt = sbT("logt", [128, NTT, 8], F32)
        comb = sbT("comb", [128, NTT, 8], F32)
        combB = sbT("combB", [128, NTT, 8], BF16)
        combF = sbT("combF", [8, NTT * 128], BF16)

        dma("po", wg_s[:], Wg.rearrange("(dt p) e -> p dt e", p=128), [], ["wg"])
        dma("po", b2b[:], b2, [], ["b2b"])
        dma("sy", bgr[:], bg, [], ["bgr"])

        rlane = POL[0]
        rbase = rlane.count

        def rload(g):
            with nc.gpsimd.register(rname("r_rk")) as r_rk, \
                 nc.gpsimd.register(rname("r_c")) as r_c, \
                 nc.gpsimd.register(rname("r_v")) as r_v, \
                 nc.gpsimd.register(rname("r_w")) as r_w:
                g.reg_load(r_rk, rank_s[:1, :1])
                for rk in range(min(n_ranks, T // moe_tok)):
                    par = rk & 1
                    P0 = (rk * moe_tok) // TH
                    own = (par == P0)
                    partner = rk ^ 1
                    myloc = rk * moe_tok - P0 * TH
                    with g.If_eq(r_rk, rk):
                        if own:
                            pnon = ((partner & 1) != ((partner * moe_tok) // TH))
                            if pnon:
                                ploc = partner * moe_tok - P0 * TH
                                g.dma_start(out=shR_d[0:moe_tok, :],
                                            in_=r2[ploc:ploc + moe_tok, :]
                                            ).then_inc(rlane.h, 16)
                                g.wait_ge(rlane.h, rbase + 16)
                                g.dma_start(out=shF_d[3:4, 0:1],
                                            in_=rank_s[:1, 4:5]).then_inc(rlane.h, 16)
                            else:
                                g.dma_start(out=shF_d[7:8, 0:1],
                                            in_=rank_s[:1, 4:5]).then_inc(rlane.h, 16)
                                g.wait_ge(rlane.h, rbase + 16)
                                g.dma_start(out=shF_d[7:8, 1:2],
                                            in_=rank_s[:1, 4:5]).then_inc(rlane.h, 16)
                            g.dma_start(out=RM[:],
                                        in_=r2[myloc:myloc + moe_tok, :]
                                        .rearrange("(c p) d -> p c d", p=128)
                                        ).then_inc(rlane.h, 16)
                        else:
                            g.dma_start(out=shF_d[7:8, 2:3],
                                        in_=rank_s[:1, 4:5]).then_inc(rlane.h, 16)
                            g.wait_ge(rlane.h, rbase + 16)
                            g.dma_start(out=shF_d[7:8, 3:4],
                                        in_=rank_s[:1, 4:5]).then_inc(rlane.h, 16)
                            g.reg_mov(r_c, 1)
                            g.reg_mov(r_w, 0)
                            with g.While(r_c):
                                g.reg_add(r_w, r_w, 16)
                                g.dma_start(out=pollb[:1, 0:1],
                                            in_=shF_d[3:4, 0:1]).then_inc(pollM.h, 16)
                                g.wait_ge(pollM.h, r_w)
                                g.reg_load(r_v, pollb[:1, 0:1])
                                g.reg_alu(r_c, r_v, MAGICS[3], OP.bitwise_xor)
                            g.dma_start(out=RM[:],
                                        in_=shR_d[0:moe_tok, :]
                                        .rearrange("(c p) d -> p c d", p=128)
                                        ).then_inc(rlane.h, 16)
                    g.end_ifs()
            return None
        P.op("po", rlane, ["rank", "rstore", "shFz", "lane_" + rlane.name],
             ["RMall", "lane_" + rlane.name], rload, inc_amt=48, inc_attached=True)

        for tt in range(NTT):
            op("ve", ["RMall", "RM%d" % tt], ["lnm"],
               lambda v, tt=tt: v.reduce_sum(sc1[:, 0:1], RM[:, tt, :], axis=AX.X))
            op("ve", ["lnm"], ["lnm2"],
               lambda v: v.tensor_scalar(sc1[:, 1:2], sc1[:, 0:1], 1.0 / D_MODEL, 0.0,
                                         op0=OP.mult, op1=OP.bypass))
            op("ve", ["lnm2", "RMall"], ["RM%d" % tt],
               lambda v, tt=tt: v.tensor_scalar(RM[:, tt, :], RM[:, tt, :], sc1[:, 1:2],
                                                0.0, op0=OP.subtract, op1=OP.bypass))
            op("sc", ["RM%d" % tt], ["lnv", "hT"],
               lambda s, tt=tt: s.activation(hT[:], RM[:, tt, :], AF.Square,
                                             accum_out=sc1[:, 2:3]))
            op("ve", ["lnv"], ["lnv2"],
               lambda v: v.tensor_scalar(sc1[:, 3:4], sc1[:, 2:3], 1.0 / D_MODEL, EPS,
                                         op0=OP.mult, op1=OP.add))
            op("sc", ["lnv2"], ["lnsq"],
               lambda s: s.activation(sc1[:, 4:5], sc1[:, 3:4], AF.Ln))
            op("ve", ["lnsq"], ["lnsq"],
               lambda v: v.tensor_scalar(sc1[:, 4:5], sc1[:, 4:5], -0.5, 0.0,
                                         op0=OP.mult, op1=OP.bypass))
            op("sc", ["lnsq"], ["lnr"],
               lambda s: s.activation(sc1[:, 5:6], sc1[:, 4:5], AF.Exp))
            op("ve", ["lnr", "RM%d" % tt], ["RM%d" % tt],
               lambda v, tt=tt: v.tensor_scalar(RM[:, tt, :], RM[:, tt, :], sc1[:, 5:6],
                                                0.0, op0=OP.mult, op1=OP.bypass))
            op("ve", ["RM%d" % tt], ["hT"],
               lambda v, tt=tt: v.tensor_scalar(hT[:], RM[:, tt, :], 1.0, 0.0,
                                                op0=OP.mult, op1=OP.bypass))
            def xtr2(t):
                for dt in range(8):
                    ins = t.transpose(pT[:, dt * 128:(dt + 1) * 128],
                                      hT[:, dt * 128:(dt + 1) * 128], i128b_s[:])
                return ins
            op("pe", ["hT", "i128b"], ["pT"], xtr2)
            op("ve", ["pT"], ["XF%d" % tt],
               lambda v, tt=tt: v.tensor_copy(
                   XF[:, :, tt * 128:(tt + 1) * 128],
                   pT[:].rearrange("p (a s) -> p a s", a=8)))
        xf_deps = ["XF%d" % tt for tt in range(NTT)]

        op("pe", ["bgr", "ones_f"], ["pB6"],
           lambda t: t.matmul(pB2[:, 0:8], ones_f[:, 0:128], bgr[:], start=True, stop=True))
        op("ve", ["pB6"], ["bgb", "pB6"], lambda v: v.tensor_copy(bgb[:], pB2[:, 0:8]))
        for tt in range(NTT):
            def lgmm(t, tt=tt):
                for dt in range(8):
                    ins = t.matmul(pB2[:, 0:8], XF[:, dt, tt * 128:(tt + 1) * 128],
                                   wg_s[:, dt, :], start=(dt == 0), stop=(dt == 7))
                return ins
            op("pe", ["XF%d" % tt, "wg"], ["pB6"], lgmm)
            op("ve", ["pB6", "bgb"], ["logt%d" % tt, "pB6"],
               lambda v, tt=tt: v.tensor_tensor(logt[:, tt, :], pB2[:, 0:8], bgb[:], op=OP.add))
            op("ve", ["logt%d" % tt], ["m1"],
               lambda v, tt=tt: v.reduce_max(sc1[:, 0:1], logt[:, tt, :], axis=AX.X))
            op("ve", ["m1", "logt%d" % tt], ["cm%d" % tt],
               lambda v, tt=tt: v.tensor_scalar(comb[:, tt, :], logt[:, tt, :], sc1[:, 0:1],
                                                0.0, op0=OP.is_ge, op1=OP.bypass))
            op("ve", ["cm%d" % tt, "logt%d" % tt], ["cm%d" % tt],
               lambda v, tt=tt: v.scalar_tensor_tensor(comb[:, tt, :], comb[:, tt, :], -1e30,
                                                       logt[:, tt, :], op0=OP.mult, op1=OP.add))
            op("ve", ["cm%d" % tt], ["m2"],
               lambda v, tt=tt: v.reduce_max(sc1[:, 1:2], comb[:, tt, :], axis=AX.X))
            op("ve", ["m2", "logt%d" % tt], ["c0%d" % tt],
               lambda v, tt=tt: v.tensor_scalar(comb[:, tt, :], logt[:, tt, :], sc1[:, 1:2],
                                                0.0, op0=OP.is_ge, op1=OP.bypass))
            op("ve", ["m1"], ["nm1"],
               lambda v: v.tensor_scalar(sc1[:, 2:3], sc1[:, 0:1], -1.0, 0.0,
                                         op0=OP.mult, op1=OP.bypass))
            op("ve", ["m2", "nm1"], ["dm"],
               lambda v: v.tensor_tensor(sc1[:, 3:4], sc1[:, 1:2], sc1[:, 2:3], op=OP.add))
            op("sc", ["nm1", "logt%d" % tt], ["el%d" % tt],
               lambda s, tt=tt: s.activation(logt[:, tt, :], logt[:, tt, :], AF.Exp,
                                             bias=sc1[:, 2:3]))
            op("sc", ["dm"], ["ed"],
               lambda s: s.activation(sc1[:, 4:5], sc1[:, 3:4], AF.Exp))
            op("ve", ["ed"], ["zd"],
               lambda v: v.tensor_scalar(sc1[:, 5:6], sc1[:, 4:5], 1.0, 0.0,
                                         op0=OP.add, op1=OP.bypass))
            op("ve", ["zd"], ["rz"], lambda v: v.reciprocal(sc1[:, 6:7], sc1[:, 5:6]))
            op("ve", ["c0%d" % tt, "el%d" % tt], ["comb%d" % tt],
               lambda v, tt=tt: v.tensor_tensor(comb[:, tt, :], comb[:, tt, :],
                                                logt[:, tt, :], op=OP.mult))
            op("ve", ["rz", "comb%d" % tt], ["comb%d" % tt],
               lambda v, tt=tt: v.tensor_scalar(comb[:, tt, :], comb[:, tt, :], sc1[:, 6:7],
                                                0.0, op0=OP.mult, op1=OP.bypass))
            op("ve", ["comb%d" % tt], ["combB%d" % tt],
               lambda v, tt=tt: v.tensor_copy(combB[:, tt, :], comb[:, tt, :]))
        for cg in range(0, NTT, 8):
            ng = min(8, NTT - cg)
            def ctr(t, cg=cg, ng=ng):
                for j in range(ng):
                    ins = t.transpose(pT[0:8, j * 128:(j + 1) * 128],
                                      combB[:, cg + j, :], i128b_s[:])
                return ins
            op("pe", ["i128b"] + ["combB%d" % (cg + j) for j in range(ng)], ["pT"], ctr)
            op("ve", ["pT"], ["combF"],
               lambda v, cg=cg, ng=ng: v.tensor_copy(
                   combF[:, cg * 128:(cg + ng) * 128], pT[0:8, 0:ng * 128]))
        for tt in range(NTT):
            def b2mm(t, tt=tt):
                for dh in range(2):
                    ins = t.matmul(pB0[:, dh * 512:(dh + 1) * 512],
                                   combF[:, tt * 128:(tt + 1) * 128],
                                   b2b[:, dh * 512:(dh + 1) * 512], start=True, stop=True)
                return ins
            op("pe", ["combF", "b2b"], ["pB4", "pB5"], b2mm)
            op("ve", ["pB4", "pB5", "RM%d" % tt], ["RM%d" % tt, "pB4", "pB5"],
               lambda v, tt=tt: v.tensor_tensor(RM[:, tt, :], RM[:, tt, :], pB0[:, 0:1024],
                                                op=OP.add))

        for e in range(8):
            gfun = AF.Gelu if e % 2 == 0 else AF.Silu
            for q in range(4):
                par = (e * 4 + q) % 2
                wa = W1q if par == 0 else W1q2
                wb_ = W2q if par == 0 else W2q2
                wtag, w2tag = "W1q%d" % par, "W2q%d" % par
                dma("po", wa[:],
                    W1[e].rearrange("(dt p) f -> p dt f", p=128)[:, :, q * 1024:(q + 1) * 1024],
                    [], [wtag])
                dma("po", wb_[:],
                    W2[e][q * 1024:(q + 1) * 1024, :].rearrange("(ft p) d -> p ft d", p=128),
                    [], [w2tag])
                dma("sy", b1q[:], b1[e, q], ["b1u"], ["b1q"])
                for tt2 in range(moe_tok // 512):
                    for fg in range(2):
                        def hmm(t, q=q, tt2=tt2, fg=fg, wa=wa):
                            for ft in range(4):
                                for dt in range(8):
                                    ins = t.matmul(
                                        pAt[ft][:, :],
                                        wa[:, dt, (fg * 4 + ft) * 128:(fg * 4 + ft + 1) * 128],
                                        XF[:, dt, tt2 * 512:(tt2 + 1) * 512],
                                        start=(dt == 0), stop=(dt == 7))
                            return ins
                        op("pe", [wtag, "ab%d" % fg] + xf_deps,
                           ["pA0", "pA1", "pA2", "pA3"], hmm)
                        def hact(s, fg=fg, gfun=gfun):
                            for ft in range(4):
                                ins = s.activation(ab[:, fg * 4 + ft, :],
                                                   pAt[ft][:, :], gfun,
                                                   bias=b1q[:, fg * 4 + ft:fg * 4 + ft + 1])
                            return ins
                        op("sc", ["pA0", "pA1", "pA2", "pA3", "b1q"],
                           ["ab%d" % fg, "b1u", "pA0", "pA1", "pA2", "pA3"], hact)
                    for mt in range(4):
                        def ymm(t, mt=mt, wb_=wb_):
                            for dh in range(2):
                                for ft in range(8):
                                    ins = t.matmul(pB0[:, dh * 512:(dh + 1) * 512],
                                                   ab[:, ft, mt * 128:(mt + 1) * 128],
                                                   wb_[:, ft, dh * 512:(dh + 1) * 512],
                                                   start=(ft == 0), stop=(ft == 7))
                            return ins
                        op("pe", [w2tag, "ab0", "ab1"], ["pB4", "pB5"], ymm)
                        tt = tt2 * 4 + mt
                        op("ve", ["pB4", "pB5", "comb%d" % tt, "RM%d" % tt],
                           ["RM%d" % tt, "pB4", "pB5", "ab0", "ab1"],
                           lambda v, tt=tt, e=e: v.scalar_tensor_tensor(
                               RM[:, tt, :], pB0[:, 0:1024], comb[:, tt, e:e + 1],
                               RM[:, tt, :], op0=OP.mult, op1=OP.add))

        for tt in range(NTT):
            op("ve", ["RM%d" % tt], ["m3"],
               lambda v, tt=tt: v.reduce_sum(sc1[:, 0:1], RM[:, tt, :], axis=AX.X))
            op("ve", ["m3"], ["m4"],
               lambda v: v.tensor_scalar(sc1[:, 1:2], sc1[:, 0:1], 1.0 / D_MODEL, 0.0,
                                         op0=OP.mult, op1=OP.bypass))
            op("ve", ["m4", "RM%d" % tt], ["RM%d" % tt],
               lambda v, tt=tt: v.tensor_scalar(RM[:, tt, :], RM[:, tt, :], sc1[:, 1:2],
                                                0.0, op0=OP.subtract, op1=OP.bypass))
            op("sc", ["RM%d" % tt], ["v2", "hT"],
               lambda s, tt=tt: s.activation(hT[:], RM[:, tt, :], AF.Square,
                                             accum_out=sc1[:, 2:3]))
            op("ve", ["v2"], ["v3"],
               lambda v: v.tensor_scalar(sc1[:, 3:4], sc1[:, 2:3], 1.0 / D_MODEL, EPS,
                                         op0=OP.mult, op1=OP.add))
            op("sc", ["v3"], ["v4"], lambda s: s.activation(sc1[:, 4:5], sc1[:, 3:4], AF.Ln))
            op("ve", ["v4"], ["v4"],
               lambda v: v.tensor_scalar(sc1[:, 4:5], sc1[:, 4:5], -0.5, 0.0,
                                         op0=OP.mult, op1=OP.bypass))
            op("sc", ["v4"], ["v5"], lambda s: s.activation(sc1[:, 5:6], sc1[:, 4:5], AF.Exp))
            op("ve", ["v5", "RM%d" % tt], ["RM%d" % tt],
               lambda v, tt=tt: v.tensor_scalar(RM[:, tt, :], RM[:, tt, :], sc1[:, 5:6],
                                                0.0, op0=OP.mult, op1=OP.bypass))
            dma("sy", out[tt * 128:(tt + 1) * 128, :], RM[:, tt, :], ["RM%d" % tt],
                ["outs%d" % tt])

        def fin(eng):
            for s in allsems:
                if s.count > 0:
                    eng.wait_ge(s.h, s.count)
        P.raw("sy", fin)

        block = es.enter_context(nc.Block())

        @block.sync
        def _(sy):
            for name, f in P.script:
                if name == "sy":
                    f(sy)

        @block.gpsimd
        def _(g):
            for name, f in P.script:
                if name == "po":
                    f(g)

        @block.tensor
        def _(t):
            for name, f in P.script:
                if name == "pe":
                    f(t)

        @block.scalar
        def _(s):
            for name, f in P.script:
                if name == "sc":
                    f(s)

        @block.vector
        def _(v):
            for name, f in P.script:
                if name == "ve":
                    f(v)

    return nc


def _host_inputs(inputs, T, n_ranks):
    x = np.ascontiguousarray(np.asarray(inputs["L0_attn_x"], np.float32).reshape(-1, D_MODEL))[:T]
    inw = np.asarray(inputs["L0_attn_in_proj_w"], np.float32)
    TH = T // 2
    moe_tok = T // n_ranks
    base = dict(
        Wg=np.ascontiguousarray(np.asarray(inputs["L0_ffn_Wg"], np.float32)),
        bg=np.ascontiguousarray(np.asarray(inputs["L0_ffn_bg"], np.float32).reshape(1, 8)),
        W1=np.ascontiguousarray(np.asarray(inputs["L0_ffn_W1"], np.float32)),
        b1=np.ascontiguousarray(np.asarray(inputs["L0_ffn_b1"], np.float32).reshape(8, 4, 8, 128).transpose(0, 1, 3, 2)),
        W2=np.ascontiguousarray(np.asarray(inputs["L0_ffn_W2"], np.float32)),
        b2=np.ascontiguousarray(np.asarray(inputs["L0_ffn_b2"], np.float32)),
        i128f=np.eye(128, dtype=np.float32),
        i128b=np.eye(128, dtype=np.float32),
        triuc=np.triu(np.ones((128, 128), np.float32)),
        onesr=np.ones((1, 128), np.float32),
        cvec=np.tile(np.array([[EPS, 1.0, 0.0, 0.0]], np.float32), (128, 1)),
    )
    for L in range(2):
        base[f"inwT{L}"] = np.ascontiguousarray(inw[L].T)
        base[f"convw{L}"] = np.ascontiguousarray(np.asarray(inputs["L0_attn_conv_w"], np.float32)[L])
        base[f"convb{L}"] = np.ascontiguousarray(np.asarray(inputs["L0_attn_conv_b"], np.float32)[L].reshape(17, 128).T)
        base[f"dtb{L}"] = np.ascontiguousarray(np.asarray(inputs["L0_attn_dt_bias"], np.float32)[L])
        base[f"alog{L}"] = np.ascontiguousarray(np.asarray(inputs["L0_attn_A_log"], np.float32)[L])
        base[f"dsk{L}"] = np.ascontiguousarray(np.asarray(inputs["L0_attn_D"], np.float32)[L])
        base[f"nw{L}"] = np.ascontiguousarray(np.asarray(inputs["L0_attn_norm_w"], np.float32)[L].reshape(8, 128).T)
        base[f"gw{L}"] = np.ascontiguousarray(np.asarray(inputs["L0_attn_gnorm_w"], np.float32)[L].reshape(16, 128).T)
        base[f"owT{L}"] = np.ascontiguousarray(np.asarray(inputs["L0_attn_out_proj_w"], np.float32)[L].T)
    maps = []
    for rk in range(n_ranks):
        m = dict(base)
        par = rk & 1
        m["xin"] = np.ascontiguousarray(x[par * TH:(par + 1) * TH])
        hxv = np.zeros((3, D_MODEL), np.float32)
        if par == 1:
            hxv[:] = x[TH - 3:TH]
        m["hx"] = hxv
        rv = np.zeros((1, 16), np.int32)
        rv[0, 0] = rk
        rv[0, 1:5] = [1111, 2222, 3333, 4444]
        rv[0, 9] = par
        rv[0, 10] = 1 if par == ((rk * moe_tok) // TH) else 0
        m["rankin"] = rv
        maps.append(m)
    return maps


def kernel(**inputs):
    T = 16384
    n_ranks = 8
    moe_tok = T // n_ranks
    nc = build(T, n_ranks, moe_tok)
    in_maps = _host_inputs(inputs, T, n_ranks)
    res = run_bass_kernel_spmd(nc, in_maps, core_ids=list(range(n_ranks)))
    parts = [res.results[rk]["out"] for rk in range(n_ranks)]
    full = np.concatenate(parts, axis=0).astype(np.float32)
    return full.reshape(1, T, D_MODEL)


# version 11
# speedup vs baseline: 1.1272x; 1.1272x over previous
import numpy as np
from contextlib import ExitStack

import concourse.bass as bass
import concourse.mybir as mybir
from concourse.bass_utils import run_bass_kernel_spmd
from concourse.alu_op_type import AluOpType as OP

F32 = mybir.dt.float32
BF16 = mybir.dt.bfloat16
I32 = mybir.dt.int32
AF = mybir.ActivationFunctionType
AX = mybir.AxisListType

D_MODEL = 1024
D_INNER = 2048
N_HEADS = 32
D_HEAD = 64
D_STATE = 64
CONV_DIM = D_INNER + 2 * D_STATE
D_IN_PROJ = 2 * D_INNER + 2 * D_STATE + N_HEADS
EPS = 1e-5
CHUNK = 128
SEG = 512
NCH = SEG // CHUNK
E_XBC0 = D_INNER
E_DT0 = D_INNER + CONV_DIM


class Sem:
    __slots__ = ("name", "cm", "h", "count")

    def __init__(self, nc, name):
        self.name = name
        self.cm = nc.semaphore(name)
        self.h = None
        self.count = 0


class Planner:
    def __init__(self):
        self.script = []
        self.hw = {}
        self.res = {}

    def _wait_list(self, ename, reads, writes):
        need = {}
        def add(sem, val):
            if val <= 0:
                return
            if self.hw.get((ename, sem.name), 0) >= val:
                return
            if need.get(sem, 0) < val:
                need[sem] = val
        for n in reads:
            r = self.res.get(n)
            if r and r['w']:
                add(*r['w'])
        for n in writes:
            r = self.res.get(n)
            if not r:
                continue
            if r['w']:
                add(*r['w'])
            for m in r['r']:
                add(*m)
        out = list(need.items())
        for s, v in out:
            self.hw[(ename, s.name)] = v
        return out

    def op(self, ename, sem, reads, writes, emit, inc_amt=1, inc_attached=False):
        waits = self._wait_list(ename, reads, writes)
        sem.count += inc_amt
        val = sem.count
        for n in writes:
            self.res[n] = {'w': (sem, val), 'r': []}
        for n in reads:
            r = self.res.setdefault(n, {'w': None, 'r': []})
            r['r'].append((sem, val))

        def closure(eng, waits=waits, emit=emit, sem=sem, inc_amt=inc_amt,
                    inc_attached=inc_attached):
            for s, v in waits:
                eng.wait_ge(s.h, v)
            ins = emit(eng)
            if not inc_attached:
                if ins is None:
                    ins = eng.nop()
                ins.then_inc(sem.h, inc_amt)
        self.script.append((ename, closure))

    def raw(self, ename, fn):
        self.script.append((ename, fn))


def build(T, n_ranks, moe_tok, dbg=False):
    nc = bass.Bass("TRN2", target_bir_lowering=False)
    NSEG = (T // 2) // SEG
    NTT = moe_tok // 128

    TH = T // 2
    xin = nc.dram_tensor("xin", [TH, D_MODEL], F32, kind="ExternalInput").ap()
    w = {}
    for L in range(2):
        w[L] = dict(
            inwT=nc.dram_tensor(f"inwT{L}", [D_MODEL, D_IN_PROJ], F32, kind="ExternalInput").ap(),
            convw=nc.dram_tensor(f"convw{L}", [CONV_DIM, 4], F32, kind="ExternalInput").ap(),
            convb=nc.dram_tensor(f"convb{L}", [128, 17], F32, kind="ExternalInput").ap(),
            dtb=nc.dram_tensor(f"dtb{L}", [N_HEADS], F32, kind="ExternalInput").ap(),
            alog=nc.dram_tensor(f"alog{L}", [N_HEADS], F32, kind="ExternalInput").ap(),
            dsk=nc.dram_tensor(f"dsk{L}", [N_HEADS], F32, kind="ExternalInput").ap(),
            nw=nc.dram_tensor(f"nw{L}", [128, 8], F32, kind="ExternalInput").ap(),
            gw=nc.dram_tensor(f"gw{L}", [128, 16], F32, kind="ExternalInput").ap(),
            owT=nc.dram_tensor(f"owT{L}", [D_INNER, D_MODEL], F32, kind="ExternalInput").ap(),
        )
    Wg = nc.dram_tensor("Wg", [D_MODEL, 8], F32, kind="ExternalInput").ap()
    bg = nc.dram_tensor("bg", [1, 8], F32, kind="ExternalInput").ap()
    W1 = nc.dram_tensor("W1", [8, D_MODEL, 4096], F32, kind="ExternalInput").ap()
    b1 = nc.dram_tensor("b1", [8, 4, 128, 8], F32, kind="ExternalInput").ap()
    W2 = nc.dram_tensor("W2", [8, 4096, D_MODEL], F32, kind="ExternalInput").ap()
    b2 = nc.dram_tensor("b2", [8, D_MODEL], F32, kind="ExternalInput").ap()
    i128f = nc.dram_tensor("i128f", [128, 128], F32, kind="ExternalInput").ap()
    i128b = nc.dram_tensor("i128b", [128, 128], F32, kind="ExternalInput").ap()
    triuc = nc.dram_tensor("triuc", [128, 128], F32, kind="ExternalInput").ap()
    onesr = nc.dram_tensor("onesr", [1, 128], F32, kind="ExternalInput").ap()
    cvec = nc.dram_tensor("cvec", [128, 4], F32, kind="ExternalInput").ap()
    rankin = nc.dram_tensor("rankin", [1, 16], I32, kind="ExternalInput").ap()

    out = nc.dram_tensor("out", [moe_tok, D_MODEL], F32, kind="ExternalOutput").ap()

    rkind = "ExternalOutput" if dbg else "Internal"
    r1 = nc.dram_tensor("r1d", [T // 2, D_MODEL], F32, kind=rkind).ap()
    r2 = nc.dram_tensor("r2d", [T // 2, D_MODEL], F32, kind=rkind).ap()
    dach_d = nc.dram_tensor("dachd", [2 * N_HEADS, T // 2], BF16, kind="Internal").ap()
    xr_d = nc.dram_tensor("xr_d", [128, NSEG * NCH * D_INNER], BF16, kind="Internal").ap()
    hf_d = nc.dram_tensor("hf_d", [128, NSEG * 8 * SEG], BF16, kind="Internal").ap()
    bc_d = nc.dram_tensor("bc_d", [64, NSEG * 2 * SEG], BF16, kind="Internal").ap()
    dtt_d = nc.dram_tensor("dtt_d", [128, NSEG * 2 * NCH * N_HEADS], F32, kind="Internal").ap()
    dzx_d = nc.dram_tensor("dzx_d", [64, NSEG * NCH * D_INNER], BF16, kind="Internal").ap()
    stz_d = nc.dram_tensor("stz_d", [64, NSEG * NCH * D_INNER], BF16, kind="Internal").ap()
    zz_d = nc.dram_tensor("zz_d", [64, D_INNER], F32, kind="Internal").ap()
    dum_d = nc.dram_tensor("dum_d", [64, D_INNER], F32, kind="Internal").ap()
    shS_d = nc.dram_tensor("shS_d", [2, 64, D_INNER], F32, kind="Internal",
                           addr_space="Shared").ap()
    shH_d = nc.dram_tensor("shH_d", [3, D_MODEL], F32, kind="Internal",
                           addr_space="Shared").ap()
    shR_d = nc.dram_tensor("shR_d", [2048, D_MODEL], F32, kind="Internal",
                           addr_space="Shared").ap()
    shF_d = nc.dram_tensor("shF_d", [8, 16], I32, kind="Internal",
                           addr_space="Shared").ap()
    hx = nc.dram_tensor("hx", [3, D_MODEL], F32, kind="ExternalInput").ap()

    P = Planner()
    dbg_t = {}
    es = ExitStack()
    with es:
        sbT = lambda name, shape, dt: es.enter_context(nc.sbuf_tensor(name, shape, dt))
        psT = lambda name, shape, dt: es.enter_context(nc.psum_tensor(name, shape, dt))

        consts = sbT("consts", [128, 4], F32)
        i128b_s = sbT("i128b_s", [128, 128], BF16)
        i128f_s = sbT("i128f_s", [128, 128], F32)
        triu_s = sbT("triu_s", [128, 128], BF16)
        ones_b = sbT("ones_b", [1, 128], BF16)
        onesq = sbT("onesq", [65, 128], BF16)
        ones_f = sbT("ones_f", [1, 128], F32)
        rank_s = sbT("rank_s", [1, 16], I32)
        pollb = sbT("pollb", [1, 16], I32)
        zf = sbT("zf", [4, 16], I32)
        sc1 = sbT("sc1", [128, 8], F32)
        hT = sbT("hT", [128, D_MODEL], BF16)

        pAt = [psT("pAb%d" % i, [128, 512], F32) for i in range(4)]
        pB0 = psT("pB0", [128, 1024], F32)
        pB2 = psT("pB2", [128, 512], F32)
        pT = psT("pT", [128, 1024], BF16)

        SYL = [Sem(nc, "qS%d" % i) for i in range(4)]
        POL = [Sem(nc, "qP%d" % i) for i in range(3)]
        sPE = Sem(nc, "sPE"); sSC = Sem(nc, "sSC"); sVE = Sem(nc, "sVE"); sPO = Sem(nc, "sPO")
        allsems = SYL + POL + [sPE, sSC, sVE, sPO]
        for s in allsems:
            s.h = es.enter_context(s.cm)
        ESEM = {"pe": sPE, "sc": sSC, "ve": sVE, "po": sPO, "sy": sPO}
        LANES = {"sy": SYL, "po": POL}
        rrq = {"sy": 0, "po": 0}

        def dma(q, outap, inap, reads, writes):
            lanes = LANES[q]
            lane = lanes[rrq[q] % len(lanes)]
            rrq[q] += 1
            gate = "lane_" + lane.name
            P.op(q, lane, list(reads) + [gate], list(writes) + [gate],
                 lambda eng, o=outap, i=inap: eng.dma_start(out=o, in_=i), inc_amt=16)

        def op(ename, reads, writes, emit):
            P.op(ename, ESEM[ename], reads, writes, emit)

        def dump(name, ap, reads):
            if not dbg:
                return
            shape = list(ap.shape)
            t = nc.dram_tensor("dbg_" + name, shape, F32, kind="ExternalOutput").ap()
            dbg_t[name] = t
            dma("po", t, ap, reads, ["dbg_" + name])

        dma("sy", consts[:], cvec, [], ["consts"])
        dma("sy", i128f_s[:], i128f, [], ["i128f"])
        dma("po", i128b_s[:], i128b, [], ["i128b"])
        dma("po", triu_s[:], triuc, [], ["triu"])
        dma("po", ones_b[:], onesr, [], ["ones_b"])
        dma("po", onesq[0:1, :], onesr, [], ["onesq0"])
        dma("po", onesq[32:33, :], onesr, [], ["onesq1"])
        dma("po", onesq[64:65, :], onesr, [], ["onesq2"])
        dma("sy", ones_f[:], onesr, [], ["ones_f"])
        dma("sy", rank_s[:], rankin, [], ["rank"])

        MAGICS = [1111, 2222, 3333, 4444]
        dynsems = []
        regc = [0]

        def rname(p):
            regc[0] += 1
            return "%s%d" % (p, regc[0])

        def dsem(name):
            s = Sem(nc, name)
            s.h = es.enter_context(s.cm)
            dynsems.append(s)
            return s
        pollS = [dsem("plS0"), dsem("plS1")]
        pollH = dsem("plH")
        pollM = dsem("plM")

        def polane():
            lanes = LANES["po"]
            lane = lanes[rrq["po"] % len(lanes)]
            rrq["po"] += 1
            return lane, lane.count, "lane_" + lane.name

        def zflags():
            P.op("po", ESEM["po"], [], ["zf"], lambda g: g.memset(zf[:], 0))
            lane, base, gate = polane()

            def emit(g, lane=lane, base=base):
                with nc.gpsimd.register(rname("r_p")) as r_p, \
                     nc.gpsimd.register(rname("r_o")) as r_o:
                    g.reg_load(r_p, rank_s[:1, 9:10])
                    with g.If_eq(r_p, 0):
                        g.dma_start(out=shF_d[0:3, :], in_=zf[0:3, :]).then_inc(lane.h, 16)
                    with g.Else():
                        g.dma_start(out=shF_d[4:7, :], in_=zf[0:3, :]).then_inc(lane.h, 16)
                    g.end_ifs()
                    g.reg_load(r_o, rank_s[:1, 10:11])
                    with g.If_eq(r_o, 1):
                        g.dma_start(out=shF_d[3:4, :], in_=zf[0:1, :]).then_inc(lane.h, 16)
                    with g.Else():
                        g.dma_start(out=shF_d[7:8, :], in_=zf[0:1, :]).then_inc(lane.h, 16)
                    g.end_ifs()
                g.wait_ge(lane.h, base + 32)
                return None
            P.op("po", lane, ["rank", "zf", gate], ["shFz", gate], emit,
                 inc_amt=32, inc_attached=True)

        def pub2(even_out, odd_out, payload_in, row_even, row_odd, mcol,
                 reads, writes):
            lane, base, gate = polane()

            def emit(g, lane=lane, base=base):
                with nc.gpsimd.register(rname("r_p")) as r_p:
                    g.reg_load(r_p, rank_s[:1, 9:10])
                    for par in (0, 1):
                        ifg = g.If_eq(r_p, 0) if par == 0 else g.Else()
                        with ifg:
                            po = even_out if par == 0 else odd_out
                            fr = row_even if par == 0 else row_odd
                            g.dma_start(out=po, in_=payload_in).then_inc(lane.h, 16)
                            g.wait_ge(lane.h, base + 16)
                            g.dma_start(out=shF_d[fr:fr + 1, 0:1],
                                        in_=rank_s[:1, mcol:mcol + 1]).then_inc(lane.h, 16)
                    g.end_ifs()
                return None
            P.op("po", lane, list(reads) + ["rank", "shFz", gate],
                 list(writes) + [gate], emit, inc_amt=32, inc_attached=True)

        def init_pair(dst, shared_src, zero_src, row, magic, psem, reads, writes):
            lane, base, gate = polane()

            def emit(g, lane=lane, psem=psem, magic=magic):
                with nc.gpsimd.register(rname("r_p")) as r_p, \
                     nc.gpsimd.register(rname("r_c")) as r_c, \
                     nc.gpsimd.register(rname("r_v")) as r_v, \
                     nc.gpsimd.register(rname("r_w")) as r_w:
                    g.reg_load(r_p, rank_s[:1, 9:10])
                    with g.If_eq(r_p, 0):
                        g.dma_start(out=dst, in_=zero_src).then_inc(lane.h, 16)
                    with g.Else():
                        g.reg_mov(r_c, 1)
                        g.reg_mov(r_w, 0)
                        with g.While(r_c):
                            g.reg_add(r_w, r_w, 16)
                            g.dma_start(out=pollb[:1, 0:1],
                                        in_=shF_d[row:row + 1, 0:1]).then_inc(psem.h, 16)
                            g.wait_ge(psem.h, r_w)
                            g.reg_load(r_v, pollb[:1, 0:1])
                            g.reg_alu(r_c, r_v, magic, OP.bitwise_xor)
                        g.dma_start(out=dst, in_=shared_src).then_inc(lane.h, 16)
                    g.end_ifs()
                return None
            P.op("po", lane, list(reads) + ["rank", gate],
                 list(writes) + [gate], emit, inc_amt=16, inc_attached=True)

        zflags()

        with ExitStack() as ms:
            msb = lambda name, shape, dt: ms.enter_context(nc.sbuf_tensor(name, shape, dt))
            WB = msb("WBs", [128, 8, D_IN_PROJ], BF16)
            OWT = msb("OWTs", [128, 16, D_MODEL], BF16)
            cw_s = msb("cw_s", [128, 17, 4], F32)
            cb_s = msb("cb_s", [128, 17], F32)
            nw_s = msb("nw_s", [128, 8], F32)
            gw_s = msb("gw_s", [128, 16], F32)
            hd_s = msb("hd_s", [N_HEADS, 8], F32)
            Dbt = msb("Dbt", [128, N_HEADS], F32)
            drow = msb("drow", [1, N_HEADS], F32)
            rbufh = msb("rbufh", [128, D_MODEL], F32)
            rbuf2 = msb("rbuf2", [128, D_MODEL], F32)
            hF = msb("hF", [128, 8, SEG], BF16)
            xbc = msb("xbc", [128, 3 + SEG], BF16)
            xhal = msb("xhal", [128, 17, 3], BF16)
            hHal = msb("hHal", [128, 8, 3], BF16)
            xacc = msb("xacc", [128, SEG], BF16)
            xsil = msb("xsil", [128, SEG], BF16)
            Xr = msb("Xr", [128, NCH, D_INNER], BF16)
            BFt = msb("BFt", [64, SEG], BF16)
            CFt = msb("CFt", [64, SEG], BF16)
            BTt = msb("BTt", [128, NCH, D_STATE], BF16)
            dtF = msb("dtF", [N_HEADS, SEG], F32)
            dtmpF = msb("dtmpF", [N_HEADS, SEG], F32)
            dAcsF = msb("dAcsF", [N_HEADS, SEG], F32)
            decF = msb("decF", [N_HEADS, SEG], F32)
            dAcsT = msb("dAcsT", [128, NCH, N_HEADS], F32)
            ldT = msb("ldT", [128, NCH, N_HEADS], F32)
            lmdF = msb("lmdF", [N_HEADS, SEG], F32)
            his = msb("his", [N_HEADS, SEG], BF16)
            decT = msb("decT", [128, NCH, N_HEADS], F32)
            expT = msb("expT", [128, N_HEADS], F32)
            dzrow = msb("dzrow", [1, NCH, N_HEADS], F32)
            dzrx = msb("dzrx", [1, 512], BF16)
            rowq = msb("rowq", [65, 2048], BF16)
            dzB = msb("dzB", [64, D_INNER], BF16)
            S_run = msb("S_run", [64, D_INNER], F32)
            prevb = msb("prevb", [64, D_INNER], BF16)
            stz = msb("stz", [64, D_INNER], BF16)
            xds = msb("xds", [128, D_INNER], BF16)
            Mt = msb("Mt", [128, 8 * CHUNK], BF16)
            Mt2 = msb("Mt2", [128, 8 * CHUNK], BF16)
            Gt2 = msb("Gt2", [128, CHUNK], BF16)
            yosb = msb("yosb", [128, 1024], F32)
            Gt = msb("Gt", [128, CHUNK], BF16)
            Ybf = msb("Ybf", [128, D_INNER], BF16)
            zs = msb("zs", [128, D_INNER], BF16)
            yg = msb("yg", [128, D_INNER], BF16)
            yn = yg
            yF = msb("yF", [128, 16, 128], BF16)

            hF3 = hF[:]

            def mamba_layer(L, rin, rout):
                wl = w[L]
                dma("po", WB[:], wl["inwT"].rearrange("(dt p) e -> p dt e", p=128), [], ["WB"])
                dma("po", OWT[:], wl["owT"].rearrange("(it p) d -> p it d", p=128), [], ["OWT"])
                dma("sy", cw_s[:], wl["convw"].rearrange("(et p) k -> p et k", p=128), [], ["convw"])
                dma("sy", cb_s[:], wl["convb"], [], ["convb"])
                dma("sy", nw_s[:], wl["nw"], [], ["nw"])
                dma("sy", gw_s[:], wl["gw"], [], ["gw"])
                dma("sy", hd_s[:, 0:1], wl["dtb"].unsqueeze(1), ["dtU"], ["hd0"])
                dma("sy", hd_s[:, 1:2], wl["alog"].unsqueeze(1), ["hdA"], ["hd1"])
                dma("sy", hd_s[:, 3:4], wl["dsk"].unsqueeze(1), ["Dbt"], ["hd3"])

                def fold_wb(v):
                    for dt in range(8):
                        ins = v.tensor_scalar(WB[:, dt, :], WB[:, dt, :], nw_s[:, dt:dt + 1],
                                              0.0, op0=OP.mult, op1=OP.bypass)
                    return ins
                op("ve", ["WB", "nw"], ["WB"], fold_wb)

                def fold_ow(v):
                    for it in range(16):
                        ins = v.tensor_scalar(OWT[:, it, :], OWT[:, it, :], gw_s[:, it:it + 1],
                                              0.0, op0=OP.mult, op1=OP.bypass)
                    return ins
                op("ve", ["OWT", "gw"], ["OWT"], fold_ow)

                op("sc", ["hd1"], ["hdA"],
                   lambda s: s.activation(hd_s[:, 2:3], hd_s[:, 1:2], AF.Exp))
                op("ve", ["hdA"], ["hdA"],
                   lambda v: v.tensor_scalar(hd_s[:, 2:3], hd_s[:, 2:3], -1.0, 0.0,
                                             op0=OP.mult, op1=OP.bypass))
                op("pe", ["hd3", "i128f"], ["pB6"],
                   lambda t: t.transpose(pB2[0:1, 0:N_HEADS], hd_s[:, 3:4],
                                         i128f_s[0:N_HEADS, 0:N_HEADS]))
                op("ve", ["pB6"], ["drow", "pB6"],
                   lambda v: v.tensor_copy(drow[:], pB2[0:1, 0:N_HEADS]))
                op("pe", ["drow", "ones_f"], ["pB6"],
                   lambda t: t.matmul(pB2[:, 0:N_HEADS], ones_f[:, 0:128], drow[:],
                                      start=True, stop=True))
                op("ve", ["pB6"], ["Dbt", "pB6"],
                   lambda v: v.tensor_copy(Dbt[:], pB2[:, 0:N_HEADS]))

                def zinit(g):
                    g.memset(xhal[:], 0.0)
                    return g.memset(S_run[:], 0.0)
                op("po", ["xhal", "S_run"], ["xhal", "S_run"], zinit)
                if L == 0:
                    dma("sy", zz_d[:, :], S_run[:], ["S_run"], ["zzd"])

                hF_deps = ["hF%d" % c for c in range(NCH)]
                xr_deps = ["Xr%d" % et for et in range(16)]

                if L == 0:
                    dma("sy", rbufh[0:3, :], hx, ["rbufh_u"], ["rbufh"])
                else:
                    init_pair(rbufh[0:3, :], shH_d[:, :], zz_d[0:3, 0:D_MODEL],
                              2, MAGICS[2], pollH,
                              ["hpub", "zzd", "rbufh_u"], ["rbufh"])
                op("sc", ["rbufh"], ["ssq", "hT"],
                   lambda s: s.activation(hT[0:3, :], rbufh[0:3, :], AF.Square,
                                          accum_out=sc1[0:3, 0:1]))
                op("ve", ["ssq"], ["ms"],
                   lambda v: v.tensor_scalar(sc1[0:3, 1:2], sc1[0:3, 0:1], 1.0 / D_MODEL,
                                             EPS, op0=OP.mult, op1=OP.add))
                op("sc", ["ms"], ["sqv"],
                   lambda s: s.activation(sc1[0:3, 2:3], sc1[0:3, 1:2], AF.Ln))
                op("ve", ["sqv"], ["sqv"],
                   lambda v: v.tensor_scalar(sc1[0:3, 2:3], sc1[0:3, 2:3], -0.5, 0.0,
                                             op0=OP.mult, op1=OP.bypass))
                op("sc", ["sqv"], ["rstd"],
                   lambda s: s.activation(sc1[0:3, 3:4], sc1[0:3, 2:3], AF.Exp))
                op("ve", ["rstd", "rbufh"], ["hT", "rbufh_u"],
                   lambda v: v.tensor_scalar(hT[0:3, :], rbufh[0:3, :], sc1[0:3, 3:4],
                                             0.0, op0=OP.mult, op1=OP.bypass))

                def htrh(t):
                    for dt in range(8):
                        ins = t.transpose(pT[:, dt * 4:dt * 4 + 3],
                                          hT[0:3, dt * 128:(dt + 1) * 128],
                                          i128b_s[0:3, 0:3])
                    return ins
                op("pe", ["hT", "i128b"], ["pT"], htrh)
                def hcp(sx):
                    for dt in range(8):
                        ins = sx.copy(hHal[:, dt, :], pT[:, dt * 4:dt * 4 + 3])
                    return ins
                op("sc", ["pT"], ["hHal"], hcp)
                for et in range(17):
                    preg = "pB6" if et % 2 == 0 else "pB5"
                    pap = pB2[:, 0:3] if et % 2 == 0 else pB0[:, 512:515]

                    def hmm_h(t, et=et, pap=pap):
                        e0 = E_XBC0 + et * 128
                        for dt in range(8):
                            ins = t.matmul(pap, WB[:, dt, e0:e0 + 128],
                                           hHal[:, dt, :],
                                           start=(dt == 0), stop=(dt == 7))
                        return ins
                    op("pe", ["WB", "hHal"], [preg], hmm_h)
                    op("ve", [preg], ["xhal", preg],
                       lambda v, et=et, pap=pap: v.tensor_copy(xhal[:, et, :], pap))

                def seg_A(sg):
                    t0 = sg * SEG
                    for c in range(NCH):
                        tok0 = t0 + c * CHUNK
                        p = c % 2
                        rb = rbufh if p == 0 else rbuf2
                        ht = hT[:] if p == 0 else Ybf[:, 1024:2048]
                        scr = hT[:] if p == 0 else Ybf[:, 0:1024]
                        cb = 4 * p
                        rN, rU, nS, nM, nV, nR, nH = (
                            ("rbufh", "rbufh_u", "ssq", "ms", "sqv", "rstd", "hT")
                            if p == 0 else
                            ("rbuf2", "rbuf2_u", "ssqy", "msy", "sqy", "rsy", "Ybf1"))
                        nSw = [nS, nH] if p == 0 else [nS, "Ybf0"]
                        dma("sy", rb[:], rin[tok0:tok0 + CHUNK, :], [rU], [rN])
                        op("sc", [rN], nSw,
                           lambda s, rb=rb, scr=scr, cb=cb: s.activation(
                               scr, rb[:], AF.Square, accum_out=sc1[:, cb:cb + 1]))
                        op("ve", [nS], [nM],
                           lambda v, cb=cb: v.tensor_scalar(
                               sc1[:, cb + 1:cb + 2], sc1[:, cb:cb + 1], 1.0 / D_MODEL,
                               EPS, op0=OP.mult, op1=OP.add))
                        op("sc", [nM], [nV],
                           lambda s, cb=cb: s.activation(
                               sc1[:, cb + 2:cb + 3], sc1[:, cb + 1:cb + 2], AF.Ln))
                        op("ve", [nV], [nV],
                           lambda v, cb=cb: v.tensor_scalar(
                               sc1[:, cb + 2:cb + 3], sc1[:, cb + 2:cb + 3], -0.5, 0.0,
                               op0=OP.mult, op1=OP.bypass))
                        op("sc", [nV], [nR],
                           lambda s, cb=cb: s.activation(
                               sc1[:, cb + 3:cb + 4], sc1[:, cb + 2:cb + 3], AF.Exp))
                        op("ve", [nR, rN], [nH, rU],
                           lambda v, rb=rb, ht=ht, cb=cb: v.tensor_scalar(
                               ht, rb[:], sc1[:, cb + 3:cb + 4],
                               0.0, op0=OP.mult, op1=OP.bypass))
                        def htr(t, ht=ht):
                            for dt in range(8):
                                ins = t.transpose(pT[:, dt * 128:(dt + 1) * 128],
                                                  ht[:, dt * 128:(dt + 1) * 128], i128b_s[:])
                            return ins
                        op("pe", [nH, "i128b"], ["pT"], htr)
                        op("sc", ["pT"], ["hF%d" % c],
                           lambda sx, c=c: sx.copy(
                               hF3[:, :, c * 128:(c + 1) * 128],
                               pT[:].rearrange("p (a s) -> p a s", a=8)))
                    dma("po", hf_d[:, sg * 8 * SEG:(sg + 1) * 8 * SEG],
                        hF3.rearrange("p a t -> p (a t)"), hF_deps, ["hfd%d" % sg])
                    if L == 0 and sg == 0:
                        dump("hF", hF[:], hF_deps)

                    def dtmm(t):
                        for dt in range(8):
                            ins = t.matmul(pB2[0:N_HEADS, 0:SEG], WB[:, dt, E_DT0:E_DT0 + N_HEADS],
                                           hF3[:, dt, :], start=(dt == 0), stop=(dt == 7))
                        return ins
                    op("pe", ["WB"] + hF_deps, ["pB6"], dtmm)
                    op("ve", ["pB6", "hd0"], ["dtU", "pB6"],
                       lambda v: v.tensor_scalar(dtF[:], pB2[0:N_HEADS, 0:SEG],
                                                 hd_s[0:N_HEADS, 0:1], 0.0,
                                                 op0=OP.add, op1=OP.bypass))
                    op("ve", ["dtU"], ["dtm"],
                       lambda v: v.tensor_scalar(dtmpF[:], dtF[:], -1.0, 0.0,
                                                 op0=OP.mult, op1=OP.bypass))
                    op("ve", ["dtm", "dtU"], ["dtm"],
                       lambda v: v.tensor_tensor(dtmpF[:], dtmpF[:], dtF[:], op=OP.min))
                    op("sc", ["dtm"], ["dtm"],
                       lambda s: s.activation(dtmpF[:], dtmpF[:], AF.Exp))
                    op("sc", ["dtm", "consts"], ["dtm"],
                       lambda s: s.activation(dtmpF[:], dtmpF[:], AF.Ln,
                                              bias=consts[0:N_HEADS, 1:2]))
                    op("ve", ["dtU"], ["dtF"],
                       lambda v: v.tensor_scalar(dtF[:], dtF[:], 0.0, 0.0,
                                                 op0=OP.max, op1=OP.bypass))
                    op("ve", ["dtm", "dtF"], ["dtF"],
                       lambda v: v.tensor_tensor(dtF[:], dtF[:], dtmpF[:], op=OP.add))
                    op("ve", ["dtF", "hdA"], ["dAF"],
                       lambda v: v.tensor_scalar(dtmpF[:], dtF[:], hd_s[0:N_HEADS, 2:3], 0.0,
                                                 op0=OP.mult, op1=OP.bypass))
                    def dscan(v):
                        for c in range(NCH):
                            sl = slice(c * CHUNK, (c + 1) * CHUNK)
                            ins = v.tensor_tensor_scan(dAcsF[:, sl], dtmpF[:, sl], dtmpF[:, sl],
                                                       0.0, op0=OP.add, op1=OP.bypass)
                        return ins
                    op("ve", ["dAF"], ["dAcsF"], dscan)
                    def dec1(v):
                        for c in range(NCH):
                            sl = slice(c * CHUNK, (c + 1) * CHUNK)
                            last = slice(c * CHUNK + CHUNK - 1, c * CHUNK + CHUNK)
                            ins = v.tensor_scalar(decF[:, sl], dAcsF[:, sl], dAcsF[:, last],
                                                  -1.0, op0=OP.subtract, op1=OP.mult)
                        return ins
                    op("ve", ["dAcsF"], ["decF"], dec1)
                    op("sc", ["decF"], ["decF"],
                       lambda s: s.activation(decF[:], decF[:], AF.Exp))
                    op("ve", ["decF", "dtF"], ["decF"],
                       lambda v: v.tensor_tensor(decF[:], decF[:], dtF[:], op=OP.mult))
                    if L == 0 and sg == 0:
                        dump("dtF", dtF[:], ["dtF"])
                        dump("dAcsF", dAcsF[:], ["dAcsF"])
                        dump("decF", decF[:], ["decF"])
                    op("sc", ["dtF"], ["lmdF"],
                       lambda s: s.activation(lmdF[:], dtF[:], AF.Ln))
                    op("ve", ["lmdF", "dAcsF"], ["lmdF"],
                       lambda v: v.tensor_tensor(lmdF[:], lmdF[:], dAcsF[:], op=OP.subtract))
                    op("ve", ["dAcsF"], ["his"],
                       lambda v: v.tensor_copy(his[:], dAcsF[:]))
                    op("ve", ["his", "dAcsF"], ["dtmF2"],
                       lambda v: v.tensor_tensor(dtmpF[:], dAcsF[:], his[:], op=OP.subtract))
                    dma("sy", dach_d[0:N_HEADS, t0:t0 + SEG], his[:], ["his"], ["dach_d"])
                    dma("po", dach_d[N_HEADS:2 * N_HEADS, t0:t0 + SEG], dtmpF[:],
                        ["dtmF2"], ["dach_d2"])
                    def dtr(t):
                        for c in range(NCH):
                            t.transpose(pB0[:, c * 32:(c + 1) * 32],
                                        lmdF[:, c * CHUNK:(c + 1) * CHUNK],
                                        i128f_s[0:N_HEADS, 0:N_HEADS])
                        for c in range(NCH):
                            t.transpose(pB0[:, 128 + c * 32:128 + (c + 1) * 32],
                                        dAcsF[:, c * CHUNK:(c + 1) * CHUNK],
                                        i128f_s[0:N_HEADS, 0:N_HEADS])
                        for c in range(NCH):
                            ins = t.transpose(pB0[:, 256 + c * 32:256 + (c + 1) * 32],
                                              decF[:, c * CHUNK:(c + 1) * CHUNK],
                                              i128f_s[0:N_HEADS, 0:N_HEADS])
                        return ins
                    op("pe", ["lmdF", "dAcsF", "decF", "i128f"], ["pB4"], dtr)
                    def dtcp(sx):
                        sx.copy(ldT[:].rearrange("p c h -> p (c h)"), pB0[:, 0:128])
                        sx.copy(dAcsT[:].rearrange("p c h -> p (c h)"), pB0[:, 128:256])
                        return sx.copy(decT[:].rearrange("p c h -> p (c h)"),
                                       pB0[:, 256:384])
                    op("sc", ["pB4"], ["ldT", "dAcsT", "decT"], dtcp)
                    dma("sy", dzrow[:].rearrange("o c h -> o (c h)"),
                        dAcsT[127:128, :, :].rearrange("o c h -> o (c h)"),
                        ["dAcsT"], ["dzrow"])
                    op("sc", ["dzrow"], ["dzrow"],
                       lambda s: s.activation(dzrow[:].rearrange("o c h -> o (c h)"),
                                              dzrow[:].rearrange("o c h -> o (c h)"), AF.Exp))
                    dma("sy", dtt_d[:, sg * 256:sg * 256 + 128],
                        dAcsT[:].rearrange("p c h -> p (c h)"), ["dAcsT"],
                        ["dttd%d" % sg])
                    dma("sy", dtt_d[:, sg * 256 + 128:sg * 256 + 256],
                        ldT[:].rearrange("p c h -> p (c h)"), ["ldT"],
                        ["dttl%d" % sg])

                    def emit_amm(et):
                        e0 = E_XBC0 + et * 128
                        preg = "pB6" if et % 2 == 0 else "pB5"
                        pap = pB2[:, 0:SEG] if et % 2 == 0 else pB0[:, 512:512 + SEG]
                        def amm(t, e0=e0, pap=pap):
                            for dt in range(8):
                                ins = t.matmul(pap, WB[:, dt, e0:e0 + 128],
                                               hF3[:, dt, :], start=(dt == 0), stop=(dt == 7))
                            return ins
                        op("pe", ["WB"] + hF_deps, [preg], amm)

                    def emit_rest(et):
                        preg = "pB6" if et % 2 == 0 else "pB5"
                        pap = pB2[:, 0:SEG] if et % 2 == 0 else pB0[:, 512:512 + SEG]
                        def xasm(v, et=et, pap=pap):
                            v.tensor_copy(xbc[:, 0:3], xhal[:, et, :])
                            return v.tensor_copy(xbc[:, 3:3 + SEG], pap)
                        op("ve", [preg, "xhal"], ["xbc", preg], xasm)
                        op("ve", ["xbc"], ["xhal"],
                           lambda v, et=et: v.tensor_copy(xhal[:, et, :], xbc[:, SEG:SEG + 3]))
                        op("ve", ["xbc", "convw", "convb"], ["xacc"],
                           lambda v, et=et: v.tensor_scalar(xacc[:], xbc[:, 0:SEG],
                                                            cw_s[:, et, 0:1], cb_s[:, et:et + 1],
                                                            op0=OP.mult, op1=OP.add))
                        for k in (1, 2, 3):
                            op("ve", ["xbc", "convw", "xacc"], ["xacc"],
                               lambda v, et=et, k=k: v.scalar_tensor_tensor(
                                   xacc[:], xbc[:, k:SEG + k], cw_s[:, et, k:k + 1], xacc[:],
                                   op0=OP.mult, op1=OP.add))
                        op("sc", ["xacc"], ["xsil"],
                           lambda s: s.activation(xsil[:], xacc[:], AF.Silu))
                        if et < 16:
                            def xtr(t):
                                for c in range(NCH):
                                    ins = t.transpose(pT[:, c * 128:(c + 1) * 128],
                                                      xsil[:, c * 128:(c + 1) * 128], i128b_s[:])
                                return ins
                            op("pe", ["xsil", "i128b"], ["pT"], xtr)
                            op("sc", ["pT"], ["Xr%d" % et],
                               lambda sx, et=et: sx.copy(
                                   Xr[:, :, et * 128:(et + 1) * 128],
                                   pT[:, 0:NCH * 128].rearrange("p (c f) -> p c f", c=NCH)))
                        else:
                            op("ve", ["xsil"], ["BFt"],
                               lambda v: v.tensor_copy(BFt[:], xsil[0:64, :]))
                            dma("po", CFt[:], xsil[64:128, :], ["xsil"], ["CFt"])
                            def btr(t):
                                for c in range(NCH):
                                    ins = t.transpose(pT[:, c * 64:(c + 1) * 64],
                                                      BFt[:, c * 128:(c + 1) * 128],
                                                      i128b_s[0:64, 0:64])
                                return ins
                            op("pe", ["BFt", "i128b"], ["pT"], btr)
                            op("sc", ["pT"], ["BTt"],
                               lambda sx: sx.copy(
                                   BTt[:].rearrange("p c n -> p (c n)"), pT[:, 0:NCH * 64]))

                    emit_amm(0)
                    for et in range(1, 17):
                        emit_amm(et)
                        emit_rest(et - 1)
                    emit_rest(16)
                    dma("po", xr_d[:, sg * NCH * D_INNER:(sg + 1) * NCH * D_INNER],
                        Xr[:].rearrange("p c f -> p (c f)"), xr_deps, ["xrd%d" % sg])
                    dma("sy", bc_d[:, sg * 2 * SEG:sg * 2 * SEG + SEG], BFt[:],
                        ["BFt"], ["bcb%d" % sg])
                    dma("sy", bc_d[:, sg * 2 * SEG + SEG:(sg + 1) * 2 * SEG], CFt[:],
                        ["CFt"], ["bcc%d" % sg])
                    if L == 0 and sg == 0:
                        dump("Xr", Xr[:], xr_deps)
                        dump("BFt", BFt[:], ["BFt"])
                        dump("CFt", CFt[:], ["CFt"])
                        dump("BTt", BTt[:], ["BTt"])
                        dump("decT", decT[:], ["decT"])
                        dump("dzrow", dzrow[:], ["dzrow"])

                    for c in range(NCH):
                        op("ve", xr_deps + ["decT"], ["xds"],
                           lambda v, c=c: v.tensor_tensor(
                               xds[:].rearrange("p (h q) -> p h q", h=N_HEADS),
                               Xr[:, c, :].rearrange("p (h q) -> p h q", h=N_HEADS),
                               decT[:, c, :].unsqueeze(2).broadcast_to([128, N_HEADS, D_HEAD]),
                               op=OP.mult))
                        def smm(t, c=c):
                            for n in range(4):
                                ins = t.matmul(pAt[n][0:64, :], BTt[:, c, :],
                                               xds[:, n * 512:(n + 1) * 512],
                                               start=True, stop=True)
                            return ins
                        op("pe", ["BTt", "xds"], ["pA0", "pA1", "pA2", "pA3"], smm)
                        def stzev(sx):
                            for n in range(4):
                                ins = sx.copy(stz[:, n * 512:(n + 1) * 512], pAt[n][0:64, :])
                            return ins
                        op("sc", ["pA0", "pA1", "pA2", "pA3"], ["stz"], stzev)
                        dma("po", stz_d[:, (sg * NCH + c) * D_INNER:
                                        (sg * NCH + c + 1) * D_INNER],
                            stz[:], ["stz"], ["stzd_%d_%d" % (sg, c)])
                        for n in range(4):
                            op("ve", ["dzrow"], ["dzrx"],
                               lambda v, c=c, n=n: v.tensor_copy(
                                   dzrx[:].rearrange("o (h q) -> o h q", h=8),
                                   dzrow[:, c, n * 8:(n + 1) * 8].unsqueeze(2)
                                       .broadcast_to([1, 8, D_HEAD])))
                            op("pe", ["dzrx", "ones_b", "stz"], ["pA%d" % n],
                               lambda t, n=n: t.matmul(pAt[n][0:64, :],
                                                       ones_b[:, 0:64], dzrx[:],
                                                       start=True, stop=True))
                        def dzev(sx):
                            for n in range(4):
                                ins = sx.copy(dzB[:, n * 512:(n + 1) * 512],
                                              pAt[n][0:64, :])
                            return ins
                        op("sc", ["pA0", "pA1", "pA2", "pA3", "dzB"], ["dzB"], dzev)
                        dma("po", dzx_d[:, (sg * NCH + c) * D_INNER:
                                        (sg * NCH + c + 1) * D_INNER],
                            dzB[:], ["dzB"], ["dzx_%d_%d" % (sg, c)])
                        op("ve", ["dzB", "S_run"], ["S_run"],
                           lambda v: v.tensor_tensor(S_run[:], S_run[:], dzB[:],
                                                     op=OP.mult))
                        op("ve", ["stz", "S_run"], ["S_run"],
                           lambda v: v.tensor_tensor(S_run[:], S_run[:], stz[:], op=OP.add))
                        if dbg and L == 0 and sg == 0 and c == 0:
                            dump("xds0", xds[:], ["xds"])
                            dump("stz0", stz[:], ["stz"])
                        if dbg and L == 0 and sg == 0 and c == 1:
                            dump("Srun1", S_run[:], ["S_run"])

                def seg_B(sg):
                    t0 = sg * SEG
                    dma("sy", hF3.rearrange("p a t -> p (a t)"),
                        hf_d[:, sg * 8 * SEG:(sg + 1) * 8 * SEG],
                        ["hfd%d" % sg], hF_deps)
                    dma("po", Xr[:].rearrange("p c f -> p (c f)"),
                        xr_d[:, sg * NCH * D_INNER:(sg + 1) * NCH * D_INNER],
                        ["xrd%d" % sg], xr_deps)
                    dma("sy", BFt[:], bc_d[:, sg * 2 * SEG:sg * 2 * SEG + SEG],
                        ["bcb%d" % sg], ["BFt"])
                    dma("sy", CFt[:], bc_d[:, sg * 2 * SEG + SEG:(sg + 1) * 2 * SEG],
                        ["bcc%d" % sg], ["CFt"])
                    dma("po", dAcsT[:].rearrange("p c h -> p (c h)"),
                        dtt_d[:, sg * 256:sg * 256 + 128], ["dttd%d" % sg], ["dAcsT"])
                    dma("po", ldT[:].rearrange("p c h -> p (c h)"),
                        dtt_d[:, sg * 256 + 128:sg * 256 + 256], ["dttl%d" % sg], ["ldT"])

                    def emit_proB(c):
                        gts = Gt if c % 2 == 0 else Gt2
                        op("pe", ["BFt", "CFt"], ["pB6"],
                           lambda t, c=c: t.matmul(pB2[:, 0:128],
                                                   BFt[:, c * 128:(c + 1) * 128],
                                                   CFt[:, c * 128:(c + 1) * 128],
                                                   start=True, stop=True))
                        op("ve", ["pB6", "triu"], ["Gt%d" % (c % 2), "pB6"],
                           lambda v, gts=gts: v.tensor_tensor(gts[:], pB2[:, 0:128],
                                                              triu_s[:], op=OP.mult))
                        op("ve", ["S_run"], ["prevb"],
                           lambda v: v.tensor_copy(prevb[:], S_run[:]))
                        dma("sy", stz[:],
                            stz_d[:, (sg * NCH + c) * D_INNER:
                                  (sg * NCH + c + 1) * D_INNER],
                            ["stzd_%d_%d" % (sg, c)], ["stz"])
                        dma("po", dzB[:],
                            dzx_d[:, (sg * NCH + c) * D_INNER:
                                  (sg * NCH + c + 1) * D_INNER],
                            ["dzx_%d_%d" % (sg, c)], ["dzB"])
                        op("ve", ["dzB", "prevb", "S_run"], ["S_run"],
                           lambda v: v.tensor_tensor(S_run[:], S_run[:], dzB[:],
                                                     op=OP.mult))
                        op("ve", ["stz", "S_run"], ["S_run"],
                           lambda v: v.tensor_tensor(S_run[:], S_run[:], stz[:], op=OP.add))
                        if dbg and L == 0 and sg == 0 and c == 1:
                            dump("prev1", prevb[:], ["prevb"])

                    def emit_mb(c):
                        tok0 = t0 + c * CHUNK
                        gts = Gt if c % 2 == 0 else Gt2
                        op("sc", ["dAcsT"], ["expT"],
                           lambda s, c=c: s.activation(expT[:], dAcsT[:, c, :], AF.Exp))
                        def yd_emit(q, h8):
                            mts = Mt if q % 2 == 0 else Mt2
                            def ydmm(t, h8=h8, q=q, c=c, mts=mts):
                                for hh in range(8):
                                    h = h8 + hh
                                    ins = t.matmul(pAt[q][:, hh * 64:(hh + 1) * 64],
                                                   mts[:, hh * 128:(hh + 1) * 128],
                                                   Xr[:, c, h * 64:(h + 1) * 64],
                                                   start=(hh == 0), stop=False)
                                return ins
                            op("pe", ["Mtq%d" % (q % 2), "S_run"] + xr_deps,
                               ["pA%d" % q, "Mtq%d" % (q % 2)], ydmm)
                        def rql(qq4):
                            h84 = qq4 * 8
                            bq = qq4 % 3
                            dma("sy", rowq[32 * bq:32 * bq + 1, :]
                                .rearrange("o (g h l) -> o g h l", g=2, h=8),
                                dach_d[:, tok0:tok0 + CHUNK]
                                .rearrange("(g h) l -> g h l", g=2)[:, h84:h84 + 8, :]
                                .unsqueeze(0),
                                ["dach_d", "dach_d2", "rowf_u%d" % bq],
                                ["rowf%d" % bq])
                        for qq4 in range(3):
                            rql(qq4)
                        pend = []
                        for half in range(2):
                            for q in range(2):
                                h8 = half * 16 + q * 8
                                qq = half * 2 + q
                                if qq % 2 == 0:
                                    pregs = ["pB4", "pB5"]
                                    bcaps = [pB0[:, 0:512], pB0[:, 512:1024]]
                                else:
                                    pregs = ["pA2", "pA3"]
                                    bcaps = [pAt[2][:, :], pAt[3][:, :]]
                                mts = Mt if qq % 2 == 0 else Mt2
                                bq3 = 32 * (qq % 3)
                                rqa = rowq[bq3:bq3 + 1, :]
                                oq = onesq[bq3:bq3 + 1, :]
                                def bcmm(t, bcaps=bcaps, rqa=rqa, oq=oq):
                                    for n in range(2):
                                        t.matmul(bcaps[n], oq,
                                                 rqa[:, n * 512:(n + 1) * 512],
                                                 start=True, stop=False)
                                        ins = t.matmul(bcaps[n], oq,
                                                       rqa[:, 1024 + n * 512:1024 + (n + 1) * 512],
                                                       start=False, stop=True)
                                    return ins
                                op("pe", ["rowf%d" % (qq % 3), "onesq%d" % (qq % 3)],
                                   pregs, bcmm)
                                def madd(v, c=c, h8=h8, bcaps=bcaps):
                                    for n in range(2):
                                        ins = v.tensor_tensor(
                                            bcaps[n].rearrange("p (h l) -> p h l", h=4),
                                            bcaps[n].rearrange("p (h l) -> p h l", h=4),
                                            ldT[:, c, h8 + n * 4:h8 + (n + 1) * 4]
                                                .unsqueeze(2).broadcast_to([128, 4, CHUNK]),
                                            op=OP.add)
                                    return ins
                                op("ve", pregs + ["ldT"],
                                   pregs + ["rowf_u%d" % (qq % 3)], madd)
                                if qq == 0:
                                    rql(3)
                                def mmin(v, bcaps=bcaps):
                                    for n in range(2):
                                        ins = v.tensor_scalar(bcaps[n], bcaps[n], 80.0, 0.0,
                                                              op0=OP.min, op1=OP.bypass)
                                    return ins
                                op("ve", pregs, pregs, mmin)
                                def mexp(sx, bcaps=bcaps):
                                    for n in range(2):
                                        ins = sx.activation(bcaps[n], bcaps[n], AF.Exp)
                                    return ins
                                op("sc", pregs, pregs, mexp)
                                def gmult(v, bcaps=bcaps, mts=mts, gts=gts):
                                    for n in range(2):
                                        ins = v.tensor_tensor(
                                            mts[:, n * 512:(n + 1) * 512]
                                                .rearrange("p (h l) -> p h l", h=4),
                                            bcaps[n].rearrange("p (h l) -> p h l", h=4),
                                            gts[:].unsqueeze(1).broadcast_to([128, 4, CHUNK]),
                                            op=OP.mult)
                                    return ins
                                op("ve", pregs + ["Gt%d" % (c % 2)], ["Mtq%d" % (qq % 2)], gmult)
                                pend.append((qq, h8))
                                if len(pend) >= 2:
                                    yd_emit(*pend.pop(0))
                        for pq in pend:
                            yd_emit(*pq)
                        op("ve", xr_deps + ["Dbt", "xds"], ["xds"],
                           lambda v, c=c: v.tensor_tensor(
                               xds[:].rearrange("p (h q) -> p h q", h=N_HEADS),
                               Xr[:, c, :].rearrange("p (h q) -> p h q", h=N_HEADS),
                               Dbt[:, :].unsqueeze(2).broadcast_to([128, N_HEADS, D_HEAD]),
                               op=OP.mult))
                        def skmm(t):
                            for n in range(4):
                                ins = t.matmul(pAt[n][:, :], i128b_s[:],
                                               xds[:, n * 512:(n + 1) * 512],
                                               start=False, stop=True)
                            return ins
                        op("pe", ["xds", "i128b"], ["pA0", "pA1", "pA2", "pA3"], skmm)
                        for half in range(2):
                            def yomm(t, c=c, half=half):
                                for n in range(2):
                                    ins = t.matmul(
                                        pB0[:, n * 512:(n + 1) * 512],
                                        CFt[:, c * 128:(c + 1) * 128],
                                        prevb[:, half * 1024 + n * 512:half * 1024 + (n + 1) * 512],
                                        start=True, stop=True)
                                return ins
                            op("pe", ["CFt", "prevb"], ["pB4", "pB5"], yomm)
                            op("ve", ["pB4", "pB5", "expT"], ["yosb"],
                               lambda v, half=half: v.tensor_tensor(
                                   yosb[:].rearrange("p (h q) -> p h q", h=16),
                                   pB0[:, 0:1024].rearrange("p (h q) -> p h q", h=16),
                                   expT[:, half * 16:(half + 1) * 16].unsqueeze(2)
                                       .broadcast_to([128, 16, D_HEAD]),
                                   op=OP.mult))
                            def ybadd(v, half=half):
                                for n in range(2):
                                    bk = 2 * half + n
                                    ins = v.tensor_tensor(
                                        Ybf[:, bk * 512:(bk + 1) * 512],
                                        pAt[2 * half + n][:, :],
                                        yosb[:, n * 512:(n + 1) * 512],
                                        op=OP.add)
                                return ins
                            op("ve", ["yosb", "pA%d" % (2 * half), "pA%d" % (2 * half + 1)],
                               ["Ybf%d" % half, "pA%d" % (2 * half), "pA%d" % (2 * half + 1)],
                               ybadd)
                        if dbg and L == 0 and sg == 0 and c == 0:
                            dump("Ybf0d", Ybf[:], ["Ybf0", "Ybf1"])

                    def emit_epi1(c):
                        tok0 = t0 + c * CHUNK
                        for nt in range(4):
                            zreg = "pB6" if nt % 2 == 0 else "pB4"
                            zap = pB2[:, 0:512] if nt % 2 == 0 else pB0[:, 0:512]
                            def zmm(t, c=c, nt=nt, zap=zap):
                                for dt in range(8):
                                    ins = t.matmul(zap,
                                                   hF3[:, dt, c * 128:(c + 1) * 128],
                                                   WB[:, dt, nt * 512:(nt + 1) * 512],
                                                   start=(dt == 0), stop=(dt == 7))
                                return ins
                            op("pe", ["WB"] + hF_deps + ["zs%d" % nt], [zreg], zmm)
                            op("sc", [zreg], ["zs%d" % nt, zreg],
                               lambda s, nt=nt, zap=zap: s.activation(
                                   zs[:, nt * 512:(nt + 1) * 512], zap, AF.Silu))
                        if dbg and L == 0 and sg == 0 and c == 0:
                            dump("zs0d", zs[:], ["zs0", "zs1", "zs2", "zs3"])
                        op("ve", ["Ybf0", "Ybf1", "zs0", "zs1", "zs2", "zs3"], ["yg"],
                           lambda v: v.tensor_tensor(yg[:], Ybf[:], zs[:], op=OP.mult))
                        op("sc", ["yg"], ["ssqy", "Ybf0", "Ybf1"],
                           lambda s: s.activation(Ybf[:], yg[:], AF.Square,
                                                  accum_out=sc1[:, 4:5]))
                        op("ve", ["ssqy"], ["msy"],
                           lambda v: v.tensor_scalar(sc1[:, 5:6], sc1[:, 4:5], 1.0 / D_INNER,
                                                     EPS, op0=OP.mult, op1=OP.add))
                        op("sc", ["msy"], ["sqy"],
                           lambda s: s.activation(sc1[:, 6:7], sc1[:, 5:6], AF.Ln))
                        op("ve", ["sqy"], ["sqy"],
                           lambda v: v.tensor_scalar(sc1[:, 6:7], sc1[:, 6:7], -0.5, 0.0,
                                                     op0=OP.mult, op1=OP.bypass))
                        op("sc", ["sqy"], ["rsy"],
                           lambda s: s.activation(sc1[:, 7:8], sc1[:, 6:7], AF.Exp))
                        op("ve", ["rsy", "yg"], ["yg"],
                           lambda v: v.tensor_scalar(yn[:], yg[:], sc1[:, 7:8], 0.0,
                                                     op0=OP.mult, op1=OP.bypass))
                        if dbg and L == 0 and sg == 0 and c == 0:
                            dump("yn0", yn[:], ["yg"])
                    def emit_epi2(c):
                        tok0 = t0 + c * CHUNK
                        for g2 in range(2):
                            def ytr(t, g2=g2):
                                for it in range(8):
                                    ins = t.transpose(
                                        pT[:, it * 128:(it + 1) * 128],
                                        yn[:, (g2 * 8 + it) * 128:(g2 * 8 + it + 1) * 128],
                                        i128b_s[:])
                                return ins
                            op("pe", ["yg", "i128b"], ["pT"], ytr)
                            op("ve", ["pT"], ["yF%d" % g2],
                               lambda v, g2=g2: v.tensor_copy(
                                   yF[:, g2 * 8:(g2 + 1) * 8, :].rearrange("p i f -> p (i f)"),
                                   pT[:]))
                        dma("sy", rbuf2[:], rin[tok0:tok0 + CHUNK, :], ["rbuf2_u"], ["rbuf2"])
                        if L == 1:
                            dma("sy", rbufh[:], xin[tok0:tok0 + CHUNK, :], ["rbufh_u"], ["rbufh"])
                        def opmm(t):
                            for dh in range(2):
                                for it in range(16):
                                    ins = t.matmul(pB0[:, dh * 512:(dh + 1) * 512], yF[:, it, :],
                                                   OWT[:, it, dh * 512:(dh + 1) * 512],
                                                   start=(it == 0), stop=(it == 15))
                            return ins
                        op("pe", ["yF0", "yF1", "OWT"], ["pB4", "pB5"], opmm)
                        op("ve", ["pB4", "pB5", "rbuf2"], ["rbuf2", "pB4", "pB5"],
                           lambda v: v.tensor_tensor(rbuf2[:], rbuf2[:], pB0[:, 0:1024],
                                                     op=OP.add))
                        if L == 1:
                            op("ve", ["rbuf2", "rbufh"], ["rbuf2", "rbufh_u"],
                               lambda v: v.tensor_tensor(rbuf2[:], rbuf2[:], rbufh[:],
                                                         op=OP.add))
                        dma("sy", rout[tok0:tok0 + CHUNK, :], rbuf2[:], ["rbuf2"],
                            ["rbuf2_u", "rstore"])

                    emit_proB(0)
                    emit_mb(0)
                    for c in range(1, NCH):
                        emit_epi1(c - 1)
                        emit_proB(c)
                        emit_epi2(c - 1)
                        emit_mb(c)
                    emit_epi1(NCH - 1)
                    emit_epi2(NCH - 1)

                for sg in range(NSEG):
                    seg_A(sg)
                pub2(shS_d[L], dum_d[:, :], S_run[:], L, 4 + L, 1 + L,
                     ["S_run"], ["spub%d" % L])
                init_pair(S_run[:], shS_d[L], zz_d[:, :], L, MAGICS[L],
                          pollS[L], ["spub%d" % L, "zzd"], ["S_run"])
                for sg in range(NSEG):
                    seg_B(sg)
                if L == 0:
                    pub2(shH_d[:, :], dum_d[0:3, 0:D_MODEL], rout[TH - 3:TH, :],
                         2, 6, 3, ["rstore"], ["hpub"])

            mamba_layer(0, xin, r1)
            mamba_layer(1, r1, r2)

            snap = [(s, s.count) for s in allsems]
            for en in ("sy", "po", "pe", "sc", "ve"):
                def bar(eng, snap=snap):
                    for s, v in snap:
                        if v > 0:
                            eng.wait_ge(s.h, v)
                P.raw(en, bar)
                for s, v in snap:
                    P.hw[(en, s.name)] = max(P.hw.get((en, s.name), 0), v)

        RM = sbT("RM", [128, NTT, D_MODEL], F32)
        XF = sbT("XF", [128, 8, moe_tok], BF16)
        W1q = sbT("W1q", [128, 8, 1024], BF16)
        W2q = sbT("W2q", [128, 8, 1024], BF16)
        W1q2 = sbT("W1q2", [128, 8, 1024], BF16)
        W2q2 = sbT("W2q2", [128, 8, 1024], BF16)
        ab = sbT("ab", [128, 8, 512], BF16)
        wg_s = sbT("wg_s", [128, 8, 8], BF16)
        bgb = sbT("bgb", [128, 8], F32)
        bgr = sbT("bgr", [1, 8], F32)
        b1q = sbT("b1q", [128, 8], F32)
        b2b = sbT("b2b", [8, D_MODEL], BF16)
        logt = sbT("logt", [128, NTT, 8], F32)
        comb = sbT("comb", [128, NTT, 8], F32)
        combB = sbT("combB", [128, NTT, 8], BF16)
        combF = sbT("combF", [8, NTT * 128], BF16)

        dma("po", wg_s[:], Wg.rearrange("(dt p) e -> p dt e", p=128), [], ["wg"])
        dma("po", b2b[:], b2, [], ["b2b"])
        dma("sy", bgr[:], bg, [], ["bgr"])

        rlane = POL[0]
        rbase = rlane.count

        def rload(g):
            with nc.gpsimd.register(rname("r_rk")) as r_rk, \
                 nc.gpsimd.register(rname("r_c")) as r_c, \
                 nc.gpsimd.register(rname("r_v")) as r_v, \
                 nc.gpsimd.register(rname("r_w")) as r_w:
                g.reg_load(r_rk, rank_s[:1, :1])
                for rk in range(min(n_ranks, T // moe_tok)):
                    par = rk & 1
                    P0 = (rk * moe_tok) // TH
                    own = (par == P0)
                    partner = rk ^ 1
                    myloc = rk * moe_tok - P0 * TH
                    with g.If_eq(r_rk, rk):
                        if own:
                            pnon = ((partner & 1) != ((partner * moe_tok) // TH))
                            if pnon:
                                ploc = partner * moe_tok - P0 * TH
                                g.dma_start(out=shR_d[0:moe_tok, :],
                                            in_=r2[ploc:ploc + moe_tok, :]
                                            ).then_inc(rlane.h, 16)
                                g.wait_ge(rlane.h, rbase + 16)
                                g.dma_start(out=shF_d[3:4, 0:1],
                                            in_=rank_s[:1, 4:5]).then_inc(rlane.h, 16)
                            else:
                                g.dma_start(out=shF_d[7:8, 0:1],
                                            in_=rank_s[:1, 4:5]).then_inc(rlane.h, 16)
                                g.wait_ge(rlane.h, rbase + 16)
                                g.dma_start(out=shF_d[7:8, 1:2],
                                            in_=rank_s[:1, 4:5]).then_inc(rlane.h, 16)
                            g.dma_start(out=RM[:],
                                        in_=r2[myloc:myloc + moe_tok, :]
                                        .rearrange("(c p) d -> p c d", p=128)
                                        ).then_inc(rlane.h, 16)
                        else:
                            g.dma_start(out=shF_d[7:8, 2:3],
                                        in_=rank_s[:1, 4:5]).then_inc(rlane.h, 16)
                            g.wait_ge(rlane.h, rbase + 16)
                            g.dma_start(out=shF_d[7:8, 3:4],
                                        in_=rank_s[:1, 4:5]).then_inc(rlane.h, 16)
                            g.reg_mov(r_c, 1)
                            g.reg_mov(r_w, 0)
                            with g.While(r_c):
                                g.reg_add(r_w, r_w, 16)
                                g.dma_start(out=pollb[:1, 0:1],
                                            in_=shF_d[3:4, 0:1]).then_inc(pollM.h, 16)
                                g.wait_ge(pollM.h, r_w)
                                g.reg_load(r_v, pollb[:1, 0:1])
                                g.reg_alu(r_c, r_v, MAGICS[3], OP.bitwise_xor)
                            g.dma_start(out=RM[:],
                                        in_=shR_d[0:moe_tok, :]
                                        .rearrange("(c p) d -> p c d", p=128)
                                        ).then_inc(rlane.h, 16)
                    g.end_ifs()
            return None
        P.op("po", rlane, ["rank", "rstore", "shFz", "lane_" + rlane.name],
             ["RMall", "lane_" + rlane.name], rload, inc_amt=48, inc_attached=True)

        for tt in range(NTT):
            op("ve", ["RMall", "RM%d" % tt], ["lnm"],
               lambda v, tt=tt: v.reduce_sum(sc1[:, 0:1], RM[:, tt, :], axis=AX.X))
            op("ve", ["lnm"], ["lnm2"],
               lambda v: v.tensor_scalar(sc1[:, 1:2], sc1[:, 0:1], 1.0 / D_MODEL, 0.0,
                                         op0=OP.mult, op1=OP.bypass))
            op("ve", ["lnm2", "RMall"], ["RM%d" % tt],
               lambda v, tt=tt: v.tensor_scalar(RM[:, tt, :], RM[:, tt, :], sc1[:, 1:2],
                                                0.0, op0=OP.subtract, op1=OP.bypass))
            op("sc", ["RM%d" % tt], ["lnv", "hT"],
               lambda s, tt=tt: s.activation(hT[:], RM[:, tt, :], AF.Square,
                                             accum_out=sc1[:, 2:3]))
            op("ve", ["lnv"], ["lnv2"],
               lambda v: v.tensor_scalar(sc1[:, 3:4], sc1[:, 2:3], 1.0 / D_MODEL, EPS,
                                         op0=OP.mult, op1=OP.add))
            op("sc", ["lnv2"], ["lnsq"],
               lambda s: s.activation(sc1[:, 4:5], sc1[:, 3:4], AF.Ln))
            op("ve", ["lnsq"], ["lnsq"],
               lambda v: v.tensor_scalar(sc1[:, 4:5], sc1[:, 4:5], -0.5, 0.0,
                                         op0=OP.mult, op1=OP.bypass))
            op("sc", ["lnsq"], ["lnr"],
               lambda s: s.activation(sc1[:, 5:6], sc1[:, 4:5], AF.Exp))
            op("ve", ["lnr", "RM%d" % tt], ["RM%d" % tt],
               lambda v, tt=tt: v.tensor_scalar(RM[:, tt, :], RM[:, tt, :], sc1[:, 5:6],
                                                0.0, op0=OP.mult, op1=OP.bypass))
            op("ve", ["RM%d" % tt], ["hT"],
               lambda v, tt=tt: v.tensor_scalar(hT[:], RM[:, tt, :], 1.0, 0.0,
                                                op0=OP.mult, op1=OP.bypass))
            def xtr2(t):
                for dt in range(8):
                    ins = t.transpose(pT[:, dt * 128:(dt + 1) * 128],
                                      hT[:, dt * 128:(dt + 1) * 128], i128b_s[:])
                return ins
            op("pe", ["hT", "i128b"], ["pT"], xtr2)
            op("ve", ["pT"], ["XF%d" % tt],
               lambda v, tt=tt: v.tensor_copy(
                   XF[:, :, tt * 128:(tt + 1) * 128],
                   pT[:].rearrange("p (a s) -> p a s", a=8)))
        xf_deps = ["XF%d" % tt for tt in range(NTT)]

        op("pe", ["bgr", "ones_f"], ["pB6"],
           lambda t: t.matmul(pB2[:, 0:8], ones_f[:, 0:128], bgr[:], start=True, stop=True))
        op("ve", ["pB6"], ["bgb", "pB6"], lambda v: v.tensor_copy(bgb[:], pB2[:, 0:8]))
        for tt in range(NTT):
            def lgmm(t, tt=tt):
                for dt in range(8):
                    ins = t.matmul(pB2[:, 0:8], XF[:, dt, tt * 128:(tt + 1) * 128],
                                   wg_s[:, dt, :], start=(dt == 0), stop=(dt == 7))
                return ins
            op("pe", ["XF%d" % tt, "wg"], ["pB6"], lgmm)
            op("ve", ["pB6", "bgb"], ["logt%d" % tt, "pB6"],
               lambda v, tt=tt: v.tensor_tensor(logt[:, tt, :], pB2[:, 0:8], bgb[:], op=OP.add))
            op("ve", ["logt%d" % tt], ["m1"],
               lambda v, tt=tt: v.reduce_max(sc1[:, 0:1], logt[:, tt, :], axis=AX.X))
            op("ve", ["m1", "logt%d" % tt], ["cm%d" % tt],
               lambda v, tt=tt: v.tensor_scalar(comb[:, tt, :], logt[:, tt, :], sc1[:, 0:1],
                                                0.0, op0=OP.is_ge, op1=OP.bypass))
            op("ve", ["cm%d" % tt, "logt%d" % tt], ["cm%d" % tt],
               lambda v, tt=tt: v.scalar_tensor_tensor(comb[:, tt, :], comb[:, tt, :], -1e30,
                                                       logt[:, tt, :], op0=OP.mult, op1=OP.add))
            op("ve", ["cm%d" % tt], ["m2"],
               lambda v, tt=tt: v.reduce_max(sc1[:, 1:2], comb[:, tt, :], axis=AX.X))
            op("ve", ["m2", "logt%d" % tt], ["c0%d" % tt],
               lambda v, tt=tt: v.tensor_scalar(comb[:, tt, :], logt[:, tt, :], sc1[:, 1:2],
                                                0.0, op0=OP.is_ge, op1=OP.bypass))
            op("ve", ["m1"], ["nm1"],
               lambda v: v.tensor_scalar(sc1[:, 2:3], sc1[:, 0:1], -1.0, 0.0,
                                         op0=OP.mult, op1=OP.bypass))
            op("ve", ["m2", "nm1"], ["dm"],
               lambda v: v.tensor_tensor(sc1[:, 3:4], sc1[:, 1:2], sc1[:, 2:3], op=OP.add))
            op("sc", ["nm1", "logt%d" % tt], ["el%d" % tt],
               lambda s, tt=tt: s.activation(logt[:, tt, :], logt[:, tt, :], AF.Exp,
                                             bias=sc1[:, 2:3]))
            op("sc", ["dm"], ["ed"],
               lambda s: s.activation(sc1[:, 4:5], sc1[:, 3:4], AF.Exp))
            op("ve", ["ed"], ["zd"],
               lambda v: v.tensor_scalar(sc1[:, 5:6], sc1[:, 4:5], 1.0, 0.0,
                                         op0=OP.add, op1=OP.bypass))
            op("ve", ["zd"], ["rz"], lambda v: v.reciprocal(sc1[:, 6:7], sc1[:, 5:6]))
            op("ve", ["c0%d" % tt, "el%d" % tt], ["comb%d" % tt],
               lambda v, tt=tt: v.tensor_tensor(comb[:, tt, :], comb[:, tt, :],
                                                logt[:, tt, :], op=OP.mult))
            op("ve", ["rz", "comb%d" % tt], ["comb%d" % tt],
               lambda v, tt=tt: v.tensor_scalar(comb[:, tt, :], comb[:, tt, :], sc1[:, 6:7],
                                                0.0, op0=OP.mult, op1=OP.bypass))
            op("ve", ["comb%d" % tt], ["combB%d" % tt],
               lambda v, tt=tt: v.tensor_copy(combB[:, tt, :], comb[:, tt, :]))
        for cg in range(0, NTT, 8):
            ng = min(8, NTT - cg)
            def ctr(t, cg=cg, ng=ng):
                for j in range(ng):
                    ins = t.transpose(pT[0:8, j * 128:(j + 1) * 128],
                                      combB[:, cg + j, :], i128b_s[:])
                return ins
            op("pe", ["i128b"] + ["combB%d" % (cg + j) for j in range(ng)], ["pT"], ctr)
            op("ve", ["pT"], ["combF"],
               lambda v, cg=cg, ng=ng: v.tensor_copy(
                   combF[:, cg * 128:(cg + ng) * 128], pT[0:8, 0:ng * 128]))
        for tt in range(NTT):
            def b2mm(t, tt=tt):
                for dh in range(2):
                    ins = t.matmul(pB0[:, dh * 512:(dh + 1) * 512],
                                   combF[:, tt * 128:(tt + 1) * 128],
                                   b2b[:, dh * 512:(dh + 1) * 512], start=True, stop=True)
                return ins
            op("pe", ["combF", "b2b"], ["pB4", "pB5"], b2mm)
            op("ve", ["pB4", "pB5", "RM%d" % tt], ["RM%d" % tt, "pB4", "pB5"],
               lambda v, tt=tt: v.tensor_tensor(RM[:, tt, :], RM[:, tt, :], pB0[:, 0:1024],
                                                op=OP.add))

        for e in range(8):
            gfun = AF.Gelu if e % 2 == 0 else AF.Silu
            for q in range(4):
                par = (e * 4 + q) % 2
                wa = W1q if par == 0 else W1q2
                wb_ = W2q if par == 0 else W2q2
                wtag, w2tag = "W1q%d" % par, "W2q%d" % par
                dma("po", wa[:],
                    W1[e].rearrange("(dt p) f -> p dt f", p=128)[:, :, q * 1024:(q + 1) * 1024],
                    [], [wtag])
                dma("po", wb_[:],
                    W2[e][q * 1024:(q + 1) * 1024, :].rearrange("(ft p) d -> p ft d", p=128),
                    [], [w2tag])
                dma("sy", b1q[:], b1[e, q], ["b1u"], ["b1q"])
                abn = ["ab%d" % f8 for f8 in range(8)]
                for tt2 in range(moe_tok // 512):
                    for fg in range(2):
                        for ft in range(4):
                            f8 = fg * 4 + ft
                            def hmm(t, tt2=tt2, f8=f8, ft=ft, wa=wa):
                                for dt in range(8):
                                    ins = t.matmul(
                                        pAt[ft][:, :],
                                        wa[:, dt, f8 * 128:(f8 + 1) * 128],
                                        XF[:, dt, tt2 * 512:(tt2 + 1) * 512],
                                        start=(dt == 0), stop=(dt == 7))
                                return ins
                            op("pe", [wtag] + xf_deps, ["pA%d" % ft], hmm)
                            op("sc", ["pA%d" % ft, "b1q"],
                               ["ab%d" % f8, "b1u", "pA%d" % ft],
                               lambda s, f8=f8, ft=ft, gfun=gfun: s.activation(
                                   ab[:, f8, :], pAt[ft][:, :], gfun,
                                   bias=b1q[:, f8:f8 + 1]))
                    for mt in range(4):
                        tt = tt2 * 4 + mt
                        for dh in range(2):
                            def ymm(t, mt=mt, dh=dh, wb_=wb_):
                                for ft in range(8):
                                    ins = t.matmul(pB0[:, dh * 512:(dh + 1) * 512],
                                                   ab[:, ft, mt * 128:(mt + 1) * 128],
                                                   wb_[:, ft, dh * 512:(dh + 1) * 512],
                                                   start=(ft == 0), stop=(ft == 7))
                                return ins
                            op("pe", [w2tag] + abn, ["pB%d" % (4 + dh)], ymm)
                            op("ve", ["pB%d" % (4 + dh), "comb%d" % tt, "RM%d" % tt],
                               ["RM%d" % tt, "pB%d" % (4 + dh)],
                               lambda v, tt=tt, e=e, dh=dh: v.scalar_tensor_tensor(
                                   RM[:, tt, dh * 512:(dh + 1) * 512],
                                   pB0[:, dh * 512:(dh + 1) * 512],
                                   comb[:, tt, e:e + 1],
                                   RM[:, tt, dh * 512:(dh + 1) * 512],
                                   op0=OP.mult, op1=OP.add))

        for tt in range(NTT):
            op("ve", ["RM%d" % tt], ["m3"],
               lambda v, tt=tt: v.reduce_sum(sc1[:, 0:1], RM[:, tt, :], axis=AX.X))
            op("ve", ["m3"], ["m4"],
               lambda v: v.tensor_scalar(sc1[:, 1:2], sc1[:, 0:1], 1.0 / D_MODEL, 0.0,
                                         op0=OP.mult, op1=OP.bypass))
            op("ve", ["m4", "RM%d" % tt], ["RM%d" % tt],
               lambda v, tt=tt: v.tensor_scalar(RM[:, tt, :], RM[:, tt, :], sc1[:, 1:2],
                                                0.0, op0=OP.subtract, op1=OP.bypass))
            op("sc", ["RM%d" % tt], ["v2", "hT"],
               lambda s, tt=tt: s.activation(hT[:], RM[:, tt, :], AF.Square,
                                             accum_out=sc1[:, 2:3]))
            op("ve", ["v2"], ["v3"],
               lambda v: v.tensor_scalar(sc1[:, 3:4], sc1[:, 2:3], 1.0 / D_MODEL, EPS,
                                         op0=OP.mult, op1=OP.add))
            op("sc", ["v3"], ["v4"], lambda s: s.activation(sc1[:, 4:5], sc1[:, 3:4], AF.Ln))
            op("ve", ["v4"], ["v4"],
               lambda v: v.tensor_scalar(sc1[:, 4:5], sc1[:, 4:5], -0.5, 0.0,
                                         op0=OP.mult, op1=OP.bypass))
            op("sc", ["v4"], ["v5"], lambda s: s.activation(sc1[:, 5:6], sc1[:, 4:5], AF.Exp))
            op("ve", ["v5", "RM%d" % tt], ["RM%d" % tt],
               lambda v, tt=tt: v.tensor_scalar(RM[:, tt, :], RM[:, tt, :], sc1[:, 5:6],
                                                0.0, op0=OP.mult, op1=OP.bypass))
            dma("sy", out[tt * 128:(tt + 1) * 128, :], RM[:, tt, :], ["RM%d" % tt],
                ["outs%d" % tt])

        def fin(eng):
            for s in allsems:
                if s.count > 0:
                    eng.wait_ge(s.h, s.count)
        P.raw("sy", fin)

        block = es.enter_context(nc.Block())

        @block.sync
        def _(sy):
            for name, f in P.script:
                if name == "sy":
                    f(sy)

        @block.gpsimd
        def _(g):
            for name, f in P.script:
                if name == "po":
                    f(g)

        @block.tensor
        def _(t):
            for name, f in P.script:
                if name == "pe":
                    f(t)

        @block.scalar
        def _(s):
            for name, f in P.script:
                if name == "sc":
                    f(s)

        @block.vector
        def _(v):
            for name, f in P.script:
                if name == "ve":
                    f(v)

    return nc


def _host_inputs(inputs, T, n_ranks):
    x = np.ascontiguousarray(np.asarray(inputs["L0_attn_x"], np.float32).reshape(-1, D_MODEL))[:T]
    inw = np.asarray(inputs["L0_attn_in_proj_w"], np.float32)
    TH = T // 2
    moe_tok = T // n_ranks
    base = dict(
        Wg=np.ascontiguousarray(np.asarray(inputs["L0_ffn_Wg"], np.float32)),
        bg=np.ascontiguousarray(np.asarray(inputs["L0_ffn_bg"], np.float32).reshape(1, 8)),
        W1=np.ascontiguousarray(np.asarray(inputs["L0_ffn_W1"], np.float32)),
        b1=np.ascontiguousarray(np.asarray(inputs["L0_ffn_b1"], np.float32).reshape(8, 4, 8, 128).transpose(0, 1, 3, 2)),
        W2=np.ascontiguousarray(np.asarray(inputs["L0_ffn_W2"], np.float32)),
        b2=np.ascontiguousarray(np.asarray(inputs["L0_ffn_b2"], np.float32)),
        i128f=np.eye(128, dtype=np.float32),
        i128b=np.eye(128, dtype=np.float32),
        triuc=np.triu(np.ones((128, 128), np.float32)),
        onesr=np.ones((1, 128), np.float32),
        cvec=np.tile(np.array([[EPS, 1.0, 0.0, 0.0]], np.float32), (128, 1)),
    )
    for L in range(2):
        base[f"inwT{L}"] = np.ascontiguousarray(inw[L].T)
        base[f"convw{L}"] = np.ascontiguousarray(np.asarray(inputs["L0_attn_conv_w"], np.float32)[L])
        base[f"convb{L}"] = np.ascontiguousarray(np.asarray(inputs["L0_attn_conv_b"], np.float32)[L].reshape(17, 128).T)
        base[f"dtb{L}"] = np.ascontiguousarray(np.asarray(inputs["L0_attn_dt_bias"], np.float32)[L])
        base[f"alog{L}"] = np.ascontiguousarray(np.asarray(inputs["L0_attn_A_log"], np.float32)[L])
        base[f"dsk{L}"] = np.ascontiguousarray(np.asarray(inputs["L0_attn_D"], np.float32)[L])
        base[f"nw{L}"] = np.ascontiguousarray(np.asarray(inputs["L0_attn_norm_w"], np.float32)[L].reshape(8, 128).T)
        base[f"gw{L}"] = np.ascontiguousarray(np.asarray(inputs["L0_attn_gnorm_w"], np.float32)[L].reshape(16, 128).T)
        base[f"owT{L}"] = np.ascontiguousarray(np.asarray(inputs["L0_attn_out_proj_w"], np.float32)[L].T)
    maps = []
    for rk in range(n_ranks):
        m = dict(base)
        par = rk & 1
        m["xin"] = np.ascontiguousarray(x[par * TH:(par + 1) * TH])
        hxv = np.zeros((3, D_MODEL), np.float32)
        if par == 1:
            hxv[:] = x[TH - 3:TH]
        m["hx"] = hxv
        rv = np.zeros((1, 16), np.int32)
        rv[0, 0] = rk
        rv[0, 1:5] = [1111, 2222, 3333, 4444]
        rv[0, 9] = par
        rv[0, 10] = 1 if par == ((rk * moe_tok) // TH) else 0
        m["rankin"] = rv
        maps.append(m)
    return maps


def kernel(**inputs):
    T = 16384
    n_ranks = 8
    moe_tok = T // n_ranks
    nc = build(T, n_ranks, moe_tok)
    in_maps = _host_inputs(inputs, T, n_ranks)
    res = run_bass_kernel_spmd(nc, in_maps, core_ids=list(range(n_ranks)))
    parts = [res.results[rk]["out"] for rk in range(n_ranks)]
    full = np.concatenate(parts, axis=0).astype(np.float32)
    return full.reshape(1, T, D_MODEL)


# version 12 (final)
# speedup vs baseline: 1.1634x; 1.0321x over previous
import numpy as np
from contextlib import ExitStack

import concourse.bass as bass
import concourse.mybir as mybir
from concourse.bass_utils import run_bass_kernel_spmd
from concourse.alu_op_type import AluOpType as OP

F32 = mybir.dt.float32
BF16 = mybir.dt.bfloat16
I32 = mybir.dt.int32
AF = mybir.ActivationFunctionType
AX = mybir.AxisListType

D_MODEL = 1024
D_INNER = 2048
N_HEADS = 32
D_HEAD = 64
D_STATE = 64
CONV_DIM = D_INNER + 2 * D_STATE
D_IN_PROJ = 2 * D_INNER + 2 * D_STATE + N_HEADS
EPS = 1e-5
CHUNK = 128
SEG = 512
NCH = SEG // CHUNK
E_XBC0 = D_INNER
E_DT0 = D_INNER + CONV_DIM


class Sem:
    __slots__ = ("name", "cm", "h", "count")

    def __init__(self, nc, name):
        self.name = name
        self.cm = nc.semaphore(name)
        self.h = None
        self.count = 0


class Planner:
    def __init__(self):
        self.script = []
        self.hw = {}
        self.res = {}

    def _wait_list(self, ename, reads, writes):
        need = {}
        def add(sem, val):
            if val <= 0:
                return
            if self.hw.get((ename, sem.name), 0) >= val:
                return
            if need.get(sem, 0) < val:
                need[sem] = val
        for n in reads:
            r = self.res.get(n)
            if r and r['w']:
                add(*r['w'])
        for n in writes:
            r = self.res.get(n)
            if not r:
                continue
            if r['w']:
                add(*r['w'])
            for m in r['r']:
                add(*m)
        out = list(need.items())
        for s, v in out:
            self.hw[(ename, s.name)] = v
        return out

    def op(self, ename, sem, reads, writes, emit, inc_amt=1, inc_attached=False):
        waits = self._wait_list(ename, reads, writes)
        sem.count += inc_amt
        val = sem.count
        for n in writes:
            self.res[n] = {'w': (sem, val), 'r': []}
        for n in reads:
            r = self.res.setdefault(n, {'w': None, 'r': []})
            r['r'].append((sem, val))

        def closure(eng, waits=waits, emit=emit, sem=sem, inc_amt=inc_amt,
                    inc_attached=inc_attached):
            for s, v in waits:
                eng.wait_ge(s.h, v)
            ins = emit(eng)
            if not inc_attached:
                if ins is None:
                    ins = eng.nop()
                ins.then_inc(sem.h, inc_amt)
        self.script.append((ename, closure))

    def raw(self, ename, fn):
        self.script.append((ename, fn))


def build(T, n_ranks, moe_tok, dbg=False):
    nc = bass.Bass("TRN2", target_bir_lowering=False)
    NSEG = (T // 2) // SEG
    NTT = moe_tok // 128

    TH = T // 2
    xin = nc.dram_tensor("xin", [TH, D_MODEL], F32, kind="ExternalInput").ap()
    w = {}
    for L in range(2):
        w[L] = dict(
            inwT=nc.dram_tensor(f"inwT{L}", [D_MODEL, D_IN_PROJ], F32, kind="ExternalInput").ap(),
            convw=nc.dram_tensor(f"convw{L}", [CONV_DIM, 4], F32, kind="ExternalInput").ap(),
            convb=nc.dram_tensor(f"convb{L}", [128, 17], F32, kind="ExternalInput").ap(),
            dtb=nc.dram_tensor(f"dtb{L}", [N_HEADS], F32, kind="ExternalInput").ap(),
            alog=nc.dram_tensor(f"alog{L}", [N_HEADS], F32, kind="ExternalInput").ap(),
            dsk=nc.dram_tensor(f"dsk{L}", [N_HEADS], F32, kind="ExternalInput").ap(),
            nw=nc.dram_tensor(f"nw{L}", [128, 8], F32, kind="ExternalInput").ap(),
            gw=nc.dram_tensor(f"gw{L}", [128, 16], F32, kind="ExternalInput").ap(),
            owT=nc.dram_tensor(f"owT{L}", [D_INNER, D_MODEL], F32, kind="ExternalInput").ap(),
        )
    Wg = nc.dram_tensor("Wg", [D_MODEL, 8], F32, kind="ExternalInput").ap()
    bg = nc.dram_tensor("bg", [1, 8], F32, kind="ExternalInput").ap()
    W1 = nc.dram_tensor("W1", [8, D_MODEL, 4096], F32, kind="ExternalInput").ap()
    b1 = nc.dram_tensor("b1", [8, 4, 128, 8], F32, kind="ExternalInput").ap()
    W2 = nc.dram_tensor("W2", [8, 4096, D_MODEL], F32, kind="ExternalInput").ap()
    b2 = nc.dram_tensor("b2", [8, D_MODEL], F32, kind="ExternalInput").ap()
    i128f = nc.dram_tensor("i128f", [128, 128], F32, kind="ExternalInput").ap()
    i128b = nc.dram_tensor("i128b", [128, 128], F32, kind="ExternalInput").ap()
    triuc = nc.dram_tensor("triuc", [128, 128], F32, kind="ExternalInput").ap()
    onesr = nc.dram_tensor("onesr", [1, 128], F32, kind="ExternalInput").ap()
    cvec = nc.dram_tensor("cvec", [128, 4], F32, kind="ExternalInput").ap()
    rankin = nc.dram_tensor("rankin", [1, 16], I32, kind="ExternalInput").ap()

    out = nc.dram_tensor("out", [moe_tok, D_MODEL], F32, kind="ExternalOutput").ap()

    rkind = "ExternalOutput" if dbg else "Internal"
    r1 = nc.dram_tensor("r1d", [T // 2, D_MODEL], F32, kind=rkind).ap()
    r2 = nc.dram_tensor("r2d", [T // 2, D_MODEL], F32, kind=rkind).ap()
    dach_d = nc.dram_tensor("dachd", [2 * N_HEADS, T // 2], BF16, kind="Internal").ap()
    xr_d = nc.dram_tensor("xr_d", [128, NSEG * NCH * D_INNER], BF16, kind="Internal").ap()
    hf_d = nc.dram_tensor("hf_d", [128, NSEG * 8 * SEG], BF16, kind="Internal").ap()
    bc_d = nc.dram_tensor("bc_d", [64, NSEG * 2 * SEG], BF16, kind="Internal").ap()
    dtt_d = nc.dram_tensor("dtt_d", [128, NSEG * 2 * NCH * N_HEADS], F32, kind="Internal").ap()
    dzx_d = nc.dram_tensor("dzx_d", [64, NSEG * NCH * D_INNER], BF16, kind="Internal").ap()
    stz_d = nc.dram_tensor("stz_d", [64, NSEG * NCH * D_INNER], BF16, kind="Internal").ap()
    zz_d = nc.dram_tensor("zz_d", [64, D_INNER], F32, kind="Internal").ap()
    dum_d = nc.dram_tensor("dum_d", [64, D_INNER], F32, kind="Internal").ap()
    shS_d = nc.dram_tensor("shS_d", [2, 64, D_INNER], F32, kind="Internal",
                           addr_space="Shared").ap()
    shH_d = nc.dram_tensor("shH_d", [3, D_MODEL], F32, kind="Internal",
                           addr_space="Shared").ap()
    shR_d = nc.dram_tensor("shR_d", [2048, D_MODEL], F32, kind="Internal",
                           addr_space="Shared").ap()
    shF_d = nc.dram_tensor("shF_d", [8, 16], I32, kind="Internal",
                           addr_space="Shared").ap()
    hx = nc.dram_tensor("hx", [3, D_MODEL], F32, kind="ExternalInput").ap()

    P = Planner()
    dbg_t = {}
    es = ExitStack()
    with es:
        sbT = lambda name, shape, dt: es.enter_context(nc.sbuf_tensor(name, shape, dt))
        psT = lambda name, shape, dt: es.enter_context(nc.psum_tensor(name, shape, dt))

        consts = sbT("consts", [128, 4], F32)
        i128b_s = sbT("i128b_s", [128, 128], BF16)
        i128f_s = sbT("i128f_s", [128, 128], F32)
        triu_s = sbT("triu_s", [128, 128], BF16)
        ones_b = sbT("ones_b", [1, 128], BF16)
        onesq = sbT("onesq", [65, 128], BF16)
        ones_f = sbT("ones_f", [1, 128], F32)
        rank_s = sbT("rank_s", [1, 16], I32)
        pollb = sbT("pollb", [1, 16], I32)
        zf = sbT("zf", [4, 16], I32)
        sc1 = sbT("sc1", [128, 8], F32)
        hT = sbT("hT", [128, D_MODEL], BF16)

        pAt = [psT("pAb%d" % i, [128, 512], F32) for i in range(4)]
        pB0 = psT("pB0", [128, 1024], F32)
        pB2 = psT("pB2", [128, 512], F32)
        pT = psT("pT", [128, 1024], BF16)

        SYL = [Sem(nc, "qS%d" % i) for i in range(4)]
        POL = [Sem(nc, "qP%d" % i) for i in range(3)]
        sPE = Sem(nc, "sPE"); sSC = Sem(nc, "sSC"); sVE = Sem(nc, "sVE"); sPO = Sem(nc, "sPO")
        allsems = SYL + POL + [sPE, sSC, sVE, sPO]
        for s in allsems:
            s.h = es.enter_context(s.cm)
        ESEM = {"pe": sPE, "sc": sSC, "ve": sVE, "po": sPO, "sy": sPO}
        LANES = {"sy": SYL, "po": POL}
        rrq = {"sy": 0, "po": 0}

        def dma(q, outap, inap, reads, writes):
            lanes = LANES[q]
            lane = lanes[rrq[q] % len(lanes)]
            rrq[q] += 1
            gate = "lane_" + lane.name
            P.op(q, lane, list(reads) + [gate], list(writes) + [gate],
                 lambda eng, o=outap, i=inap: eng.dma_start(out=o, in_=i), inc_amt=16)

        def op(ename, reads, writes, emit):
            P.op(ename, ESEM[ename], reads, writes, emit)

        def dump(name, ap, reads):
            if not dbg:
                return
            shape = list(ap.shape)
            t = nc.dram_tensor("dbg_" + name, shape, F32, kind="ExternalOutput").ap()
            dbg_t[name] = t
            dma("po", t, ap, reads, ["dbg_" + name])

        dma("sy", consts[:], cvec, [], ["consts"])
        dma("sy", i128f_s[:], i128f, [], ["i128f"])
        dma("po", i128b_s[:], i128b, [], ["i128b"])
        dma("po", triu_s[:], triuc, [], ["triu"])
        dma("po", ones_b[:], onesr, [], ["ones_b"])
        dma("po", onesq[0:1, :], onesr, [], ["onesq0"])
        dma("po", onesq[32:33, :], onesr, [], ["onesq1"])
        dma("po", onesq[64:65, :], onesr, [], ["onesq2"])
        dma("sy", ones_f[:], onesr, [], ["ones_f"])
        dma("sy", rank_s[:], rankin, [], ["rank"])

        MAGICS = [1111, 2222, 3333, 4444]
        dynsems = []
        regc = [0]

        def rname(p):
            regc[0] += 1
            return "%s%d" % (p, regc[0])

        def dsem(name):
            s = Sem(nc, name)
            s.h = es.enter_context(s.cm)
            dynsems.append(s)
            return s
        pollS = [dsem("plS0"), dsem("plS1")]
        pollH = dsem("plH")
        pollM = dsem("plM")

        def polane():
            lanes = LANES["po"]
            lane = lanes[rrq["po"] % len(lanes)]
            rrq["po"] += 1
            return lane, lane.count, "lane_" + lane.name

        def zflags():
            P.op("po", ESEM["po"], [], ["zf"], lambda g: g.memset(zf[:], 0))
            lane, base, gate = polane()

            def emit(g, lane=lane, base=base):
                with nc.gpsimd.register(rname("r_p")) as r_p, \
                     nc.gpsimd.register(rname("r_o")) as r_o:
                    g.reg_load(r_p, rank_s[:1, 9:10])
                    with g.If_eq(r_p, 0):
                        g.dma_start(out=shF_d[0:3, :], in_=zf[0:3, :]).then_inc(lane.h, 16)
                    with g.Else():
                        g.dma_start(out=shF_d[4:7, :], in_=zf[0:3, :]).then_inc(lane.h, 16)
                    g.end_ifs()
                    g.reg_load(r_o, rank_s[:1, 10:11])
                    with g.If_eq(r_o, 1):
                        g.dma_start(out=shF_d[3:4, :], in_=zf[0:1, :]).then_inc(lane.h, 16)
                    with g.Else():
                        g.dma_start(out=shF_d[7:8, :], in_=zf[0:1, :]).then_inc(lane.h, 16)
                    g.end_ifs()
                g.wait_ge(lane.h, base + 32)
                return None
            P.op("po", lane, ["rank", "zf", gate], ["shFz", gate], emit,
                 inc_amt=32, inc_attached=True)

        def pub2(even_out, odd_out, payload_in, row_even, row_odd, mcol,
                 reads, writes):
            lane, base, gate = polane()

            def emit(g, lane=lane, base=base):
                with nc.gpsimd.register(rname("r_p")) as r_p:
                    g.reg_load(r_p, rank_s[:1, 9:10])
                    for par in (0, 1):
                        ifg = g.If_eq(r_p, 0) if par == 0 else g.Else()
                        with ifg:
                            po = even_out if par == 0 else odd_out
                            fr = row_even if par == 0 else row_odd
                            g.dma_start(out=po, in_=payload_in).then_inc(lane.h, 16)
                            g.wait_ge(lane.h, base + 16)
                            g.dma_start(out=shF_d[fr:fr + 1, 0:1],
                                        in_=rank_s[:1, mcol:mcol + 1]).then_inc(lane.h, 16)
                    g.end_ifs()
                return None
            P.op("po", lane, list(reads) + ["rank", "shFz", gate],
                 list(writes) + [gate], emit, inc_amt=32, inc_attached=True)

        def init_pair(dst, shared_src, zero_src, row, magic, psem, reads, writes):
            lane, base, gate = polane()

            def emit(g, lane=lane, psem=psem, magic=magic):
                with nc.gpsimd.register(rname("r_p")) as r_p, \
                     nc.gpsimd.register(rname("r_c")) as r_c, \
                     nc.gpsimd.register(rname("r_v")) as r_v, \
                     nc.gpsimd.register(rname("r_w")) as r_w:
                    g.reg_load(r_p, rank_s[:1, 9:10])
                    with g.If_eq(r_p, 0):
                        g.dma_start(out=dst, in_=zero_src).then_inc(lane.h, 16)
                    with g.Else():
                        g.reg_mov(r_c, 1)
                        g.reg_mov(r_w, 0)
                        with g.While(r_c):
                            g.reg_add(r_w, r_w, 16)
                            g.dma_start(out=pollb[:1, 0:1],
                                        in_=shF_d[row:row + 1, 0:1]).then_inc(psem.h, 16)
                            g.wait_ge(psem.h, r_w)
                            g.reg_load(r_v, pollb[:1, 0:1])
                            g.reg_alu(r_c, r_v, magic, OP.bitwise_xor)
                        g.dma_start(out=dst, in_=shared_src).then_inc(lane.h, 16)
                    g.end_ifs()
                return None
            P.op("po", lane, list(reads) + ["rank", gate],
                 list(writes) + [gate], emit, inc_amt=16, inc_attached=True)

        zflags()

        with ExitStack() as ms:
            msb = lambda name, shape, dt: ms.enter_context(nc.sbuf_tensor(name, shape, dt))
            WB = msb("WBs", [128, 8, D_IN_PROJ], BF16)
            OWT = msb("OWTs", [128, 16, D_MODEL], BF16)
            cw_s = msb("cw_s", [128, 17, 4], F32)
            cb_s = msb("cb_s", [128, 17], F32)
            nw_s = msb("nw_s", [128, 8], F32)
            gw_s = msb("gw_s", [128, 16], F32)
            hd_s = msb("hd_s", [N_HEADS, 8], F32)
            Dbt = msb("Dbt", [128, N_HEADS], F32)
            drow = msb("drow", [1, N_HEADS], F32)
            rbufh = msb("rbufh", [128, D_MODEL], F32)
            rbuf2 = msb("rbuf2", [128, D_MODEL], F32)
            hF = msb("hF", [128, 8, SEG], BF16)
            xbc = msb("xbc", [128, 3 + SEG], BF16)
            xhal = msb("xhal", [128, 17, 3], BF16)
            hHal = msb("hHal", [128, 8, 3], BF16)
            xacc = msb("xacc", [128, SEG], BF16)
            xsil = msb("xsil", [128, SEG], BF16)
            Xr = msb("Xr", [128, NCH, D_INNER], BF16)
            BFt = msb("BFt", [64, SEG], BF16)
            CFt = msb("CFt", [64, SEG], BF16)
            BTt = msb("BTt", [128, NCH, D_STATE], BF16)
            dtF = msb("dtF", [N_HEADS, SEG], F32)
            dtmpF = msb("dtmpF", [N_HEADS, SEG], F32)
            dAcsF = msb("dAcsF", [N_HEADS, SEG], F32)
            decF = msb("decF", [N_HEADS, SEG], F32)
            dAcsT = msb("dAcsT", [128, NCH, N_HEADS], F32)
            ldT = msb("ldT", [128, NCH, N_HEADS], F32)
            lmdF = msb("lmdF", [N_HEADS, SEG], F32)
            his = msb("his", [N_HEADS, SEG], BF16)
            decT = msb("decT", [128, NCH, N_HEADS], F32)
            expT = msb("expT", [128, N_HEADS], F32)
            dzrow = msb("dzrow", [1, NCH, N_HEADS], F32)
            dzrx = msb("dzrx", [1, 512], BF16)
            rowq = msb("rowq", [65, 2048], BF16)
            dzB = msb("dzB", [64, D_INNER], BF16)
            S_run = msb("S_run", [64, D_INNER], F32)
            prevb = msb("prevb", [64, D_INNER], BF16)
            stz = msb("stz", [64, D_INNER], BF16)
            xds = msb("xds", [128, D_INNER], BF16)
            Mt = msb("Mt", [128, 8 * CHUNK], BF16)
            Mt2 = msb("Mt2", [128, 8 * CHUNK], BF16)
            Gt2 = msb("Gt2", [128, CHUNK], BF16)
            yosb = msb("yosb", [128, 1024], F32)
            Gt = msb("Gt", [128, CHUNK], BF16)
            Ybf = msb("Ybf", [128, D_INNER], BF16)
            zs = msb("zs", [128, D_INNER], BF16)
            yg = msb("yg", [128, D_INNER], BF16)
            yn = yg
            yF = msb("yF", [128, 16, 128], BF16)

            hF3 = hF[:]

            def mamba_layer(L, rin, rout):
                wl = w[L]
                dma("po", WB[:], wl["inwT"].rearrange("(dt p) e -> p dt e", p=128), [], ["WB"])
                dma("po", OWT[:], wl["owT"].rearrange("(it p) d -> p it d", p=128), [], ["OWT"])
                dma("sy", cw_s[:], wl["convw"].rearrange("(et p) k -> p et k", p=128), [], ["convw"])
                dma("sy", cb_s[:], wl["convb"], [], ["convb"])
                dma("sy", nw_s[:], wl["nw"], [], ["nw"])
                dma("sy", gw_s[:], wl["gw"], [], ["gw"])
                dma("sy", hd_s[:, 0:1], wl["dtb"].unsqueeze(1), ["dtU"], ["hd0"])
                dma("sy", hd_s[:, 1:2], wl["alog"].unsqueeze(1), ["hdA"], ["hd1"])
                dma("sy", hd_s[:, 3:4], wl["dsk"].unsqueeze(1), ["Dbt"], ["hd3"])

                def fold_wb(v):
                    for dt in range(8):
                        ins = v.tensor_scalar(WB[:, dt, :], WB[:, dt, :], nw_s[:, dt:dt + 1],
                                              0.0, op0=OP.mult, op1=OP.bypass)
                    return ins
                op("ve", ["WB", "nw"], ["WB"], fold_wb)

                def fold_ow(v):
                    for it in range(16):
                        ins = v.tensor_scalar(OWT[:, it, :], OWT[:, it, :], gw_s[:, it:it + 1],
                                              0.0, op0=OP.mult, op1=OP.bypass)
                    return ins
                op("ve", ["OWT", "gw"], ["OWT"], fold_ow)

                op("sc", ["hd1"], ["hdA"],
                   lambda s: s.activation(hd_s[:, 2:3], hd_s[:, 1:2], AF.Exp))
                op("ve", ["hdA"], ["hdA"],
                   lambda v: v.tensor_scalar(hd_s[:, 2:3], hd_s[:, 2:3], -1.0, 0.0,
                                             op0=OP.mult, op1=OP.bypass))
                op("pe", ["hd3", "i128f"], ["pB6"],
                   lambda t: t.transpose(pB2[0:1, 0:N_HEADS], hd_s[:, 3:4],
                                         i128f_s[0:N_HEADS, 0:N_HEADS]))
                op("ve", ["pB6"], ["drow", "pB6"],
                   lambda v: v.tensor_copy(drow[:], pB2[0:1, 0:N_HEADS]))
                op("pe", ["drow", "ones_f"], ["pB6"],
                   lambda t: t.matmul(pB2[:, 0:N_HEADS], ones_f[:, 0:128], drow[:],
                                      start=True, stop=True))
                op("ve", ["pB6"], ["Dbt", "pB6"],
                   lambda v: v.tensor_copy(Dbt[:], pB2[:, 0:N_HEADS]))

                def zinit(g):
                    g.memset(xhal[:], 0.0)
                    return g.memset(S_run[:], 0.0)
                op("po", ["xhal", "S_run"], ["xhal", "S_run"], zinit)
                if L == 0:
                    dma("sy", zz_d[:, :], S_run[:], ["S_run"], ["zzd"])

                hF_deps = ["hF%d" % c for c in range(NCH)]
                xr_deps = ["Xr%d" % et for et in range(16)]

                if L == 0:
                    dma("sy", rbufh[0:3, :], hx, ["rbufh_u"], ["rbufh"])
                else:
                    init_pair(rbufh[0:3, :], shH_d[:, :], zz_d[0:3, 0:D_MODEL],
                              2, MAGICS[2], pollH,
                              ["hpub", "zzd", "rbufh_u"], ["rbufh"])
                op("sc", ["rbufh"], ["ssq", "hT"],
                   lambda s: s.activation(hT[0:3, :], rbufh[0:3, :], AF.Square,
                                          accum_out=sc1[0:3, 0:1]))
                op("ve", ["ssq"], ["ms"],
                   lambda v: v.tensor_scalar(sc1[0:3, 1:2], sc1[0:3, 0:1], 1.0 / D_MODEL,
                                             EPS, op0=OP.mult, op1=OP.add))
                op("sc", ["ms"], ["sqv"],
                   lambda s: s.activation(sc1[0:3, 2:3], sc1[0:3, 1:2], AF.Ln))
                op("ve", ["sqv"], ["sqv"],
                   lambda v: v.tensor_scalar(sc1[0:3, 2:3], sc1[0:3, 2:3], -0.5, 0.0,
                                             op0=OP.mult, op1=OP.bypass))
                op("sc", ["sqv"], ["rstd"],
                   lambda s: s.activation(sc1[0:3, 3:4], sc1[0:3, 2:3], AF.Exp))
                op("ve", ["rstd", "rbufh"], ["hT", "rbufh_u"],
                   lambda v: v.tensor_scalar(hT[0:3, :], rbufh[0:3, :], sc1[0:3, 3:4],
                                             0.0, op0=OP.mult, op1=OP.bypass))

                def htrh(t):
                    for dt in range(8):
                        ins = t.transpose(pT[:, dt * 4:dt * 4 + 3],
                                          hT[0:3, dt * 128:(dt + 1) * 128],
                                          i128b_s[0:3, 0:3])
                    return ins
                op("pe", ["hT", "i128b"], ["pT"], htrh)
                def hcp(sx):
                    for dt in range(8):
                        ins = sx.copy(hHal[:, dt, :], pT[:, dt * 4:dt * 4 + 3])
                    return ins
                op("sc", ["pT"], ["hHal"], hcp)
                for et in range(17):
                    preg = "pB6" if et % 2 == 0 else "pB5"
                    pap = pB2[:, 0:3] if et % 2 == 0 else pB0[:, 512:515]

                    def hmm_h(t, et=et, pap=pap):
                        e0 = E_XBC0 + et * 128
                        for dt in range(8):
                            ins = t.matmul(pap, WB[:, dt, e0:e0 + 128],
                                           hHal[:, dt, :],
                                           start=(dt == 0), stop=(dt == 7))
                        return ins
                    op("pe", ["WB", "hHal"], [preg], hmm_h)
                    op("ve", [preg], ["xhal", preg],
                       lambda v, et=et, pap=pap: v.tensor_copy(xhal[:, et, :], pap))

                def seg_A(sg):
                    t0 = sg * SEG
                    for c in range(NCH):
                        tok0 = t0 + c * CHUNK
                        p = c % 2
                        rb = rbufh if p == 0 else rbuf2
                        ht = hT[:] if p == 0 else Ybf[:, 1024:2048]
                        scr = hT[:] if p == 0 else Ybf[:, 0:1024]
                        cb = 4 * p
                        rN, rU, nS, nM, nV, nR, nH = (
                            ("rbufh", "rbufh_u", "ssq", "ms", "sqv", "rstd", "hT")
                            if p == 0 else
                            ("rbuf2", "rbuf2_u", "ssqy", "msy", "sqy", "rsy", "Ybf1"))
                        nSw = [nS, nH] if p == 0 else [nS, "Ybf0"]
                        dma("sy", rb[:], rin[tok0:tok0 + CHUNK, :], [rU], [rN])
                        op("sc", [rN], nSw,
                           lambda s, rb=rb, scr=scr, cb=cb: s.activation(
                               scr, rb[:], AF.Square, accum_out=sc1[:, cb:cb + 1]))
                        op("ve", [nS], [nM],
                           lambda v, cb=cb: v.tensor_scalar(
                               sc1[:, cb + 1:cb + 2], sc1[:, cb:cb + 1], 1.0 / D_MODEL,
                               EPS, op0=OP.mult, op1=OP.add))
                        op("sc", [nM], [nV],
                           lambda s, cb=cb: s.activation(
                               sc1[:, cb + 2:cb + 3], sc1[:, cb + 1:cb + 2], AF.Ln))
                        op("ve", [nV], [nV],
                           lambda v, cb=cb: v.tensor_scalar(
                               sc1[:, cb + 2:cb + 3], sc1[:, cb + 2:cb + 3], -0.5, 0.0,
                               op0=OP.mult, op1=OP.bypass))
                        op("sc", [nV], [nR],
                           lambda s, cb=cb: s.activation(
                               sc1[:, cb + 3:cb + 4], sc1[:, cb + 2:cb + 3], AF.Exp))
                        op("ve", [nR, rN], [nH, rU],
                           lambda v, rb=rb, ht=ht, cb=cb: v.tensor_scalar(
                               ht, rb[:], sc1[:, cb + 3:cb + 4],
                               0.0, op0=OP.mult, op1=OP.bypass))
                        def htr(t, ht=ht):
                            for dt in range(8):
                                ins = t.transpose(pT[:, dt * 128:(dt + 1) * 128],
                                                  ht[:, dt * 128:(dt + 1) * 128], i128b_s[:])
                            return ins
                        op("pe", [nH, "i128b"], ["pT"], htr)
                        op("sc", ["pT"], ["hF%d" % c],
                           lambda sx, c=c: sx.copy(
                               hF3[:, :, c * 128:(c + 1) * 128],
                               pT[:].rearrange("p (a s) -> p a s", a=8)))
                    dma("po", hf_d[:, sg * 8 * SEG:(sg + 1) * 8 * SEG],
                        hF3.rearrange("p a t -> p (a t)"), hF_deps, ["hfd%d" % sg])
                    if L == 0 and sg == 0:
                        dump("hF", hF[:], hF_deps)

                    def dtmm(t):
                        for dt in range(8):
                            ins = t.matmul(pB2[0:N_HEADS, 0:SEG], WB[:, dt, E_DT0:E_DT0 + N_HEADS],
                                           hF3[:, dt, :], start=(dt == 0), stop=(dt == 7))
                        return ins
                    op("pe", ["WB"] + hF_deps, ["pB6"], dtmm)
                    op("ve", ["pB6", "hd0"], ["dtU", "pB6"],
                       lambda v: v.tensor_scalar(dtF[:], pB2[0:N_HEADS, 0:SEG],
                                                 hd_s[0:N_HEADS, 0:1], 0.0,
                                                 op0=OP.add, op1=OP.bypass))
                    op("ve", ["dtU"], ["dtm"],
                       lambda v: v.tensor_scalar(dtmpF[:], dtF[:], -1.0, 0.0,
                                                 op0=OP.mult, op1=OP.bypass))
                    op("ve", ["dtm", "dtU"], ["dtm"],
                       lambda v: v.tensor_tensor(dtmpF[:], dtmpF[:], dtF[:], op=OP.min))
                    op("sc", ["dtm"], ["dtm"],
                       lambda s: s.activation(dtmpF[:], dtmpF[:], AF.Exp))
                    op("sc", ["dtm", "consts"], ["dtm"],
                       lambda s: s.activation(dtmpF[:], dtmpF[:], AF.Ln,
                                              bias=consts[0:N_HEADS, 1:2]))
                    op("ve", ["dtU"], ["dtF"],
                       lambda v: v.tensor_scalar(dtF[:], dtF[:], 0.0, 0.0,
                                                 op0=OP.max, op1=OP.bypass))
                    op("ve", ["dtm", "dtF"], ["dtF"],
                       lambda v: v.tensor_tensor(dtF[:], dtF[:], dtmpF[:], op=OP.add))
                    op("ve", ["dtF", "hdA"], ["dAF"],
                       lambda v: v.tensor_scalar(dtmpF[:], dtF[:], hd_s[0:N_HEADS, 2:3], 0.0,
                                                 op0=OP.mult, op1=OP.bypass))
                    def dscan(v):
                        for c in range(NCH):
                            sl = slice(c * CHUNK, (c + 1) * CHUNK)
                            ins = v.tensor_tensor_scan(dAcsF[:, sl], dtmpF[:, sl], dtmpF[:, sl],
                                                       0.0, op0=OP.add, op1=OP.bypass)
                        return ins
                    op("ve", ["dAF"], ["dAcsF"], dscan)
                    def dec1(v):
                        for c in range(NCH):
                            sl = slice(c * CHUNK, (c + 1) * CHUNK)
                            last = slice(c * CHUNK + CHUNK - 1, c * CHUNK + CHUNK)
                            ins = v.tensor_scalar(decF[:, sl], dAcsF[:, sl], dAcsF[:, last],
                                                  -1.0, op0=OP.subtract, op1=OP.mult)
                        return ins
                    op("ve", ["dAcsF"], ["decF"], dec1)
                    op("sc", ["decF"], ["decF"],
                       lambda s: s.activation(decF[:], decF[:], AF.Exp))
                    op("ve", ["decF", "dtF"], ["decF"],
                       lambda v: v.tensor_tensor(decF[:], decF[:], dtF[:], op=OP.mult))
                    if L == 0 and sg == 0:
                        dump("dtF", dtF[:], ["dtF"])
                        dump("dAcsF", dAcsF[:], ["dAcsF"])
                        dump("decF", decF[:], ["decF"])
                    op("sc", ["dtF"], ["lmdF"],
                       lambda s: s.activation(lmdF[:], dtF[:], AF.Ln))
                    op("ve", ["lmdF", "dAcsF"], ["lmdF"],
                       lambda v: v.tensor_tensor(lmdF[:], lmdF[:], dAcsF[:], op=OP.subtract))
                    op("ve", ["dAcsF"], ["his"],
                       lambda v: v.tensor_copy(his[:], dAcsF[:]))
                    op("ve", ["his", "dAcsF"], ["dtmF2"],
                       lambda v: v.tensor_tensor(dtmpF[:], dAcsF[:], his[:], op=OP.subtract))
                    dma("sy", dach_d[0:N_HEADS, t0:t0 + SEG], his[:], ["his"], ["dach_d"])
                    dma("po", dach_d[N_HEADS:2 * N_HEADS, t0:t0 + SEG], dtmpF[:],
                        ["dtmF2"], ["dach_d2"])
                    def dtr(t):
                        for c in range(NCH):
                            t.transpose(pB0[:, c * 32:(c + 1) * 32],
                                        lmdF[:, c * CHUNK:(c + 1) * CHUNK],
                                        i128f_s[0:N_HEADS, 0:N_HEADS])
                        for c in range(NCH):
                            t.transpose(pB0[:, 128 + c * 32:128 + (c + 1) * 32],
                                        dAcsF[:, c * CHUNK:(c + 1) * CHUNK],
                                        i128f_s[0:N_HEADS, 0:N_HEADS])
                        for c in range(NCH):
                            ins = t.transpose(pB0[:, 256 + c * 32:256 + (c + 1) * 32],
                                              decF[:, c * CHUNK:(c + 1) * CHUNK],
                                              i128f_s[0:N_HEADS, 0:N_HEADS])
                        return ins
                    op("pe", ["lmdF", "dAcsF", "decF", "i128f"], ["pB4"], dtr)
                    def dtcp(sx):
                        sx.copy(ldT[:].rearrange("p c h -> p (c h)"), pB0[:, 0:128])
                        sx.copy(dAcsT[:].rearrange("p c h -> p (c h)"), pB0[:, 128:256])
                        return sx.copy(decT[:].rearrange("p c h -> p (c h)"),
                                       pB0[:, 256:384])
                    op("sc", ["pB4"], ["ldT", "dAcsT", "decT"], dtcp)
                    dma("sy", dzrow[:].rearrange("o c h -> o (c h)"),
                        dAcsT[127:128, :, :].rearrange("o c h -> o (c h)"),
                        ["dAcsT"], ["dzrow"])
                    op("sc", ["dzrow"], ["dzrow"],
                       lambda s: s.activation(dzrow[:].rearrange("o c h -> o (c h)"),
                                              dzrow[:].rearrange("o c h -> o (c h)"), AF.Exp))
                    dma("sy", dtt_d[:, sg * 256:sg * 256 + 128],
                        dAcsT[:].rearrange("p c h -> p (c h)"), ["dAcsT"],
                        ["dttd%d" % sg])
                    dma("sy", dtt_d[:, sg * 256 + 128:sg * 256 + 256],
                        ldT[:].rearrange("p c h -> p (c h)"), ["ldT"],
                        ["dttl%d" % sg])

                    def emit_amm(et):
                        e0 = E_XBC0 + et * 128
                        preg = "pB6" if et % 2 == 0 else "pB5"
                        pap = pB2[:, 0:SEG] if et % 2 == 0 else pB0[:, 512:512 + SEG]
                        def amm(t, e0=e0, pap=pap):
                            for dt in range(8):
                                ins = t.matmul(pap, WB[:, dt, e0:e0 + 128],
                                               hF3[:, dt, :], start=(dt == 0), stop=(dt == 7))
                            return ins
                        op("pe", ["WB"] + hF_deps, [preg], amm)

                    def emit_rest(et):
                        preg = "pB6" if et % 2 == 0 else "pB5"
                        pap = pB2[:, 0:SEG] if et % 2 == 0 else pB0[:, 512:512 + SEG]
                        def xasm(v, et=et, pap=pap):
                            v.tensor_copy(xbc[:, 0:3], xhal[:, et, :])
                            return v.tensor_copy(xbc[:, 3:3 + SEG], pap)
                        op("ve", [preg, "xhal"], ["xbc", preg], xasm)
                        op("ve", ["xbc"], ["xhal"],
                           lambda v, et=et: v.tensor_copy(xhal[:, et, :], xbc[:, SEG:SEG + 3]))
                        op("ve", ["xbc", "convw", "convb"], ["xacc"],
                           lambda v, et=et: v.tensor_scalar(xacc[:], xbc[:, 0:SEG],
                                                            cw_s[:, et, 0:1], cb_s[:, et:et + 1],
                                                            op0=OP.mult, op1=OP.add))
                        for k in (1, 2, 3):
                            op("ve", ["xbc", "convw", "xacc"], ["xacc"],
                               lambda v, et=et, k=k: v.scalar_tensor_tensor(
                                   xacc[:], xbc[:, k:SEG + k], cw_s[:, et, k:k + 1], xacc[:],
                                   op0=OP.mult, op1=OP.add))
                        op("sc", ["xacc"], ["xsil"],
                           lambda s: s.activation(xsil[:], xacc[:], AF.Silu))
                        if et < 16:
                            def xtr(t):
                                for c in range(NCH):
                                    ins = t.transpose(pT[:, c * 128:(c + 1) * 128],
                                                      xsil[:, c * 128:(c + 1) * 128], i128b_s[:])
                                return ins
                            op("pe", ["xsil", "i128b"], ["pT"], xtr)
                            op("sc", ["pT"], ["Xr%d" % et],
                               lambda sx, et=et: sx.copy(
                                   Xr[:, :, et * 128:(et + 1) * 128],
                                   pT[:, 0:NCH * 128].rearrange("p (c f) -> p c f", c=NCH)))
                        else:
                            op("ve", ["xsil"], ["BFt"],
                               lambda v: v.tensor_copy(BFt[:], xsil[0:64, :]))
                            dma("po", CFt[:], xsil[64:128, :], ["xsil"], ["CFt"])
                            def btr(t):
                                for c in range(NCH):
                                    ins = t.transpose(pT[:, c * 64:(c + 1) * 64],
                                                      BFt[:, c * 128:(c + 1) * 128],
                                                      i128b_s[0:64, 0:64])
                                return ins
                            op("pe", ["BFt", "i128b"], ["pT"], btr)
                            op("sc", ["pT"], ["BTt"],
                               lambda sx: sx.copy(
                                   BTt[:].rearrange("p c n -> p (c n)"), pT[:, 0:NCH * 64]))

                    emit_amm(0)
                    for et in range(1, 17):
                        emit_amm(et)
                        emit_rest(et - 1)
                    emit_rest(16)
                    dma("po", xr_d[:, sg * NCH * D_INNER:(sg + 1) * NCH * D_INNER],
                        Xr[:].rearrange("p c f -> p (c f)"), xr_deps, ["xrd%d" % sg])
                    dma("sy", bc_d[:, sg * 2 * SEG:sg * 2 * SEG + SEG], BFt[:],
                        ["BFt"], ["bcb%d" % sg])
                    dma("sy", bc_d[:, sg * 2 * SEG + SEG:(sg + 1) * 2 * SEG], CFt[:],
                        ["CFt"], ["bcc%d" % sg])
                    if L == 0 and sg == 0:
                        dump("Xr", Xr[:], xr_deps)
                        dump("BFt", BFt[:], ["BFt"])
                        dump("CFt", CFt[:], ["CFt"])
                        dump("BTt", BTt[:], ["BTt"])
                        dump("decT", decT[:], ["decT"])
                        dump("dzrow", dzrow[:], ["dzrow"])

                    for c in range(NCH):
                        op("ve", xr_deps + ["decT"], ["xds"],
                           lambda v, c=c: v.tensor_tensor(
                               xds[:].rearrange("p (h q) -> p h q", h=N_HEADS),
                               Xr[:, c, :].rearrange("p (h q) -> p h q", h=N_HEADS),
                               decT[:, c, :].unsqueeze(2).broadcast_to([128, N_HEADS, D_HEAD]),
                               op=OP.mult))
                        def smm(t, c=c):
                            for n in range(4):
                                ins = t.matmul(pAt[n][0:64, :], BTt[:, c, :],
                                               xds[:, n * 512:(n + 1) * 512],
                                               start=True, stop=True)
                            return ins
                        op("pe", ["BTt", "xds"], ["pA0", "pA1", "pA2", "pA3"], smm)
                        def stzev(sx):
                            for n in range(4):
                                ins = sx.copy(stz[:, n * 512:(n + 1) * 512], pAt[n][0:64, :])
                            return ins
                        op("sc", ["pA0", "pA1", "pA2", "pA3"], ["stz"], stzev)
                        dma("po", stz_d[:, (sg * NCH + c) * D_INNER:
                                        (sg * NCH + c + 1) * D_INNER],
                            stz[:], ["stz"], ["stzd_%d_%d" % (sg, c)])
                        for n in range(4):
                            op("ve", ["dzrow"], ["dzrx"],
                               lambda v, c=c, n=n: v.tensor_copy(
                                   dzrx[:].rearrange("o (h q) -> o h q", h=8),
                                   dzrow[:, c, n * 8:(n + 1) * 8].unsqueeze(2)
                                       .broadcast_to([1, 8, D_HEAD])))
                            op("pe", ["dzrx", "ones_b", "stz"], ["pA%d" % n],
                               lambda t, n=n: t.matmul(pAt[n][0:64, :],
                                                       ones_b[:, 0:64], dzrx[:],
                                                       start=True, stop=True))
                        def dzev(sx):
                            for n in range(4):
                                ins = sx.copy(dzB[:, n * 512:(n + 1) * 512],
                                              pAt[n][0:64, :])
                            return ins
                        op("sc", ["pA0", "pA1", "pA2", "pA3", "dzB"], ["dzB"], dzev)
                        dma("po", dzx_d[:, (sg * NCH + c) * D_INNER:
                                        (sg * NCH + c + 1) * D_INNER],
                            dzB[:], ["dzB"], ["dzx_%d_%d" % (sg, c)])
                        op("ve", ["dzB", "S_run"], ["S_run"],
                           lambda v: v.tensor_tensor(S_run[:], S_run[:], dzB[:],
                                                     op=OP.mult))
                        op("ve", ["stz", "S_run"], ["S_run"],
                           lambda v: v.tensor_tensor(S_run[:], S_run[:], stz[:], op=OP.add))
                        if dbg and L == 0 and sg == 0 and c == 0:
                            dump("xds0", xds[:], ["xds"])
                            dump("stz0", stz[:], ["stz"])
                        if dbg and L == 0 and sg == 0 and c == 1:
                            dump("Srun1", S_run[:], ["S_run"])

                def seg_B(sg):
                    t0 = sg * SEG
                    dma("sy", hF3.rearrange("p a t -> p (a t)"),
                        hf_d[:, sg * 8 * SEG:(sg + 1) * 8 * SEG],
                        ["hfd%d" % sg], hF_deps)
                    dma("po", Xr[:].rearrange("p c f -> p (c f)"),
                        xr_d[:, sg * NCH * D_INNER:(sg + 1) * NCH * D_INNER],
                        ["xrd%d" % sg], xr_deps)
                    dma("sy", BFt[:], bc_d[:, sg * 2 * SEG:sg * 2 * SEG + SEG],
                        ["bcb%d" % sg], ["BFt"])
                    dma("sy", CFt[:], bc_d[:, sg * 2 * SEG + SEG:(sg + 1) * 2 * SEG],
                        ["bcc%d" % sg], ["CFt"])
                    dma("po", dAcsT[:].rearrange("p c h -> p (c h)"),
                        dtt_d[:, sg * 256:sg * 256 + 128], ["dttd%d" % sg], ["dAcsT"])
                    dma("po", ldT[:].rearrange("p c h -> p (c h)"),
                        dtt_d[:, sg * 256 + 128:sg * 256 + 256], ["dttl%d" % sg], ["ldT"])

                    def emit_proB(c):
                        gts = Gt if c % 2 == 0 else Gt2
                        op("pe", ["BFt", "CFt"], ["pB6"],
                           lambda t, c=c: t.matmul(pB2[:, 0:128],
                                                   BFt[:, c * 128:(c + 1) * 128],
                                                   CFt[:, c * 128:(c + 1) * 128],
                                                   start=True, stop=True))
                        op("ve", ["pB6", "triu"], ["Gt%d" % (c % 2), "pB6"],
                           lambda v, gts=gts: v.tensor_tensor(gts[:], pB2[:, 0:128],
                                                              triu_s[:], op=OP.mult))
                        op("ve", ["S_run"], ["prevb"],
                           lambda v: v.tensor_copy(prevb[:], S_run[:]))
                        dma("sy", stz[:],
                            stz_d[:, (sg * NCH + c) * D_INNER:
                                  (sg * NCH + c + 1) * D_INNER],
                            ["stzd_%d_%d" % (sg, c)], ["stz"])
                        dma("po", dzB[:],
                            dzx_d[:, (sg * NCH + c) * D_INNER:
                                  (sg * NCH + c + 1) * D_INNER],
                            ["dzx_%d_%d" % (sg, c)], ["dzB"])
                        op("ve", ["dzB", "prevb", "S_run"], ["S_run"],
                           lambda v: v.tensor_tensor(S_run[:], S_run[:], dzB[:],
                                                     op=OP.mult))
                        op("ve", ["stz", "S_run"], ["S_run"],
                           lambda v: v.tensor_tensor(S_run[:], S_run[:], stz[:], op=OP.add))
                        if dbg and L == 0 and sg == 0 and c == 1:
                            dump("prev1", prevb[:], ["prevb"])

                    def emit_mb(c):
                        tok0 = t0 + c * CHUNK
                        gts = Gt if c % 2 == 0 else Gt2
                        op("sc", ["dAcsT"], ["expT"],
                           lambda s, c=c: s.activation(expT[:], dAcsT[:, c, :], AF.Exp))
                        def yd_emit(q, h8):
                            mts = Mt if q % 2 == 0 else Mt2
                            def ydmm(t, h8=h8, q=q, c=c, mts=mts):
                                for hh in range(8):
                                    h = h8 + hh
                                    ins = t.matmul(pAt[q][:, hh * 64:(hh + 1) * 64],
                                                   mts[:, hh * 128:(hh + 1) * 128],
                                                   Xr[:, c, h * 64:(h + 1) * 64],
                                                   start=(hh == 0), stop=False)
                                return ins
                            op("pe", ["Mtq%d" % (q % 2), "S_run"] + xr_deps,
                               ["pA%d" % q, "Mtq%d" % (q % 2)], ydmm)
                        def rql(qq4):
                            h84 = qq4 * 8
                            bq = qq4 % 3
                            dma("sy", rowq[32 * bq:32 * bq + 1, :]
                                .rearrange("o (g h l) -> o g h l", g=2, h=8),
                                dach_d[:, tok0:tok0 + CHUNK]
                                .rearrange("(g h) l -> g h l", g=2)[:, h84:h84 + 8, :]
                                .unsqueeze(0),
                                ["dach_d", "dach_d2", "rowf_u%d" % bq],
                                ["rowf%d" % bq])
                        for qq4 in range(3):
                            rql(qq4)
                        pend = []
                        for half in range(2):
                            for q in range(2):
                                h8 = half * 16 + q * 8
                                qq = half * 2 + q
                                if qq % 2 == 0:
                                    pregs = ["pB4", "pB5"]
                                    bcaps = [pB0[:, 0:512], pB0[:, 512:1024]]
                                else:
                                    pregs = ["pA2", "pA3"]
                                    bcaps = [pAt[2][:, :], pAt[3][:, :]]
                                mts = Mt if qq % 2 == 0 else Mt2
                                bq3 = 32 * (qq % 3)
                                rqa = rowq[bq3:bq3 + 1, :]
                                oq = onesq[bq3:bq3 + 1, :]
                                def bcmm(t, bcaps=bcaps, rqa=rqa, oq=oq):
                                    for n in range(2):
                                        t.matmul(bcaps[n], oq,
                                                 rqa[:, n * 512:(n + 1) * 512],
                                                 start=True, stop=False)
                                        ins = t.matmul(bcaps[n], oq,
                                                       rqa[:, 1024 + n * 512:1024 + (n + 1) * 512],
                                                       start=False, stop=True)
                                    return ins
                                op("pe", ["rowf%d" % (qq % 3), "onesq%d" % (qq % 3)],
                                   pregs, bcmm)
                                def madd(v, c=c, h8=h8, bcaps=bcaps):
                                    for n in range(2):
                                        ins = v.tensor_tensor(
                                            bcaps[n].rearrange("p (h l) -> p h l", h=4),
                                            bcaps[n].rearrange("p (h l) -> p h l", h=4),
                                            ldT[:, c, h8 + n * 4:h8 + (n + 1) * 4]
                                                .unsqueeze(2).broadcast_to([128, 4, CHUNK]),
                                            op=OP.add)
                                    return ins
                                op("ve", pregs + ["ldT"],
                                   pregs + ["rowf_u%d" % (qq % 3)], madd)
                                if qq == 0:
                                    rql(3)
                                def mmin(v, bcaps=bcaps):
                                    for n in range(2):
                                        ins = v.tensor_scalar(bcaps[n], bcaps[n], 80.0, 0.0,
                                                              op0=OP.min, op1=OP.bypass)
                                    return ins
                                op("ve", pregs, pregs, mmin)
                                def mexp(sx, bcaps=bcaps):
                                    for n in range(2):
                                        ins = sx.activation(bcaps[n], bcaps[n], AF.Exp)
                                    return ins
                                op("sc", pregs, pregs, mexp)
                                def gmult(v, bcaps=bcaps, mts=mts, gts=gts):
                                    for n in range(2):
                                        ins = v.tensor_tensor(
                                            mts[:, n * 512:(n + 1) * 512]
                                                .rearrange("p (h l) -> p h l", h=4),
                                            bcaps[n].rearrange("p (h l) -> p h l", h=4),
                                            gts[:].unsqueeze(1).broadcast_to([128, 4, CHUNK]),
                                            op=OP.mult)
                                    return ins
                                op("ve", pregs + ["Gt%d" % (c % 2)], ["Mtq%d" % (qq % 2)], gmult)
                                pend.append((qq, h8))
                                if len(pend) >= 2:
                                    yd_emit(*pend.pop(0))
                        for pq in pend:
                            yd_emit(*pq)
                        op("ve", xr_deps + ["Dbt", "xds"], ["xds"],
                           lambda v, c=c: v.tensor_tensor(
                               xds[:].rearrange("p (h q) -> p h q", h=N_HEADS),
                               Xr[:, c, :].rearrange("p (h q) -> p h q", h=N_HEADS),
                               Dbt[:, :].unsqueeze(2).broadcast_to([128, N_HEADS, D_HEAD]),
                               op=OP.mult))
                        def skmm(t):
                            for n in range(4):
                                ins = t.matmul(pAt[n][:, :], i128b_s[:],
                                               xds[:, n * 512:(n + 1) * 512],
                                               start=False, stop=True)
                            return ins
                        op("pe", ["xds", "i128b"], ["pA0", "pA1", "pA2", "pA3"], skmm)
                        for half in range(2):
                            def yomm(t, c=c, half=half):
                                for n in range(2):
                                    ins = t.matmul(
                                        pB0[:, n * 512:(n + 1) * 512],
                                        CFt[:, c * 128:(c + 1) * 128],
                                        prevb[:, half * 1024 + n * 512:half * 1024 + (n + 1) * 512],
                                        start=True, stop=True)
                                return ins
                            op("pe", ["CFt", "prevb"], ["pB4", "pB5"], yomm)
                            op("ve", ["pB4", "pB5", "expT"], ["yosb"],
                               lambda v, half=half: v.tensor_tensor(
                                   yosb[:].rearrange("p (h q) -> p h q", h=16),
                                   pB0[:, 0:1024].rearrange("p (h q) -> p h q", h=16),
                                   expT[:, half * 16:(half + 1) * 16].unsqueeze(2)
                                       .broadcast_to([128, 16, D_HEAD]),
                                   op=OP.mult))
                            def ybadd(v, half=half):
                                for n in range(2):
                                    bk = 2 * half + n
                                    ins = v.tensor_tensor(
                                        Ybf[:, bk * 512:(bk + 1) * 512],
                                        pAt[2 * half + n][:, :],
                                        yosb[:, n * 512:(n + 1) * 512],
                                        op=OP.add)
                                return ins
                            op("ve", ["yosb", "pA%d" % (2 * half), "pA%d" % (2 * half + 1)],
                               ["Ybf%d" % half, "pA%d" % (2 * half), "pA%d" % (2 * half + 1)],
                               ybadd)
                        if dbg and L == 0 and sg == 0 and c == 0:
                            dump("Ybf0d", Ybf[:], ["Ybf0", "Ybf1"])

                    def emit_epi1(c):
                        tok0 = t0 + c * CHUNK
                        for nt in range(4):
                            zreg = "pB6" if nt % 2 == 0 else "pB4"
                            zap = pB2[:, 0:512] if nt % 2 == 0 else pB0[:, 0:512]
                            def zmm(t, c=c, nt=nt, zap=zap):
                                for dt in range(8):
                                    ins = t.matmul(zap,
                                                   hF3[:, dt, c * 128:(c + 1) * 128],
                                                   WB[:, dt, nt * 512:(nt + 1) * 512],
                                                   start=(dt == 0), stop=(dt == 7))
                                return ins
                            op("pe", ["WB"] + hF_deps + ["zs%d" % nt], [zreg], zmm)
                            op("sc", [zreg], ["zs%d" % nt, zreg],
                               lambda s, nt=nt, zap=zap: s.activation(
                                   zs[:, nt * 512:(nt + 1) * 512], zap, AF.Silu))
                        if dbg and L == 0 and sg == 0 and c == 0:
                            dump("zs0d", zs[:], ["zs0", "zs1", "zs2", "zs3"])
                        op("ve", ["Ybf0", "Ybf1", "zs0", "zs1", "zs2", "zs3"], ["yg"],
                           lambda v: v.tensor_tensor(yg[:], Ybf[:], zs[:], op=OP.mult))
                        op("sc", ["yg"], ["ssqy", "Ybf0", "Ybf1"],
                           lambda s: s.activation(Ybf[:], yg[:], AF.Square,
                                                  accum_out=sc1[:, 4:5]))
                        op("ve", ["ssqy"], ["msy"],
                           lambda v: v.tensor_scalar(sc1[:, 5:6], sc1[:, 4:5], 1.0 / D_INNER,
                                                     EPS, op0=OP.mult, op1=OP.add))
                        op("sc", ["msy"], ["sqy"],
                           lambda s: s.activation(sc1[:, 6:7], sc1[:, 5:6], AF.Ln))
                        op("ve", ["sqy"], ["sqy"],
                           lambda v: v.tensor_scalar(sc1[:, 6:7], sc1[:, 6:7], -0.5, 0.0,
                                                     op0=OP.mult, op1=OP.bypass))
                        op("sc", ["sqy"], ["rsy"],
                           lambda s: s.activation(sc1[:, 7:8], sc1[:, 6:7], AF.Exp))
                        op("ve", ["rsy", "yg"], ["yg"],
                           lambda v: v.tensor_scalar(yn[:], yg[:], sc1[:, 7:8], 0.0,
                                                     op0=OP.mult, op1=OP.bypass))
                        if dbg and L == 0 and sg == 0 and c == 0:
                            dump("yn0", yn[:], ["yg"])
                    def emit_epi2(c):
                        tok0 = t0 + c * CHUNK
                        for g2 in range(2):
                            def ytr(t, g2=g2):
                                for it in range(8):
                                    ins = t.transpose(
                                        pT[:, it * 128:(it + 1) * 128],
                                        yn[:, (g2 * 8 + it) * 128:(g2 * 8 + it + 1) * 128],
                                        i128b_s[:])
                                return ins
                            op("pe", ["yg", "i128b"], ["pT"], ytr)
                            op("ve", ["pT"], ["yF%d" % g2],
                               lambda v, g2=g2: v.tensor_copy(
                                   yF[:, g2 * 8:(g2 + 1) * 8, :].rearrange("p i f -> p (i f)"),
                                   pT[:]))
                        dma("sy", rbuf2[:], rin[tok0:tok0 + CHUNK, :], ["rbuf2_u"], ["rbuf2"])
                        if L == 1:
                            dma("sy", rbufh[:], xin[tok0:tok0 + CHUNK, :], ["rbufh_u"], ["rbufh"])
                        def opmm(t):
                            for dh in range(2):
                                for it in range(16):
                                    ins = t.matmul(pB0[:, dh * 512:(dh + 1) * 512], yF[:, it, :],
                                                   OWT[:, it, dh * 512:(dh + 1) * 512],
                                                   start=(it == 0), stop=(it == 15))
                            return ins
                        op("pe", ["yF0", "yF1", "OWT"], ["pB4", "pB5"], opmm)
                        op("ve", ["pB4", "pB5", "rbuf2"], ["rbuf2", "pB4", "pB5"],
                           lambda v: v.tensor_tensor(rbuf2[:], rbuf2[:], pB0[:, 0:1024],
                                                     op=OP.add))
                        if L == 1:
                            op("ve", ["rbuf2", "rbufh"], ["rbuf2", "rbufh_u"],
                               lambda v: v.tensor_tensor(rbuf2[:], rbuf2[:], rbufh[:],
                                                         op=OP.add))
                        dma("sy", rout[tok0:tok0 + CHUNK, :], rbuf2[:], ["rbuf2"],
                            ["rbuf2_u", "rstore"])

                    emit_proB(0)
                    emit_mb(0)
                    for c in range(1, NCH):
                        emit_epi1(c - 1)
                        emit_proB(c)
                        emit_mb(c)
                        emit_epi2(c - 1)
                    emit_epi1(NCH - 1)
                    emit_epi2(NCH - 1)

                for sg in range(NSEG):
                    seg_A(sg)
                pub2(shS_d[L], dum_d[:, :], S_run[:], L, 4 + L, 1 + L,
                     ["S_run"], ["spub%d" % L])
                init_pair(S_run[:], shS_d[L], zz_d[:, :], L, MAGICS[L],
                          pollS[L], ["spub%d" % L, "zzd"], ["S_run"])
                for sg in range(NSEG):
                    seg_B(sg)
                if L == 0:
                    pub2(shH_d[:, :], dum_d[0:3, 0:D_MODEL], rout[TH - 3:TH, :],
                         2, 6, 3, ["rstore"], ["hpub"])

            mamba_layer(0, xin, r1)
            mamba_layer(1, r1, r2)

            snap = [(s, s.count) for s in allsems]
            for en in ("sy", "po", "pe", "sc", "ve"):
                def bar(eng, snap=snap):
                    for s, v in snap:
                        if v > 0:
                            eng.wait_ge(s.h, v)
                P.raw(en, bar)
                for s, v in snap:
                    P.hw[(en, s.name)] = max(P.hw.get((en, s.name), 0), v)

        RM = sbT("RM", [128, NTT, D_MODEL], F32)
        XF = sbT("XF", [128, 8, moe_tok], BF16)
        W1q = sbT("W1q", [128, 8, 1024], BF16)
        W2q = sbT("W2q", [128, 8, 1024], BF16)
        W1q2 = sbT("W1q2", [128, 8, 1024], BF16)
        W2q2 = sbT("W2q2", [128, 8, 1024], BF16)
        ab = sbT("ab", [128, 8, 512], BF16)
        wg_s = sbT("wg_s", [128, 8, 8], BF16)
        bgb = sbT("bgb", [128, 8], F32)
        bgr = sbT("bgr", [1, 8], F32)
        b1q = sbT("b1q", [128, 8], F32)
        b2b = sbT("b2b", [8, D_MODEL], BF16)
        logt = sbT("logt", [128, NTT, 8], F32)
        comb = sbT("comb", [128, NTT, 8], F32)
        combB = sbT("combB", [128, NTT, 8], BF16)
        combF = sbT("combF", [8, NTT * 128], BF16)

        dma("po", wg_s[:], Wg.rearrange("(dt p) e -> p dt e", p=128), [], ["wg"])
        dma("po", b2b[:], b2, [], ["b2b"])
        dma("sy", bgr[:], bg, [], ["bgr"])

        rlane = POL[0]
        rbase = rlane.count

        def rload(g):
            with nc.gpsimd.register(rname("r_rk")) as r_rk, \
                 nc.gpsimd.register(rname("r_c")) as r_c, \
                 nc.gpsimd.register(rname("r_v")) as r_v, \
                 nc.gpsimd.register(rname("r_w")) as r_w:
                g.reg_load(r_rk, rank_s[:1, :1])
                for rk in range(min(n_ranks, T // moe_tok)):
                    par = rk & 1
                    P0 = (rk * moe_tok) // TH
                    own = (par == P0)
                    partner = rk ^ 1
                    myloc = rk * moe_tok - P0 * TH
                    with g.If_eq(r_rk, rk):
                        if own:
                            pnon = ((partner & 1) != ((partner * moe_tok) // TH))
                            if pnon:
                                ploc = partner * moe_tok - P0 * TH
                                g.dma_start(out=shR_d[0:moe_tok, :],
                                            in_=r2[ploc:ploc + moe_tok, :]
                                            ).then_inc(rlane.h, 16)
                                g.wait_ge(rlane.h, rbase + 16)
                                g.dma_start(out=shF_d[3:4, 0:1],
                                            in_=rank_s[:1, 4:5]).then_inc(rlane.h, 16)
                            else:
                                g.dma_start(out=shF_d[7:8, 0:1],
                                            in_=rank_s[:1, 4:5]).then_inc(rlane.h, 16)
                                g.wait_ge(rlane.h, rbase + 16)
                                g.dma_start(out=shF_d[7:8, 1:2],
                                            in_=rank_s[:1, 4:5]).then_inc(rlane.h, 16)
                            g.dma_start(out=RM[:],
                                        in_=r2[myloc:myloc + moe_tok, :]
                                        .rearrange("(c p) d -> p c d", p=128)
                                        ).then_inc(rlane.h, 16)
                        else:
                            g.dma_start(out=shF_d[7:8, 2:3],
                                        in_=rank_s[:1, 4:5]).then_inc(rlane.h, 16)
                            g.wait_ge(rlane.h, rbase + 16)
                            g.dma_start(out=shF_d[7:8, 3:4],
                                        in_=rank_s[:1, 4:5]).then_inc(rlane.h, 16)
                            g.reg_mov(r_c, 1)
                            g.reg_mov(r_w, 0)
                            with g.While(r_c):
                                g.reg_add(r_w, r_w, 16)
                                g.dma_start(out=pollb[:1, 0:1],
                                            in_=shF_d[3:4, 0:1]).then_inc(pollM.h, 16)
                                g.wait_ge(pollM.h, r_w)
                                g.reg_load(r_v, pollb[:1, 0:1])
                                g.reg_alu(r_c, r_v, MAGICS[3], OP.bitwise_xor)
                            g.dma_start(out=RM[:],
                                        in_=shR_d[0:moe_tok, :]
                                        .rearrange("(c p) d -> p c d", p=128)
                                        ).then_inc(rlane.h, 16)
                    g.end_ifs()
            return None
        P.op("po", rlane, ["rank", "rstore", "shFz", "lane_" + rlane.name],
             ["RMall", "lane_" + rlane.name], rload, inc_amt=48, inc_attached=True)

        for tt in range(NTT):
            op("ve", ["RMall", "RM%d" % tt], ["lnm"],
               lambda v, tt=tt: v.reduce_sum(sc1[:, 0:1], RM[:, tt, :], axis=AX.X))
            op("ve", ["lnm"], ["lnm2"],
               lambda v: v.tensor_scalar(sc1[:, 1:2], sc1[:, 0:1], 1.0 / D_MODEL, 0.0,
                                         op0=OP.mult, op1=OP.bypass))
            op("ve", ["lnm2", "RMall"], ["RM%d" % tt],
               lambda v, tt=tt: v.tensor_scalar(RM[:, tt, :], RM[:, tt, :], sc1[:, 1:2],
                                                0.0, op0=OP.subtract, op1=OP.bypass))
            op("sc", ["RM%d" % tt], ["lnv", "hT"],
               lambda s, tt=tt: s.activation(hT[:], RM[:, tt, :], AF.Square,
                                             accum_out=sc1[:, 2:3]))
            op("ve", ["lnv"], ["lnv2"],
               lambda v: v.tensor_scalar(sc1[:, 3:4], sc1[:, 2:3], 1.0 / D_MODEL, EPS,
                                         op0=OP.mult, op1=OP.add))
            op("sc", ["lnv2"], ["lnsq"],
               lambda s: s.activation(sc1[:, 4:5], sc1[:, 3:4], AF.Ln))
            op("ve", ["lnsq"], ["lnsq"],
               lambda v: v.tensor_scalar(sc1[:, 4:5], sc1[:, 4:5], -0.5, 0.0,
                                         op0=OP.mult, op1=OP.bypass))
            op("sc", ["lnsq"], ["lnr"],
               lambda s: s.activation(sc1[:, 5:6], sc1[:, 4:5], AF.Exp))
            op("ve", ["lnr", "RM%d" % tt], ["RM%d" % tt],
               lambda v, tt=tt: v.tensor_scalar(RM[:, tt, :], RM[:, tt, :], sc1[:, 5:6],
                                                0.0, op0=OP.mult, op1=OP.bypass))
            op("ve", ["RM%d" % tt], ["hT"],
               lambda v, tt=tt: v.tensor_scalar(hT[:], RM[:, tt, :], 1.0, 0.0,
                                                op0=OP.mult, op1=OP.bypass))
            def xtr2(t):
                for dt in range(8):
                    ins = t.transpose(pT[:, dt * 128:(dt + 1) * 128],
                                      hT[:, dt * 128:(dt + 1) * 128], i128b_s[:])
                return ins
            op("pe", ["hT", "i128b"], ["pT"], xtr2)
            op("ve", ["pT"], ["XF%d" % tt],
               lambda v, tt=tt: v.tensor_copy(
                   XF[:, :, tt * 128:(tt + 1) * 128],
                   pT[:].rearrange("p (a s) -> p a s", a=8)))
        xf_deps = ["XF%d" % tt for tt in range(NTT)]

        op("pe", ["bgr", "ones_f"], ["pB6"],
           lambda t: t.matmul(pB2[:, 0:8], ones_f[:, 0:128], bgr[:], start=True, stop=True))
        op("ve", ["pB6"], ["bgb", "pB6"], lambda v: v.tensor_copy(bgb[:], pB2[:, 0:8]))
        for tt in range(NTT):
            def lgmm(t, tt=tt):
                for dt in range(8):
                    ins = t.matmul(pB2[:, 0:8], XF[:, dt, tt * 128:(tt + 1) * 128],
                                   wg_s[:, dt, :], start=(dt == 0), stop=(dt == 7))
                return ins
            op("pe", ["XF%d" % tt, "wg"], ["pB6"], lgmm)
            op("ve", ["pB6", "bgb"], ["logt%d" % tt, "pB6"],
               lambda v, tt=tt: v.tensor_tensor(logt[:, tt, :], pB2[:, 0:8], bgb[:], op=OP.add))
            op("ve", ["logt%d" % tt], ["m1"],
               lambda v, tt=tt: v.reduce_max(sc1[:, 0:1], logt[:, tt, :], axis=AX.X))
            op("ve", ["m1", "logt%d" % tt], ["cm%d" % tt],
               lambda v, tt=tt: v.tensor_scalar(comb[:, tt, :], logt[:, tt, :], sc1[:, 0:1],
                                                0.0, op0=OP.is_ge, op1=OP.bypass))
            op("ve", ["cm%d" % tt, "logt%d" % tt], ["cm%d" % tt],
               lambda v, tt=tt: v.scalar_tensor_tensor(comb[:, tt, :], comb[:, tt, :], -1e30,
                                                       logt[:, tt, :], op0=OP.mult, op1=OP.add))
            op("ve", ["cm%d" % tt], ["m2"],
               lambda v, tt=tt: v.reduce_max(sc1[:, 1:2], comb[:, tt, :], axis=AX.X))
            op("ve", ["m2", "logt%d" % tt], ["c0%d" % tt],
               lambda v, tt=tt: v.tensor_scalar(comb[:, tt, :], logt[:, tt, :], sc1[:, 1:2],
                                                0.0, op0=OP.is_ge, op1=OP.bypass))
            op("ve", ["m1"], ["nm1"],
               lambda v: v.tensor_scalar(sc1[:, 2:3], sc1[:, 0:1], -1.0, 0.0,
                                         op0=OP.mult, op1=OP.bypass))
            op("ve", ["m2", "nm1"], ["dm"],
               lambda v: v.tensor_tensor(sc1[:, 3:4], sc1[:, 1:2], sc1[:, 2:3], op=OP.add))
            op("sc", ["nm1", "logt%d" % tt], ["el%d" % tt],
               lambda s, tt=tt: s.activation(logt[:, tt, :], logt[:, tt, :], AF.Exp,
                                             bias=sc1[:, 2:3]))
            op("sc", ["dm"], ["ed"],
               lambda s: s.activation(sc1[:, 4:5], sc1[:, 3:4], AF.Exp))
            op("ve", ["ed"], ["zd"],
               lambda v: v.tensor_scalar(sc1[:, 5:6], sc1[:, 4:5], 1.0, 0.0,
                                         op0=OP.add, op1=OP.bypass))
            op("ve", ["zd"], ["rz"], lambda v: v.reciprocal(sc1[:, 6:7], sc1[:, 5:6]))
            op("ve", ["c0%d" % tt, "el%d" % tt], ["comb%d" % tt],
               lambda v, tt=tt: v.tensor_tensor(comb[:, tt, :], comb[:, tt, :],
                                                logt[:, tt, :], op=OP.mult))
            op("ve", ["rz", "comb%d" % tt], ["comb%d" % tt],
               lambda v, tt=tt: v.tensor_scalar(comb[:, tt, :], comb[:, tt, :], sc1[:, 6:7],
                                                0.0, op0=OP.mult, op1=OP.bypass))
            op("ve", ["comb%d" % tt], ["combB%d" % tt],
               lambda v, tt=tt: v.tensor_copy(combB[:, tt, :], comb[:, tt, :]))
        for cg in range(0, NTT, 8):
            ng = min(8, NTT - cg)
            def ctr(t, cg=cg, ng=ng):
                for j in range(ng):
                    ins = t.transpose(pT[0:8, j * 128:(j + 1) * 128],
                                      combB[:, cg + j, :], i128b_s[:])
                return ins
            op("pe", ["i128b"] + ["combB%d" % (cg + j) for j in range(ng)], ["pT"], ctr)
            op("ve", ["pT"], ["combF"],
               lambda v, cg=cg, ng=ng: v.tensor_copy(
                   combF[:, cg * 128:(cg + ng) * 128], pT[0:8, 0:ng * 128]))
        for tt in range(NTT):
            def b2mm(t, tt=tt):
                for dh in range(2):
                    ins = t.matmul(pB0[:, dh * 512:(dh + 1) * 512],
                                   combF[:, tt * 128:(tt + 1) * 128],
                                   b2b[:, dh * 512:(dh + 1) * 512], start=True, stop=True)
                return ins
            op("pe", ["combF", "b2b"], ["pB4", "pB5"], b2mm)
            op("ve", ["pB4", "pB5", "RM%d" % tt], ["RM%d" % tt, "pB4", "pB5"],
               lambda v, tt=tt: v.tensor_tensor(RM[:, tt, :], RM[:, tt, :], pB0[:, 0:1024],
                                                op=OP.add))

        for e in range(8):
            gfun = AF.Gelu if e % 2 == 0 else AF.Silu
            for q in range(4):
                par = (e * 4 + q) % 2
                wa = W1q if par == 0 else W1q2
                wb_ = W2q if par == 0 else W2q2
                wtag, w2tag = "W1q%d" % par, "W2q%d" % par
                dma("po", wa[:],
                    W1[e].rearrange("(dt p) f -> p dt f", p=128)[:, :, q * 1024:(q + 1) * 1024],
                    [], [wtag])
                dma("po", wb_[:],
                    W2[e][q * 1024:(q + 1) * 1024, :].rearrange("(ft p) d -> p ft d", p=128),
                    [], [w2tag])
                dma("sy", b1q[:], b1[e, q], ["b1u"], ["b1q"])
                abn = ["ab%d" % f8 for f8 in range(8)]
                for tt2 in range(moe_tok // 512):
                    for fg in range(2):
                        for ft in range(4):
                            f8 = fg * 4 + ft
                            def hmm(t, tt2=tt2, f8=f8, ft=ft, wa=wa):
                                for dt in range(8):
                                    ins = t.matmul(
                                        pAt[ft][:, :],
                                        wa[:, dt, f8 * 128:(f8 + 1) * 128],
                                        XF[:, dt, tt2 * 512:(tt2 + 1) * 512],
                                        start=(dt == 0), stop=(dt == 7))
                                return ins
                            op("pe", [wtag] + xf_deps, ["pA%d" % ft], hmm)
                            op("sc", ["pA%d" % ft, "b1q"],
                               ["ab%d" % f8, "b1u", "pA%d" % ft],
                               lambda s, f8=f8, ft=ft, gfun=gfun: s.activation(
                                   ab[:, f8, :], pAt[ft][:, :], gfun,
                                   bias=b1q[:, f8:f8 + 1]))
                    for mt in range(4):
                        tt = tt2 * 4 + mt
                        for dh in range(2):
                            def ymm(t, mt=mt, dh=dh, wb_=wb_):
                                for ft in range(8):
                                    ins = t.matmul(pB0[:, dh * 512:(dh + 1) * 512],
                                                   ab[:, ft, mt * 128:(mt + 1) * 128],
                                                   wb_[:, ft, dh * 512:(dh + 1) * 512],
                                                   start=(ft == 0), stop=(ft == 7))
                                return ins
                            op("pe", [w2tag] + abn, ["pB%d" % (4 + dh)], ymm)
                            op("ve", ["pB%d" % (4 + dh), "comb%d" % tt, "RM%d" % tt],
                               ["RM%d" % tt, "pB%d" % (4 + dh)],
                               lambda v, tt=tt, e=e, dh=dh: v.scalar_tensor_tensor(
                                   RM[:, tt, dh * 512:(dh + 1) * 512],
                                   pB0[:, dh * 512:(dh + 1) * 512],
                                   comb[:, tt, e:e + 1],
                                   RM[:, tt, dh * 512:(dh + 1) * 512],
                                   op0=OP.mult, op1=OP.add))

        for tt in range(NTT):
            op("ve", ["RM%d" % tt], ["m3"],
               lambda v, tt=tt: v.reduce_sum(sc1[:, 0:1], RM[:, tt, :], axis=AX.X))
            op("ve", ["m3"], ["m4"],
               lambda v: v.tensor_scalar(sc1[:, 1:2], sc1[:, 0:1], 1.0 / D_MODEL, 0.0,
                                         op0=OP.mult, op1=OP.bypass))
            op("ve", ["m4", "RM%d" % tt], ["RM%d" % tt],
               lambda v, tt=tt: v.tensor_scalar(RM[:, tt, :], RM[:, tt, :], sc1[:, 1:2],
                                                0.0, op0=OP.subtract, op1=OP.bypass))
            op("sc", ["RM%d" % tt], ["v2", "hT"],
               lambda s, tt=tt: s.activation(hT[:], RM[:, tt, :], AF.Square,
                                             accum_out=sc1[:, 2:3]))
            op("ve", ["v2"], ["v3"],
               lambda v: v.tensor_scalar(sc1[:, 3:4], sc1[:, 2:3], 1.0 / D_MODEL, EPS,
                                         op0=OP.mult, op1=OP.add))
            op("sc", ["v3"], ["v4"], lambda s: s.activation(sc1[:, 4:5], sc1[:, 3:4], AF.Ln))
            op("ve", ["v4"], ["v4"],
               lambda v: v.tensor_scalar(sc1[:, 4:5], sc1[:, 4:5], -0.5, 0.0,
                                         op0=OP.mult, op1=OP.bypass))
            op("sc", ["v4"], ["v5"], lambda s: s.activation(sc1[:, 5:6], sc1[:, 4:5], AF.Exp))
            op("ve", ["v5", "RM%d" % tt], ["RM%d" % tt],
               lambda v, tt=tt: v.tensor_scalar(RM[:, tt, :], RM[:, tt, :], sc1[:, 5:6],
                                                0.0, op0=OP.mult, op1=OP.bypass))
            dma("sy", out[tt * 128:(tt + 1) * 128, :], RM[:, tt, :], ["RM%d" % tt],
                ["outs%d" % tt])

        def fin(eng):
            for s in allsems:
                if s.count > 0:
                    eng.wait_ge(s.h, s.count)
        P.raw("sy", fin)

        block = es.enter_context(nc.Block())

        @block.sync
        def _(sy):
            for name, f in P.script:
                if name == "sy":
                    f(sy)

        @block.gpsimd
        def _(g):
            for name, f in P.script:
                if name == "po":
                    f(g)

        @block.tensor
        def _(t):
            for name, f in P.script:
                if name == "pe":
                    f(t)

        @block.scalar
        def _(s):
            for name, f in P.script:
                if name == "sc":
                    f(s)

        @block.vector
        def _(v):
            for name, f in P.script:
                if name == "ve":
                    f(v)

    return nc


def _host_inputs(inputs, T, n_ranks):
    x = np.ascontiguousarray(np.asarray(inputs["L0_attn_x"], np.float32).reshape(-1, D_MODEL))[:T]
    inw = np.asarray(inputs["L0_attn_in_proj_w"], np.float32)
    TH = T // 2
    moe_tok = T // n_ranks
    base = dict(
        Wg=np.ascontiguousarray(np.asarray(inputs["L0_ffn_Wg"], np.float32)),
        bg=np.ascontiguousarray(np.asarray(inputs["L0_ffn_bg"], np.float32).reshape(1, 8)),
        W1=np.ascontiguousarray(np.asarray(inputs["L0_ffn_W1"], np.float32)),
        b1=np.ascontiguousarray(np.asarray(inputs["L0_ffn_b1"], np.float32).reshape(8, 4, 8, 128).transpose(0, 1, 3, 2)),
        W2=np.ascontiguousarray(np.asarray(inputs["L0_ffn_W2"], np.float32)),
        b2=np.ascontiguousarray(np.asarray(inputs["L0_ffn_b2"], np.float32)),
        i128f=np.eye(128, dtype=np.float32),
        i128b=np.eye(128, dtype=np.float32),
        triuc=np.triu(np.ones((128, 128), np.float32)),
        onesr=np.ones((1, 128), np.float32),
        cvec=np.tile(np.array([[EPS, 1.0, 0.0, 0.0]], np.float32), (128, 1)),
    )
    for L in range(2):
        base[f"inwT{L}"] = np.ascontiguousarray(inw[L].T)
        base[f"convw{L}"] = np.ascontiguousarray(np.asarray(inputs["L0_attn_conv_w"], np.float32)[L])
        base[f"convb{L}"] = np.ascontiguousarray(np.asarray(inputs["L0_attn_conv_b"], np.float32)[L].reshape(17, 128).T)
        base[f"dtb{L}"] = np.ascontiguousarray(np.asarray(inputs["L0_attn_dt_bias"], np.float32)[L])
        base[f"alog{L}"] = np.ascontiguousarray(np.asarray(inputs["L0_attn_A_log"], np.float32)[L])
        base[f"dsk{L}"] = np.ascontiguousarray(np.asarray(inputs["L0_attn_D"], np.float32)[L])
        base[f"nw{L}"] = np.ascontiguousarray(np.asarray(inputs["L0_attn_norm_w"], np.float32)[L].reshape(8, 128).T)
        base[f"gw{L}"] = np.ascontiguousarray(np.asarray(inputs["L0_attn_gnorm_w"], np.float32)[L].reshape(16, 128).T)
        base[f"owT{L}"] = np.ascontiguousarray(np.asarray(inputs["L0_attn_out_proj_w"], np.float32)[L].T)
    maps = []
    for rk in range(n_ranks):
        m = dict(base)
        par = rk & 1
        m["xin"] = np.ascontiguousarray(x[par * TH:(par + 1) * TH])
        hxv = np.zeros((3, D_MODEL), np.float32)
        if par == 1:
            hxv[:] = x[TH - 3:TH]
        m["hx"] = hxv
        rv = np.zeros((1, 16), np.int32)
        rv[0, 0] = rk
        rv[0, 1:5] = [1111, 2222, 3333, 4444]
        rv[0, 9] = par
        rv[0, 10] = 1 if par == ((rk * moe_tok) // TH) else 0
        m["rankin"] = rv
        maps.append(m)
    return maps


def kernel(**inputs):
    T = 16384
    n_ranks = 8
    moe_tok = T // n_ranks
    nc = build(T, n_ranks, moe_tok)
    in_maps = _host_inputs(inputs, T, n_ranks)
    res = run_bass_kernel_spmd(nc, in_maps, core_ids=list(range(n_ranks)))
    parts = [res.results[rk]["out"] for rk in range(n_ranks)]
    full = np.concatenate(parts, axis=0).astype(np.float32)
    return full.reshape(1, T, D_MODEL)
